# version 07
# baseline (speedup 1.0000x reference)
import contextlib
import numpy as np
import ml_dtypes
import concourse.bass as bass
import concourse.mybir as mybir
from concourse.bass_utils import run_bass_kernel_spmd

F32 = mybir.dt.float32
BF16 = mybir.dt.bfloat16
I32 = mybir.dt.int32
AF = mybir.ActivationFunctionType
ALU = mybir.AluOpType
AX = mybir.AxisListType

NCORES = 8
D = 1024
T = 2048
M = T // 128
KC = D // 128
H = 16
HD = 64
F2 = 128
NEG = -1e30

ENGS = ("sync", "gpsimd", "tensor", "vector", "scalar")
SEMS = ("sync", "gpsimd", "tensor", "vector", "scalar", "gdma", "coll")


class Em:

    def __init__(self):
        self.q = {e: [] for e in ENGS}
        self.n = {s: 0 for s in SEMS}

    def do(self, e, fn, sem, inc):
        self.q[e].append(("op", fn, sem, inc))
        if inc:
            self.n[sem] += inc
        return self.n[sem] if sem else 0

    def dma(self, e, fn):
        sem = "gdma" if e == "gpsimd" else "sync"
        return self.do(e, fn, sem, 16)

    def step(self, e, fn, fence=None):
        assert e != "sync"
        if fence is None:
            fence = e in ("vector", "scalar", "gpsimd")
        if fence and self.n[e] > 0:
            self.q[e].append(("wait", e, self.n[e]))
        return self.do(e, fn, e, 1)

    def wait(self, e, prod, val):
        assert val <= self.n[prod], (e, prod, val, self.n[prod])
        self.q[e].append(("wait", prod, val))

    def barrier(self, engines=ENGS):
        vals = {e: self.n[e] for e in engines}
        gd = self.n["gdma"]
        for e in engines:
            for o in engines:
                if o != e and vals[o] > 0:
                    self.wait(e, o, vals[o])
            if gd > 0 and e != "gpsimd":
                self.wait(e, "gdma", gd)

    def raw(self, e, fn, sem=None, inc=0):
        self.q[e].append(("raw", fn, None, 0))
        if sem:
            self.n[sem] += inc
        return self.n[sem] if sem else 0

    def finalize(self, nc):
        with contextlib.ExitStack() as st:
            sems = {s: st.enter_context(nc.semaphore("sem_" + s)) for s in SEMS}
            self.sems = sems
            block = st.enter_context(nc.Block())

            def mk(name):
                def run(eng):
                    for item in self.q[name]:
                        if item[0] == "op":
                            _, a, sem, b = item
                            ins = a(eng)
                            if b:
                                ins.then_inc(sems[sem], b)
                        elif item[0] == "raw":
                            item[1](eng, sems)
                        else:
                            _, prod, v = item
                            eng.wait_ge(sems[prod], v)
                return run

            block.sync(mk("sync"))
            block.gpsimd(mk("gpsimd"))
            block.tensor(mk("tensor"))
            block.vector(mk("vector"))
            block.scalar(mk("scalar"))



def emit_cast_load(em, dst, src_ap, rows, cols):
    n = rows // 128
    for kc in range(n):
        em.dma("gpsimd", lambda g, kc=kc: g.dma_start(
            out=dst[:, kc * cols:(kc + 1) * cols],
            in_=src_ap[kc * 128:(kc + 1) * 128, :]))
    return em.n["gdma"]



def _ps_pad(nc, st, name, shape, dtype):
    esz = 2 if dtype == BF16 else 4
    cols = shape[1]
    per_bank = 2048 // esz
    padded = ((cols + per_bank - 1) // per_bank) * per_bank
    h = st.enter_context(nc.psum_tensor(name, [shape[0], padded], dtype))
    return h


def emit_ln_xnT(em, bufs, m, x_src_ap=None, ready=None):
    if x_src_ap is not None:
        em.wait("sync", "scalar", em.n["scalar"])
        em.wait("sync", "vector", em.n["vector"])
        v = em.dma("sync", lambda s: s.dma_start(
            out=bufs["xchunk"][:], in_=x_src_ap[m * 128:(m + 1) * 128, :]))
        xa = bufs["xchunk"]
        em.wait("vector", "sync", v)
        em.wait("scalar", "sync", v)
    else:
        xa = bufs["xs"][:, m * D:(m + 1) * D]
        for e, val in (ready or []):
            pass
    mini = bufs["mini"]
    mean, negm, var, rstd, nmr = (mini[:, i:i + 1] for i in range(5))
    v1 = em.step("vector", lambda vv, xa=xa: vv.tensor_reduce(mean, xa[:], AX.X, ALU.add))
    v2 = em.step("vector", lambda vv: vv.tensor_scalar(negm, mean, -1.0 / D, None, op0=ALU.mult))
    em.step("vector", lambda vv: vv.tensor_scalar(mean, mean, 1.0 / D, None, op0=ALU.mult))
    em.wait("scalar", "vector", v2)
    s1 = em.step("scalar", lambda sc, xa=xa: sc.activation(bufs["sq"][:], xa[:], AF.Square, bias=negm, scale=1.0))
    em.wait("vector", "scalar", s1)
    em.step("vector", lambda vv: vv.tensor_reduce(var, bufs["sq"][:], AX.X, ALU.add))
    v3 = em.step("vector", lambda vv: vv.tensor_scalar(var, var, 1.0 / D, None, op0=ALU.mult))
    em.wait("scalar", "vector", v3)
    s2 = em.step("scalar", lambda sc: sc.activation(rstd, var, AF.Ln, bias=bufs["eps"], scale=1.0))
    s3 = em.step("scalar", lambda sc: sc.activation(rstd, rstd, AF.Exp, bias=0.0, scale=-0.5))
    em.wait("vector", "scalar", s3)
    em.wait("vector", "tensor", em.n["tensor"])
    s4 = em.step("vector", lambda vv, xa=xa: vv.tensor_scalar(
        bufs["xnb"][:], xa[:], negm, rstd, op0=ALU.add, op1=ALU.mult))
    em.wait("tensor", "vector", s4)
    stride = bufs.get("xt_stride", T)
    off = bufs.get("xt_off", m * 128)
    for g in range(2):
        tv = None
        for j in range(4):
            jj = g * 4 + j
            tv = em.step("tensor", lambda tt, jj=jj, j=j: tt.transpose(
                bufs["tp"][:, j * 128:(j + 1) * 128],
                bufs["xnb"][:, jj * 128:(jj + 1) * 128], bufs["idb"][:]))
        em.wait("vector", "tensor", tv)
        ev = None
        for j in range(4):
            jj = g * 4 + j
            ev = em.step("vector", lambda vv, jj=jj, j=j: vv.tensor_copy(
                out=bufs["xt"][:, jj * stride + off:jj * stride + off + 128],
                in_=bufs["tp"][:, j * 128:(j + 1) * 128]))
        em.wait("tensor", "vector", ev)
    return em.n["vector"], em.n["tensor"]


def host_prep_common(inputs):
    c = {}
    c["iden"] = np.eye(128, dtype=np.float32)
    return c



def emit_phaseA(nc, em, st, sb, ps, x, wk, wv, proj, iden, kv_out, stab_out):
    xt = sb("xt", [128, KC * T], BF16)
    ktb = sb("ktb", [128, (H // 2) * T], BF16)
    vb = sb("vb", [128, M * D], BF16)
    wkb = sb("wkb", [128, KC * D], BF16)
    wvb = sb("wvb", [128, KC * D], BF16)
    idf = sb("idf", [128, 128], F32)
    idb = sb("idb", [128, 128], BF16)
    pmt = sb("pmt", [128, H * F2], BF16)
    projs = sb("projs", [128, H * HD], F32)
    projb = sb("projb", [128, H * HD], BF16)
    kw = sb("kw", [128, D], F32)
    kbb = sb("kbb", [128, D], BF16)
    sq = sb("sq", [128, D], F32)
    xnb = sb("xnb", [128, D], BF16)
    xchunk = sb("xchunk", [128, D], F32)
    mini = sb("mini", [128, 8], F32)
    diagk = sb("diagk", [128, M * H], F32)
    stacc = sb("stacc", [128, H], F32)
    stk = sb("stk", [16, 1], F32)
    strow = sb("strow", [1, 16], F32)
    stb = sb("stb", [128, H], F32)
    onesf = sb("onesf", [1, 128], F32)
    onesb = sb("onesb", [128, 1], BF16)
    escr = sb("escr", [128, 2 * F2], F32)
    kfb = sb("kfb", [128, 2 * F2], BF16)
    kvs = sb("kvs", [128, 32 * 65], F32)
    bias = sb("biasA", [128, 1], F32)

    tp = ps("tp", [128, 512], BF16)
    pk = ps("pk", [128, 512], F32)
    pk2 = ps("pk2", [128, 512], F32)
    pu = ps("pu", [128, 128], F32)
    pu2 = ps("pu2", [128, 128], F32)
    pkv = ps("pkv", [128, 130], F32)
    pkv2 = ps("pkv2", [128, 130], F32)
    pmisc = ps("pmisc", [128, 128], F32)

    bufs = dict(xt=xt, sq=sq, mini=mini, xnb=xnb, tp=tp, idb=idb,
                xchunk=xchunk, eps=mini[:, 6:7])
    em.step("vector", lambda vv: vv.memset(mini[:, 6:7], 1e-5))

    vid = em.dma("sync", lambda s: s.dma_start(out=idf[:], in_=iden[:, :]))
    vpr = em.dma("sync", lambda s: s.dma_start(
        out=projs.ap().rearrange("p (h d) -> p h d", h=H),
        in_=proj.rearrange("h f d -> f h d")))
    g1 = emit_cast_load(em, wkb, wk, D, D)
    g2 = emit_cast_load(em, wvb, wv, D, D)
    gid = em.dma("gpsimd", lambda g: g.dma_start(out=idb[:], in_=iden[:, :]))

    em.step("vector", lambda vv: vv.memset(onesf[:], 1.0))
    em.step("vector", lambda vv: vv.memset(onesb[:], 1.0))
    em.step("vector", lambda vv: vv.memset(stacc[:], NEG))
    em.wait("vector", "sync", vpr)
    vp = em.step("vector", lambda vv: vv.tensor_scalar(projb[:], projs[:], 0.125, None, op0=ALU.mult))
    em.wait("tensor", "vector", vp)
    em.wait("tensor", "gdma", gid)
    for h in range(H):
        tv = em.step("tensor", lambda tt, h=h: tt.transpose(
            tp[0:64, 0:128], projb[:, h * HD:(h + 1) * HD], idb[:]))
        em.wait("vector", "tensor", tv)
        e1 = em.step("vector", lambda vv, h=h: vv.tensor_copy(
            out=pmt[0:64, h * F2:(h + 1) * F2], in_=tp[0:64, 0:128]))
        e2 = em.step("vector", lambda vv, h=h: vv.tensor_copy(
            out=pmt[64:128, h * F2:(h + 1) * F2], in_=tp[0:64, 0:128]))
        em.wait("tensor", "vector", e2)

    for m in range(M):
        emit_ln_xnT(em, bufs, m, x_src_ap=x)

    em.barrier(("tensor", "vector", "scalar", "gpsimd"))

    for m in range(M):
        em.wait("vector", "scalar", em.n["scalar"])
        for half in range(2):
            p = pk if half == 0 else pk2
            em.wait("tensor", "vector", em.n["vector"])
            mv = None
            for kc in range(KC):
                mv = em.step("tensor", lambda tt, kc=kc, half=half, m=m, p=p: tt.matmul(
                    p[:, 0:512],
                    xt[:, kc * T + m * 128:kc * T + (m + 1) * 128],
                    wkb[:, kc * D + half * 512:kc * D + half * 512 + 512],
                    start=(kc == 0), stop=(kc == KC - 1)))
            em.wait("vector", "tensor", mv)
            em.step("vector", lambda vv, half=half, p=p: vv.tensor_copy(
                out=kw[:, half * 512:half * 512 + 512], in_=p[:, 0:512]))
        for half in range(2):
            p = pk if half == 0 else pk2
            ev_done = em.n["vector"]
            em.wait("tensor", "vector", ev_done)
            mv = None
            for kc in range(KC):
                mv = em.step("tensor", lambda tt, kc=kc, half=half, m=m, p=p: tt.matmul(
                    p[:, 0:512],
                    xt[:, kc * T + m * 128:kc * T + (m + 1) * 128],
                    wvb[:, kc * D + half * 512:kc * D + half * 512 + 512],
                    start=(kc == 0), stop=(kc == KC - 1)))
            em.wait("vector", "tensor", mv)
            em.step("vector", lambda vv, half=half, m=m, p=p: vv.tensor_copy(
                out=vb[:, m * D + half * 512:m * D + half * 512 + 512], in_=p[:, 0:512]))
        kv_evac = em.n["vector"]
        em.wait("scalar", "vector", kv_evac)
        for h in range(H):
            em.step("scalar", lambda sc, h=h, m=m: sc.activation(
                sq[:, 0:HD], kw[:, h * HD:(h + 1) * HD], AF.Square,
                bias=0.0, scale=1.0, accum_out=diagk[:, m * H + h:m * H + h + 1]))
        vc = em.step("vector", lambda vv: vv.tensor_copy(out=kbb[:], in_=kw[:]))
        em.wait("tensor", "vector", vc)
        for h in range(H):
            tv = em.step("tensor", lambda tt, h=h: tt.transpose(
                tp[0:64, 0:128], kbb[:, h * HD:(h + 1) * HD], idb[:]))
            em.wait("vector", "tensor", tv)
            pb = (h % 2) * 64
            ev = em.step("vector", lambda vv, h=h, m=m, pb=pb: vv.tensor_copy(
                out=ktb[pb:pb + 64, (h // 2) * T + m * 128:(h // 2) * T + (m + 1) * 128],
                in_=tp[0:64, 0:128]))
            em.wait("tensor", "vector", ev)
    em.barrier(("tensor", "vector", "scalar"))

    for m in range(M):
        for h in range(H):
            p = pu if (m * H + h) % 2 == 0 else pu2
            pb = (h % 2) * 64
            mv = em.step("tensor", lambda tt, m=m, h=h, p=p, pb=pb: tt.matmul(
                p[:, 0:128],
                ktb[pb:pb + 64, (h // 2) * T + m * 128:(h // 2) * T + (m + 1) * 128],
                pmt[pb:pb + 64, h * F2:(h + 1) * F2],
                start=True, stop=True))
            em.wait("vector", "tensor", mv)
            rv = em.step("vector", lambda vv, p=p: vv.tensor_reduce(
                mini[:, 5:6], p[:, 0:128], AX.X, ALU.max))
            mxv = em.step("vector", lambda vv, h=h: vv.tensor_max(
                stacc[:, h:h + 1], stacc[:, h:h + 1], mini[:, 5:6]))
            if (m * H + h) >= 1:
                pass
            em.wait("tensor", "vector", rv)

    em.wait("tensor", "vector", em.n["vector"])
    tv = em.step("tensor", lambda tt: tt.transpose(pmisc[0:16, 0:128], stacc[:], idf[:]))
    em.wait("vector", "tensor", tv)
    em.step("vector", lambda vv: vv.tensor_reduce(stk[:], pmisc[0:16, 0:128], AX.X, ALU.max))
    vv1 = em.step("vector", lambda vv: vv.tensor_copy(out=sq[0:16, 0:1], in_=stk[:]))
    em.wait("tensor", "vector", vv1)
    tv = em.step("tensor", lambda tt: tt.transpose(pmisc[0:1, 0:16], sq[0:16, 0:1], idf[0:16, 0:16]))
    em.wait("vector", "tensor", tv)
    vv2 = em.step("vector", lambda vv: vv.tensor_copy(out=strow[:], in_=pmisc[0:1, 0:16]))
    em.wait("tensor", "vector", vv2)
    tv = em.step("tensor", lambda tt: tt.matmul(pmisc[:, 0:16], onesf[:], strow[:], start=True, stop=True))
    em.wait("vector", "tensor", tv)
    em.step("vector", lambda vv: vv.tensor_copy(out=stb[:], in_=pmisc[:, 0:16]))
    em.barrier(("tensor", "vector", "scalar"))

    em.step("vector", lambda vv: vv.memset(kvs[:], 0.0))
    for m in range(M):
        for h in range(H):
            i = m * H + h
            pU = pu if i % 2 == 0 else pu2
            pK = pkv if i % 2 == 0 else pkv2
            pb = (h % 2) * 64
            mv = em.step("tensor", lambda tt, m=m, h=h, p=pU, pb=pb: tt.matmul(
                p[:, 0:128],
                ktb[pb:pb + 64, (h // 2) * T + m * 128:(h // 2) * T + (m + 1) * 128],
                pmt[pb:pb + 64, h * F2:(h + 1) * F2],
                start=True, stop=True))
            bv = em.step("vector", lambda vv, m=m, h=h: vv.tensor_scalar(
                bias[:], diagk[:, m * H + h:m * H + h + 1], -0.5,
                stb[:, h:h + 1], op0=ALU.mult, op1=ALU.subtract))
            em.wait("scalar", "tensor", mv)
            em.wait("scalar", "vector", bv)
            s1 = em.step("scalar", lambda sc, p=pU: sc.activation(
                escr[:, 0:F2], p[:, 0:128], AF.Exp, bias=bias[:], scale=1.0))
            s2 = em.step("scalar", lambda sc, p=pU: sc.activation(
                escr[:, F2:2 * F2], p[:, 0:128], AF.Exp, bias=bias[:], scale=-1.0))
            em.wait("vector", "scalar", s2)
            vc = em.step("vector", lambda vv: vv.tensor_scalar(
                kfb[:], escr[:], 0.0625, None, op0=ALU.mult))
            em.wait("tensor", "vector", vc)
            mv2 = None
            for half in range(2):
                mv2 = em.step("tensor", lambda tt, half=half, m=m, h=h, p=pK: tt.matmul(
                    p[:, half * 65:half * 65 + 64],
                    kfb[:, half * 128:half * 128 + 128],
                    vb[:, m * D + h * HD:m * D + (h + 1) * HD],
                    start=(half == 0), stop=False))
                mv2 = em.step("tensor", lambda tt, half=half, p=pK: tt.matmul(
                    p[:, half * 65 + 64:half * 65 + 65],
                    kfb[:, half * 128:half * 128 + 128],
                    onesb[:],
                    start=False, stop=(half == 1)))
            em.wait("vector", "tensor", mv2)
            av = None
            for half in range(2):
                tile = h * 2 + half
                av = em.step("vector", lambda vv, tile=tile, half=half, p=pK: vv.tensor_add(
                    kvs[:, tile * 65:(tile + 1) * 65],
                    kvs[:, tile * 65:(tile + 1) * 65],
                    p[:, half * 65:half * 65 + 65]))
            em.wait("tensor", "vector", av)
            em.wait("scalar", "vector", av)

    fv = em.n["vector"]
    em.wait("sync", "vector", fv)
    o1 = em.dma("sync", lambda s: s.dma_start(out=kv_out[:, :], in_=kvs[:]))
    o2 = em.dma("sync", lambda s: s.dma_start(out=stab_out[:, :], in_=stk[:],
                                              allow_slow_non_contiguous=True))
    em.wait("sync", "sync", o2)




def build_launchA():
    nc = bass.Bass("TRN2", target_bir_lowering=False, detect_race_conditions=False)
    x = nc.dram_tensor("x", [T, D], F32, kind="ExternalInput").ap()
    wk = nc.dram_tensor("wk", [D, D], F32, kind="ExternalInput").ap()
    wv = nc.dram_tensor("wv", [D, D], F32, kind="ExternalInput").ap()
    proj = nc.dram_tensor("proj", [H, F2, HD], F32, kind="ExternalInput").ap()
    iden = nc.dram_tensor("iden", [128, 128], F32, kind="ExternalInput").ap()
    kv_out = nc.dram_tensor("kv_out", [128, 32 * 65], F32, kind="ExternalOutput").ap()
    stab_out = nc.dram_tensor("stab_out", [16, 1], F32, kind="ExternalOutput").ap()

    em = Em()
    with contextlib.ExitStack() as st:
        def sb(name, shape, dtype):
            return st.enter_context(nc.sbuf_tensor(name, shape, dtype))

        def ps(name, shape, dtype):
            return _ps_pad(nc, st, name, shape, dtype)

        emit_phaseA(nc, em, st, sb, ps, x, wk, wv, proj, iden, kv_out, stab_out)
        em.finalize(nc)
    return nc



def emit_phaseB(nc, em, st, sb, ps, x, wq, wout, proj, iden, kv0, kv1, st0, st1, y_out):
    xt = sb("xt", [128, KC * T], BF16)
    t8a = sb("t8a", [128, KC * T], BF16)
    wqb = sb("wqb", [128, KC * D], BF16)
    wob = sb("wob", [128, KC * D], BF16)
    idf = sb("idf", [128, 128], F32)
    idb = sb("idb", [128, 128], BF16)
    pmt = sb("pmt", [128, H * F2], BF16)
    projs = sb("projs", [128, H * HD], F32)
    projb = sb("projb", [128, H * HD], BF16)
    kv0s = sb("kv0s", [128, 32 * 65], F32)
    kv1s = sb("kv1s", [128, 32 * 65], F32)
    kvc = sb("kvc", [128, 32 * 65], F32)
    kvcb = sb("kvcb", [128, 32 * 65], BF16)
    st0s = sb("st0s", [16, 1], F32)
    st1s = sb("st1s", [16, 1], F32)
    smax = sb("smax", [16, 1], F32)
    sd = sb("sd", [16, 2], F32)
    srow = sb("srow", [1, 32], F32)
    mbc = sb("mbc", [128, 32], F32)
    kw = sb("kw", [128, D], F32)
    kbb = sb("kbb", [128, D], BF16)
    sq = sb("sq", [128, D], F32)
    xnb = sb("xnb", [128, D], BF16)
    xchunk = sb("xchunk", [128, D], F32)
    mini = sb("mini", [128, 8], F32)
    diagq = sb("diagq", [128, H], F32)
    qts = sb("qts", [128, (H // 2) * 128], BF16)
    escr = sb("escr", [128, 2 * F2], F32)
    qfb = sb("qfb", [128, 2 * F2], BF16)
    qftb = sb("qftb", [128, 2 * F2], BF16)
    aob = sb("aob", [128, D], BF16)
    yo = sb("yo", [128, D], F32)
    onesf = sb("onesf", [1, 128], F32)

    tp = ps("tp", [128, 512], BF16)
    pk = ps("pk", [128, 512], F32)
    pk2 = ps("pk2", [128, 512], F32)
    pu = ps("pu", [128, 128], F32)
    pu2 = ps("pu2", [128, 128], F32)
    po = ps("po", [128, 65], F32)
    po2 = ps("po2", [128, 65], F32)
    pmisc = ps("pmisc", [128, 128], F32)

    bufs = dict(xt=xt, sq=sq, mini=mini, xnb=xnb, tp=tp, idb=idb,
                xchunk=xchunk, eps=mini[:, 6:7])
    em.step("vector", lambda vv: vv.memset(mini[:, 6:7], 1e-5))

    vid = em.dma("sync", lambda s: s.dma_start(out=idf[:], in_=iden[:, :]))
    vpr = em.dma("sync", lambda s: s.dma_start(
        out=projs.ap().rearrange("p (h d) -> p h d", h=H),
        in_=proj.rearrange("h f d -> f h d")))
    vkv = em.dma("sync", lambda s: s.dma_start(out=kv0s[:], in_=kv0[:, :]))
    vkv = em.dma("sync", lambda s: s.dma_start(out=kv1s[:], in_=kv1[:, :]))
    vst = em.dma("sync", lambda s: s.dma_start(out=st0s[:], in_=st0[:, :],
                                               allow_slow_non_contiguous=True))
    vst = em.dma("sync", lambda s: s.dma_start(out=st1s[:], in_=st1[:, :],
                                               allow_slow_non_contiguous=True))
    g1 = emit_cast_load(em, wqb, wq, D, D)
    g2 = emit_cast_load(em, wob, wout, D, D)
    gid = em.dma("gpsimd", lambda g: g.dma_start(out=idb[:], in_=iden[:, :]))

    em.step("vector", lambda vv: vv.memset(onesf[:], 1.0))
    em.wait("vector", "sync", vpr)
    vp = em.step("vector", lambda vv: vv.tensor_scalar(projb[:], projs[:], 0.125, None, op0=ALU.mult))
    em.wait("tensor", "vector", vp)
    em.wait("tensor", "gdma", gid)
    for h in range(H):
        tv = em.step("tensor", lambda tt, h=h: tt.transpose(
            tp[0:64, 0:128], projb[:, h * HD:(h + 1) * HD], idb[:]))
        em.wait("vector", "tensor", tv)
        e1 = em.step("vector", lambda vv, h=h: vv.tensor_copy(
            out=pmt[0:64, h * F2:(h + 1) * F2], in_=tp[0:64, 0:128]))
        e2 = em.step("vector", lambda vv, h=h: vv.tensor_copy(
            out=pmt[64:128, h * F2:(h + 1) * F2], in_=tp[0:64, 0:128]))
        em.wait("tensor", "vector", e2)

    em.wait("vector", "sync", em.n["sync"])
    em.step("vector", lambda vv: vv.tensor_max(smax[:], st0s[:], st1s[:]))
    em.step("vector", lambda vv: vv.tensor_sub(sd[:, 0:1], st0s[:], smax[:]))
    em.step("vector", lambda vv: vv.tensor_sub(sd[:, 1:2], st1s[:], smax[:]))
    sv = em.n["vector"]
    em.wait("scalar", "vector", sv)
    se = em.step("scalar", lambda sc: sc.activation(sd[:, 0:2], sd[:, 0:2], AF.Exp, bias=0.0, scale=1.0))
    em.wait("tensor", "scalar", se)
    em.wait("tensor", "sync", vid)
    tv = em.step("tensor", lambda tt: tt.transpose(pmisc[0:1, 0:16], sd[0:16, 0:1], idf[0:16, 0:16]))
    tv = em.step("tensor", lambda tt: tt.transpose(pmisc[0:1, 16:32], sd[0:16, 1:2], idf[0:16, 0:16]))
    em.wait("vector", "tensor", tv)
    ev = em.step("vector", lambda vv: vv.tensor_copy(out=srow[0:1, 0:16], in_=pmisc[0:1, 0:16]))
    ev = em.step("vector", lambda vv: vv.tensor_copy(out=srow[0:1, 16:32], in_=pmisc[0:1, 16:32]))
    em.wait("tensor", "vector", ev)
    tv = em.step("tensor", lambda tt: tt.matmul(pmisc[:, 0:32], onesf[:], srow[:], start=True, stop=True))
    em.wait("vector", "tensor", tv)
    em.step("vector", lambda vv: vv.tensor_copy(out=mbc[:], in_=pmisc[:, 0:32]))
    for h in range(H):
        for half in range(2):
            tile = h * 2 + half
            sl = slice(tile * 65, (tile + 1) * 65)
            em.step("vector", lambda vv, sl=sl, h=h: vv.tensor_scalar(
                kvc[:, sl], kv1s[:, sl], mbc[:, 16 + h:17 + h], None, op0=ALU.mult))
            em.step("vector", lambda vv, sl=sl, h=h: vv.scalar_tensor_tensor(
                kvc[:, sl], kv0s[:, sl], mbc[:, h:h + 1], kvc[:, sl],
                op0=ALU.mult, op1=ALU.add))
    em.step("vector", lambda vv: vv.tensor_copy(out=kvcb[:], in_=kvc[:]))

    for m in range(M):
        emit_ln_xnT(em, bufs, m, x_src_ap=x)
    em.barrier(("tensor", "vector", "scalar", "gpsimd"))

    for m in range(M):
        em.wait("vector", "scalar", em.n["scalar"])
        for half in range(2):
            p = pk if half == 0 else pk2
            em.wait("tensor", "vector", em.n["vector"])
            mv = None
            for kc in range(KC):
                mv = em.step("tensor", lambda tt, kc=kc, half=half, m=m, p=p: tt.matmul(
                    p[:, 0:512],
                    xt[:, kc * T + m * 128:kc * T + (m + 1) * 128],
                    wqb[:, kc * D + half * 512:kc * D + half * 512 + 512],
                    start=(kc == 0), stop=(kc == KC - 1)))
            em.wait("vector", "tensor", mv)
            em.step("vector", lambda vv, half=half, p=p: vv.tensor_copy(
                out=kw[:, half * 512:half * 512 + 512], in_=p[:, 0:512]))
        q_evac = em.n["vector"]
        em.wait("scalar", "vector", q_evac)
        for h in range(H):
            em.step("scalar", lambda sc, h=h: sc.activation(
                sq[:, 0:HD], kw[:, h * HD:(h + 1) * HD], AF.Square,
                bias=0.0, scale=1.0, accum_out=diagq[:, h:h + 1]))
        vc = em.step("vector", lambda vv: vv.tensor_copy(out=kbb[:], in_=kw[:]))
        em.wait("tensor", "vector", vc)
        for h in range(H):
            pb = (h % 2) * 64
            tv = em.step("tensor", lambda tt, h=h: tt.transpose(
                tp[0:64, 0:128], kbb[:, h * HD:(h + 1) * HD], idb[:]))
            em.wait("vector", "tensor", tv)
            ev = em.step("vector", lambda vv, h=h, pb=pb: vv.tensor_copy(
                out=qts[pb:pb + 64, (h // 2) * 128:(h // 2) * 128 + 128],
                in_=tp[0:64, 0:128]))
            em.wait("tensor", "vector", ev)
        sdone = em.n["scalar"]
        em.wait("tensor", "scalar", sdone)
        for h in range(H):
            i = m * H + h
            pU = pu if i % 2 == 0 else pu2
            pO = po if i % 2 == 0 else po2
            pb = (h % 2) * 64
            mv = em.step("tensor", lambda tt, h=h, p=pU, pb=pb: tt.matmul(
                p[:, 0:128],
                qts[pb:pb + 64, (h // 2) * 128:(h // 2) * 128 + 128],
                pmt[pb:pb + 64, h * F2:(h + 1) * F2],
                start=True, stop=True))
            em.wait("vector", "tensor", mv)
            rv = em.step("vector", lambda vv, p=pU: vv.tensor_reduce(
                mini[:, 5:6], p[:, 0:128], AX.X, ALU.max))
            bv = em.step("vector", lambda vv, h=h: vv.tensor_scalar(
                mini[:, 7:8], diagq[:, h:h + 1], -0.5, mini[:, 5:6],
                op0=ALU.mult, op1=ALU.subtract))
            em.wait("scalar", "vector", bv)
            s1 = em.step("scalar", lambda sc, p=pU: sc.activation(
                escr[:, 0:F2], p[:, 0:128], AF.Exp, bias=mini[:, 7:8], scale=1.0))
            s2 = em.step("scalar", lambda sc, p=pU: sc.activation(
                escr[:, F2:2 * F2], p[:, 0:128], AF.Exp, bias=mini[:, 7:8], scale=-1.0))
            em.wait("vector", "scalar", s2)
            vc = em.step("vector", lambda vv: vv.tensor_scalar(
                qfb[:], escr[:], 0.0625, None, op0=ALU.mult))
            em.wait("tensor", "vector", vc)
            tv = em.step("tensor", lambda tt: tt.transpose(tp[:, 0:128], qfb[:, 0:128], idb[:]))
            tv = em.step("tensor", lambda tt: tt.transpose(tp[:, 128:256], qfb[:, 128:256], idb[:]))
            em.wait("vector", "tensor", tv)
            ev = em.step("vector", lambda vv: vv.tensor_copy(out=qftb[:], in_=tp[:, 0:256]))
            em.wait("tensor", "vector", ev)
            mo = None
            for c in range(2):
                tile = h * 2 + c
                mo = em.step("tensor", lambda tt, c=c, tile=tile, p=pO: tt.matmul(
                    p[:, 0:65],
                    qftb[:, c * 128:(c + 1) * 128],
                    kvcb[:, tile * 65:(tile + 1) * 65],
                    start=(c == 0), stop=(c == 1)))
            em.wait("vector", "tensor", mo)
            zv = em.step("vector", lambda vv, p=pO: vv.tensor_scalar(
                mini[:, 5:6], p[:, 64:65], 1e-6, None, op0=ALU.add))
            rv = em.step("vector", lambda vv: vv.reciprocal(mini[:, 7:8], mini[:, 5:6]))
            av = em.step("vector", lambda vv, h=h, p=pO: vv.tensor_scalar(
                aob[:, h * HD:(h + 1) * HD], p[:, 0:64], mini[:, 7:8], None, op0=ALU.mult))
            em.wait("tensor", "vector", av)
        for g in range(2):
            tv = None
            for j in range(4):
                jj = g * 4 + j
                tv = em.step("tensor", lambda tt, jj=jj, j=j: tt.transpose(
                    tp[:, j * 128:(j + 1) * 128],
                    aob[:, jj * 128:(jj + 1) * 128], idb[:]))
            em.wait("vector", "tensor", tv)
            ev = None
            for j in range(4):
                jj = g * 4 + j
                ev = em.step("vector", lambda vv, jj=jj, j=j, m=m: vv.tensor_copy(
                    out=t8a[:, jj * T + m * 128:jj * T + (m + 1) * 128],
                    in_=tp[:, j * 128:(j + 1) * 128]))
            em.wait("tensor", "vector", ev)
        for half in range(2):
            p = pk if half == 0 else pk2
            em.wait("tensor", "vector", em.n["vector"])
            mv = None
            for kc in range(KC):
                mv = em.step("tensor", lambda tt, kc=kc, half=half, m=m, p=p: tt.matmul(
                    p[:, 0:512],
                    t8a[:, kc * T + m * 128:kc * T + (m + 1) * 128],
                    wob[:, kc * D + half * 512:kc * D + half * 512 + 512],
                    start=(kc == 0), stop=(kc == KC - 1)))
            em.wait("vector", "tensor", mv)
            em.step("vector", lambda vv, half=half, p=p: vv.tensor_copy(
                out=yo[:, half * 512:half * 512 + 512], in_=p[:, 0:512]))
        yv = em.n["vector"]
        em.wait("sync", "vector", yv)
        em.dma("sync", lambda s, m=m: s.dma_start(
            out=y_out[m * 128:(m + 1) * 128, :], in_=yo[:]))
        em.wait("vector", "sync", em.n["sync"])

    em.wait("sync", "sync", em.n["sync"])



def build_launchB():
    nc = bass.Bass("TRN2", target_bir_lowering=False, detect_race_conditions=False)
    x = nc.dram_tensor("x", [T, D], F32, kind="ExternalInput").ap()
    wq = nc.dram_tensor("wq", [D, D], F32, kind="ExternalInput").ap()
    wout = nc.dram_tensor("wout", [D, D], F32, kind="ExternalInput").ap()
    proj = nc.dram_tensor("proj", [H, F2, HD], F32, kind="ExternalInput").ap()
    iden = nc.dram_tensor("iden", [128, 128], F32, kind="ExternalInput").ap()
    kv0 = nc.dram_tensor("kv0", [128, 32 * 65], F32, kind="ExternalInput").ap()
    kv1 = nc.dram_tensor("kv1", [128, 32 * 65], F32, kind="ExternalInput").ap()
    st0 = nc.dram_tensor("st0", [16, 1], F32, kind="ExternalInput").ap()
    st1 = nc.dram_tensor("st1", [16, 1], F32, kind="ExternalInput").ap()
    y_out = nc.dram_tensor("y_out", [T, D], F32, kind="ExternalOutput").ap()

    em = Em()
    with contextlib.ExitStack() as st:
        def sb(name, shape, dtype):
            return st.enter_context(nc.sbuf_tensor(name, shape, dtype))

        def ps(name, shape, dtype):
            return _ps_pad(nc, st, name, shape, dtype)

        emit_phaseB(nc, em, st, sb, ps, x, wq, wout, proj, iden, kv0, kv1, st0, st1, y_out)
        em.finalize(nc)
    return nc



def emit_phaseC(nc, em, st, sb, ps, x, ya, rw, rb, w1, b1t, w2, b2, iden_in, x1_out):
    E0 = 8
    HID = 4096
    NQ = 4
    xt = sb("xt", [128, KC * T], BF16)
    acc = sb("acc", [128, M * D], F32)
    w1b = [sb(f"w1b{i}", [128, KC * 1024], BF16) for i in range(2)]
    w2b = [sb(f"w2b{i}", [128, KC * 1024], BF16) for i in range(2)]
    b1s = [sb(f"b1s{i}", [128, HID // 128], F32) for i in range(2)]
    b2r = [sb(f"b2r{i}", [1, D], BF16) for i in range(2)]
    rwb = sb("rwb", [128, KC * 8], BF16)
    rbb = sb("rbb", [1, 8], BF16)
    onesr = sb("onesr", [1, 128], BF16)
    idb = sb("idb", [128, 128], BF16)
    sq = sb("sq", [128, D], F32)
    xnb = sb("xnb", [128, D], BF16)
    xchunk = sb("xchunk", [128, D], F32)
    mini = sb("mini", [128, 8], F32)
    lg = sb("lg", [128, 8], F32)
    m8s = sb("m8s", [128, 8], F32)
    ge8 = sb("ge8", [128, 8], I32)
    es8 = sb("es8", [128, 8], F32)
    ts8 = sb("ts8", [128, 8], F32)
    zs8 = sb("zs8", [128, 8], F32)
    g_all = sb("g_all", [128, M * 8], F32)
    hb = sb("hb", [128, 8 * 512], BF16)

    tp = ps("tp", [128, 512], BF16)
    py0 = ps("py0", [128, 1024], F32)
    py1 = ps("py1", [128, 1024], F32)
    ph = ps("ph", [128, 512], F32)
    ph2 = ps("ph2", [128, 512], F32)
    pr = ps("pr", [128, 8], F32)

    bufs = dict(xt=xt, sq=sq, mini=mini, xnb=xnb, tp=tp, idb=idb,
                xchunk=xchunk, eps=mini[:, 6:7])
    em.step("vector", lambda vv: vv.memset(mini[:, 6:7], 1e-5))
    em.step("vector", lambda vv: vv.memset(onesr[:], 1.0))
    em.step("vector", lambda vv: vv.memset(zs8[:], 0.0))

    gid = em.dma("gpsimd", lambda g: g.dma_start(out=idb[:], in_=iden_in[:, :]))
    for kc in range(KC):
        em.dma("gpsimd", lambda g, kc=kc: g.dma_start(
            out=rwb[:, kc * 8:(kc + 1) * 8], in_=rw[kc * 128:(kc + 1) * 128, :]))
    grb = em.dma("gpsimd", lambda g: g.dma_start(out=rbb[:], in_=rb[:, :]))
    vya = em.dma("sync", lambda s: s.dma_start(
        out=acc.ap().rearrange("p (m n) -> p m n", m=M),
        in_=ya.rearrange("(m p) n -> p m n", p=128)))

    for m in range(M):
        emit_ln_xnT(em, bufs, m, x_src_ap=x)
        em.step("vector", lambda vv, m=m: vv.tensor_add(
            acc[:, m * D:(m + 1) * D], xchunk[:], acc[:, m * D:(m + 1) * D]))
    em.barrier(("tensor", "vector", "scalar", "gpsimd"))

    for m in range(M):
        mv = None
        em.wait("tensor", "vector", em.n["vector"])
        for kc in range(KC):
            mv = em.step("tensor", lambda tt, kc=kc, m=m: tt.matmul(
                pr[:, 0:8],
                xt[:, kc * T + m * 128:kc * T + (m + 1) * 128],
                rwb[:, kc * 8:(kc + 1) * 8],
                start=(kc == 0), stop=False))
        mv = em.step("tensor", lambda tt: tt.matmul(
            pr[:, 0:8], onesr[:], rbb[:], start=False, stop=True))
        em.wait("vector", "tensor", mv)
        em.step("vector", lambda vv: vv.tensor_copy(out=lg[:], in_=pr[:, 0:8]))
        em.step("vector", lambda vv: vv.max(out=m8s[:], in_=lg[:]))
        em.step("vector", lambda vv: vv.tensor_scalar(
            mini[:, 0:1], m8s[:, 0:1], -1.0, None, op0=ALU.mult))
        em.step("vector", lambda vv: vv.tensor_scalar(
            ge8[:], lg[:], m8s[:, 1:2], 1, op0=ALU.is_ge, op1=ALU.mult))
        sv = em.n["vector"]
        em.wait("scalar", "vector", sv)
        s1 = em.step("scalar", lambda sc: sc.activation(
            es8[:], lg[:], AF.Exp, bias=mini[:, 0:1], scale=1.0))
        em.wait("vector", "scalar", s1)
        em.step("vector", lambda vv: vv.select(ts8[:], ge8[:], es8[:], zs8[:]))
        em.step("vector", lambda vv: vv.tensor_reduce(mini[:, 1:2], ts8[:], AX.X, ALU.add))
        em.step("vector", lambda vv: vv.reciprocal(mini[:, 2:3], mini[:, 1:2]))
        em.step("vector", lambda vv, m=m: vv.tensor_scalar(
            g_all[:, m * 8:(m + 1) * 8], ts8[:], mini[:, 2:3], None, op0=ALU.mult))

    em.barrier(("tensor", "vector", "scalar"))

    NU = E0 * NQ

    def emit_loads(u):
        e, q = u // NQ, u % NQ
        bi = u % 2
        for kc in range(KC):
            em.dma("gpsimd", lambda g, e=e, q=q, kc=kc, bi=bi: g.dma_start(
                out=w1b[bi][:, kc * 1024:(kc + 1) * 1024],
                in_=w1[e, kc * 128:(kc + 1) * 128, q * 1024:(q + 1) * 1024]))
        for kc in range(KC):
            em.dma("gpsimd", lambda g, e=e, q=q, kc=kc, bi=bi: g.dma_start(
                out=w2b[bi][:, kc * 1024:(kc + 1) * 1024],
                in_=w2[e, q * 1024 + kc * 128:q * 1024 + (kc + 1) * 128, :]))
        if q == 0:
            ei = (u // NQ) % 2
            em.dma("gpsimd", lambda g, e=e, ei=ei: g.dma_start(
                out=b1s[ei][:], in_=b1t[e]))
            em.dma("gpsimd", lambda g, e=e, ei=ei: g.dma_start(
                out=b2r[ei][:], in_=b2[e:e + 1, :]))
        return (em.n["gdma"], em.n["sync"])

    load_end = {}
    compute_end = {0: 0}
    em.wait("gpsimd", "vector", em.n["vector"])
    load_end[0] = emit_loads(0)
    evac_tile = [0] * 8
    py_last = [0, 0]
    pidx = 0
    for u in range(NU):
        e, q = u // NQ, u % NQ
        bi = u % 2
        ei = (u // NQ) % 2
        if u + 1 < NU:
            if u >= 1:
                em.wait("gpsimd", "tensor", compute_end[u - 1])
            load_end[u + 1] = emit_loads(u + 1)
        em.wait("tensor", "gdma", load_end[u][0])
        for g4 in range(4):
            for s in range(8):
                pb = ph if s % 2 == 0 else ph2
                prev = evac_tile[s - 2] if s >= 2 else evac_tile[6 + s]
                if prev:
                    em.wait("tensor", "scalar", prev)
                mv = None
                for kc in range(KC):
                    mv = em.step("tensor", lambda tt, kc=kc, s=s, g4=g4, bi=bi, pb=pb: tt.matmul(
                        pb[:, 0:512],
                        w1b[bi][:, kc * 1024 + s * 128:kc * 1024 + s * 128 + 128],
                        xt[:, kc * T + g4 * 512:kc * T + g4 * 512 + 512],
                        start=(kc == 0), stop=(kc == KC - 1)))
                em.wait("scalar", "tensor", mv)
                bc = q * 8 + s
                evac_tile[s] = em.step("scalar", lambda sc, s=s, bc=bc, pb=pb, ei=ei: sc.activation(
                    hb[:, s * 512:(s + 1) * 512],
                    pb[:, 0:512],
                    AF.Relu, bias=b1s[ei][:, bc:bc + 1], scale=1.0))
            for ts_ in range(4):
                pp = pidx % 2
                p_y = py0 if pp == 0 else py1
                pidx += 1
                if py_last[pp]:
                    em.wait("tensor", "vector", py_last[pp])
                if q == 0:
                    for nh in range(2):
                        em.step("tensor", lambda tt, nh=nh, p=p_y, ei=ei: tt.matmul(
                            p[:, nh * 512:nh * 512 + 512],
                            onesr[:], b2r[ei][0:1, nh * 512:nh * 512 + 512],
                            start=True, stop=False))
                mv = None
                for s in range(8):
                    if ts_ == 0:
                        em.wait("tensor", "scalar", evac_tile[s])
                    for nh in range(2):
                        stf = (q > 0 and s == 0)
                        spf = (s == 7)
                        mv = em.step("tensor", lambda tt, s=s, ts_=ts_, nh=nh, p=p_y, bi=bi, stf=stf, spf=spf: tt.matmul(
                            p[:, nh * 512:nh * 512 + 512],
                            hb[:, s * 512 + ts_ * 128:s * 512 + ts_ * 128 + 128],
                            w2b[bi][:, s * 1024 + nh * 512:s * 1024 + nh * 512 + 512],
                            start=stf, stop=spf))
                em.wait("vector", "tensor", mv)
                mcol = g4 * 4 + ts_
                py_last[pp] = em.step("vector", lambda vv, mcol=mcol, e=e, p=p_y: vv.scalar_tensor_tensor(
                    acc[:, mcol * D:(mcol + 1) * D],
                    p[:, 0:1024],
                    g_all[:, mcol * 8 + e:mcol * 8 + e + 1],
                    acc[:, mcol * D:(mcol + 1) * D],
                    op0=ALU.mult, op1=ALU.add))
        compute_end[u] = em.n["tensor"]

    em.barrier(("tensor", "vector", "scalar", "sync", "gpsimd"))
    for m in range(M):
        eng = "sync" if m % 2 == 0 else "gpsimd"
        em.dma(eng, lambda en, m=m: en.dma_start(
            out=x1_out[m * 128:(m + 1) * 128, :], in_=acc[:, m * D:(m + 1) * D]))
    em.wait("sync", "sync", em.n["sync"])
    em.wait("sync", "gdma", em.n["gdma"])



def build_launchC():
    E0 = 8
    HID = 4096
    NQ = 4
    nc = bass.Bass("TRN2", target_bir_lowering=False, detect_race_conditions=False)
    x = nc.dram_tensor("x", [T, D], F32, kind="ExternalInput").ap()
    ya = nc.dram_tensor("ya", [T, D], F32, kind="ExternalInput").ap()
    rw = nc.dram_tensor("rw", [D, 8], F32, kind="ExternalInput").ap()
    rb = nc.dram_tensor("rb", [1, 8], F32, kind="ExternalInput").ap()
    w1 = nc.dram_tensor("w1", [E0, D, HID], F32, kind="ExternalInput").ap()
    b1t = nc.dram_tensor("b1t", [E0, 128, HID // 128], F32, kind="ExternalInput").ap()
    w2 = nc.dram_tensor("w2", [E0, HID, D], F32, kind="ExternalInput").ap()
    b2 = nc.dram_tensor("b2", [E0, D], F32, kind="ExternalInput").ap()
    x1_out = nc.dram_tensor("x1_out", [T, D], F32, kind="ExternalOutput").ap()

    iden_in = nc.dram_tensor("iden", [128, 128], F32, kind="ExternalInput").ap()
    em = Em()
    with contextlib.ExitStack() as st:
        def sb(name, shape, dtype):
            return st.enter_context(nc.sbuf_tensor(name, shape, dtype))

        def ps(name, shape, dtype):
            return _ps_pad(nc, st, name, shape, dtype)

        emit_phaseC(nc, em, st, sb, ps, x, ya, rw, rb, w1, b1t, w2, b2, iden_in, x1_out)
        em.finalize(nc)
    return nc



PIPE_AV = 1
INTERLEAVE = True
IL_IDX = True


def emit_phaseD_uniform(nc, em, st, sb, ps, x1g, wqkv, wo, wqi, wki, wwi, iden,
                        cosf, sinf, camask, y1_d, kts, vds, qts_d, qits_d, wis_d):
    RK = 4096
    MK = 32
    HI = 4
    DI = 64
    NB = 27

    qtb = sb("qtb", [128, (H // 2) * T], BF16)
    qitb = sb("qitb", [128, (HI // 2) * T], BF16)
    kitb = sb("kitb", [128, RK], BF16)
    wis = sb("wis", [128, M * HI], F32)
    woTb = sb("woTb", [128, KC * D], BF16)
    idf = sb("idf", [128, 128], F32)
    idb = sb("idb", [128, 128], BF16)
    mini = sb("mini", [128, 16], F32)
    fint = sb("fint", [128, 2], I32)
    kw = sb("kw", [128, D], F32)
    aob = sb("aob", [128, D], BF16)
    aotT = sb("aotT", [128, KC * 128], BF16)
    yo = sb("yo", [128, D], F32)
    mask01 = sb("mask01", [128, 2 * 4096], BF16)
    catile = sb("catile", [128, 4096], BF16)
    qitc = sb("qitc", [128, 2 * 128], BF16)
    probsT = sb("probsT", [128, 3 * 512], BF16)
    mtb = sb("mtb", [128, 2 * 512], BF16)

    tp = ps("tp", [128, 1024], BF16)
    psc0 = ps("psc0", [128, 1024], F32)
    psc1 = ps("psc1", [128, 1024], F32)
    pav = ps("pav", [128, 1536], F32)
    pk = psc0
    pk2 = psc1

    bufs = dict(mini=mini, tp=tp, idb=idb,
                eps=mini[:, 6:7], xt_stride=128, xt_off=0)
    em.step("vector", lambda vv: vv.memset(mini[:], 0.0))
    em.step("vector", lambda vv: vv.memset(mini[:, 6:7], 1e-5))
    vid = em.dma("sync", lambda s: s.dma_start(out=idf[:], in_=iden[:, :]))
    gid = em.dma("gpsimd", lambda g: g.dma_start(out=idb[:], in_=iden[:, :]))
    em.wait("vector", "sync", vid)
    em.wait("vector", "gdma", gid)
    em.wait("tensor", "gdma", gid)

    with contextlib.ExitStack() as stw:
        wqT = stw.enter_context(nc.sbuf_tensor("wqT", [128, KC * D], BF16))
        wkT = stw.enter_context(nc.sbuf_tensor("wkT", [128, KC * D], BF16))
        wvT = stw.enter_context(nc.sbuf_tensor("wvT", [128, KC * D], BF16))
        wqiT = stw.enter_context(nc.sbuf_tensor("wqiT", [128, KC * HI * DI], BF16))
        wkiT = stw.enter_context(nc.sbuf_tensor("wkiT", [128, KC * DI], BF16))
        wwiT = stw.enter_context(nc.sbuf_tensor("wwiT", [128, KC * HI], BF16))
        wtmp = stw.enter_context(nc.sbuf_tensor("wtmp", [128, KC * D], BF16))
        xchunk = stw.enter_context(nc.sbuf_tensor("xchunk", [128, D], F32))
        sq = stw.enter_context(nc.sbuf_tensor("sq", [128, D], F32))
        xnb = stw.enter_context(nc.sbuf_tensor("xnb", [128, D], BF16))
        xnt = stw.enter_context(nc.sbuf_tensor("xnt", [128, KC * 128], BF16))
        kw2 = stw.enter_context(nc.sbuf_tensor("kw2", [128, D], F32))
        swp = stw.enter_context(nc.sbuf_tensor("swp", [128, D], F32))
        kbb = stw.enter_context(nc.sbuf_tensor("kbb", [128, D], BF16))
        ktc = stw.enter_context(nc.sbuf_tensor("ktc", [128, KC * 128], BF16))
        ctile = stw.enter_context(nc.sbuf_tensor("ctile", [128, D], F32))
        stile = stw.enter_context(nc.sbuf_tensor("stile", [128, D], F32))
        bufs.update(sq=sq, xnb=xnb, xchunk=xchunk, xt=xnt)

        def wT_simple(src_ap, rows, dst):
            rr = min(rows, 128)
            rcn = max(rows // 128, 1)
            em.wait("gpsimd", "vector", em.n["vector"])
            for rc in range(rcn):
                em.dma("gpsimd", lambda g, rc=rc, rr=rr: g.dma_start(
                    out=wtmp[0:rr, rc * D:(rc + 1) * D],
                    in_=src_ap[rc * rr:(rc + 1) * rr, :]))
            gv = em.n["gdma"]
            em.wait("tensor", "gdma", gv)
            em.wait("vector", "gdma", gv)
            for ci in range(KC):
                for rc in range(rcn):
                    tv = em.step("tensor", lambda tt, ci=ci, rc=rc, rr=rr: tt.transpose(
                        tp[0:128, 0:rr],
                        wtmp[0:rr, rc * D + ci * 128:rc * D + ci * 128 + 128],
                        idb[0:rr, 0:rr]))
                    em.wait("vector", "tensor", tv)
                    ev = em.step("vector", lambda vv, ci=ci, rc=rc, rr=rr: vv.tensor_copy(
                        out=dst[:, ci * rows + rc * rr:ci * rows + rc * rr + rr],
                        in_=tp[0:128, 0:rr]))
                    em.wait("tensor", "vector", ev)

        wT_simple(wqkv[0:D, :], D, wqT)
        wT_simple(wqkv[D:2 * D, :], D, wkT)
        wT_simple(wqkv[2 * D:3 * D, :], D, wvT)
        wT_simple(wo, D, woTb)
        wT_simple(wqi, HI * DI, wqiT)
        wT_simple(wki, DI, wkiT)
        wT_simple(wwi, HI, wwiT)
        em.barrier(("tensor", "vector", "scalar", "gpsimd", "sync"))

        wT_colstride = {id(wqT): D, id(wkT): D, id(wvT): D, id(woTb): D,
                        id(wqiT): HI * DI, id(wkiT): DI, id(wwiT): HI}

        def proj(dst_psum, wT, ncols, half):
            stride = wT_colstride[id(wT)]
            mv = None
            for kc in range(KC):
                mv = em.step("tensor", lambda tt, kc=kc, half=half, dst_psum=dst_psum,
                             wT=wT, ncols=ncols, stride=stride: tt.matmul(
                    dst_psum[:, 0:ncols],
                    xnt[:, kc * 128:(kc + 1) * 128],
                    wT[:, kc * stride + half * 512:kc * stride + half * 512 + ncols],
                    start=(kc == 0), stop=(kc == KC - 1)))
            return mv

        def emit_rope(dst, src, n_sc, n_v, tile_c, tile_s):
            sv1 = em.step("vector", lambda vv: vv.tensor_copy(
                out=swp.ap().rearrange("q (a two) -> q a two", two=2)[:, :, 0],
                in_=src.ap().rearrange("q (a two) -> q a two", two=2)[:, :, 1]))
            sv2 = em.step("vector", lambda vv: vv.tensor_copy(
                out=swp.ap().rearrange("q (a two) -> q a two", two=2)[:, :, 1],
                in_=src.ap().rearrange("q (a two) -> q a two", two=2)[:, :, 0]))
            em.step("vector", lambda vv: vv.tensor_mul(dst[:], src[:], tile_c[:]))
            em.step("vector", lambda vv: vv.tensor_mul(swp[:], swp[:], tile_s[:]))
            return em.step("vector", lambda vv: vv.tensor_add(dst[:], dst[:], swp[:]))

        for c in range(MK):
            mm = (c % 2) * 16 + (c // 2)
            emit_ln_xnT(em, bufs, mm, x_src_ap=x1g)
            em.wait("sync", "vector", em.n["vector"])
            vc1 = em.dma("sync", lambda s, c=c: s.dma_start(
                out=ctile[:], in_=cosf[c * 128:(c + 1) * 128, :]))
            vc2 = em.dma("sync", lambda s, c=c: s.dma_start(
                out=stile[:], in_=sinf[c * 128:(c + 1) * 128, :]))
            em.wait("vector", "sync", vc2)

            for half in range(2):
                pp = pk if half == 0 else pk2
                em.wait("tensor", "vector", em.n["vector"])
                mv = proj(pp, wkT, 512, half)
                em.wait("vector", "tensor", mv)
                em.step("vector", lambda vv, half=half, pp=pp: vv.tensor_copy(
                    out=kw[:, half * 512:half * 512 + 512], in_=pp[:, 0:512]))
            rv_ = emit_rope(kw2, kw, 0, 0, ctile, stile)
            vcst = em.step("vector", lambda vv: vv.tensor_copy(out=kbb[:], in_=kw2[:]))
            em.wait("tensor", "vector", vcst)
            em.wait("vector", "sync", em.n["sync"])
            for h in range(H):
                pb = (h % 2) * 64
                tv = em.step("tensor", lambda tt, h=h: tt.transpose(
                    tp[0:64, 0:128], kbb[:, h * HD:(h + 1) * HD], idb[:]))
                em.wait("vector", "tensor", tv)
                ev = em.step("vector", lambda vv, h=h, pb=pb: vv.tensor_copy(
                    out=ktc[pb:pb + 64, (h // 2) * 128:(h // 2) * 128 + 128],
                    in_=tp[0:64, 0:128]))
                em.wait("tensor", "vector", ev)
            kv_ = em.n["vector"]
            em.wait("sync", "vector", kv_)
            for kc in range(KC):
                em.dma("sync", lambda s, kc=kc, c=c: s.dma_start(
                    out=kts[:, kc * RK + c * 128:kc * RK + (c + 1) * 128],
                    in_=ktc[:, kc * 128:(kc + 1) * 128]))
            for half in range(2):
                pp = pk if half == 0 else pk2
                em.wait("tensor", "vector", em.n["vector"])
                em.wait("tensor", "sync", em.n["sync"])
                mv = proj(pp, wvT, 512, half)
                em.wait("vector", "tensor", mv)
                em.step("vector", lambda vv, half=half, pp=pp: vv.tensor_copy(
                    out=kbb[:, half * 512:half * 512 + 512], in_=pp[:, 0:512]))
            vv_ = em.n["vector"]
            em.wait("sync", "vector", vv_)
            em.dma("sync", lambda s, c=c: s.dma_start(
                out=vds[c * 128:(c + 1) * 128, :], in_=kbb[:]))
            em.wait("vector", "sync", em.n["sync"])
            em.wait("tensor", "vector", em.n["vector"])
            mv = proj(pk, wkiT, DI, 0)
            em.wait("vector", "tensor", mv)
            vki0 = em.step("vector", lambda vv: vv.tensor_copy(out=kw2[:, 0:DI], in_=pk[:, 0:DI]))
            vki = em.step("vector", lambda vv: vv.tensor_copy(out=kbb[:, 0:DI], in_=kw2[:, 0:DI]))
            em.wait("tensor", "vector", vki)
            tv = em.step("tensor", lambda tt: tt.transpose(tp[0:64, 0:128], kbb[:, 0:DI], idb[:]))
            em.wait("vector", "tensor", tv)
            em.step("vector", lambda vv, c=c: vv.tensor_copy(
                out=kitb[0:64, c * 128:(c + 1) * 128], in_=tp[0:64, 0:128]))
            ev = em.step("vector", lambda vv, c=c: vv.tensor_copy(
                out=kitb[64:128, c * 128:(c + 1) * 128], in_=tp[0:64, 0:128]))
            em.wait("tensor", "vector", ev)

            for half in range(2):
                pp = pk if half == 0 else pk2
                em.wait("tensor", "vector", em.n["vector"])
                mv = proj(pp, wqT, 512, half)
                em.wait("vector", "tensor", mv)
                em.step("vector", lambda vv, half=half, pp=pp: vv.tensor_copy(
                    out=kw[:, half * 512:half * 512 + 512], in_=pp[:, 0:512]))
            rv_ = emit_rope(kw2, kw, 0, 0, ctile, stile)
            vcst = em.step("vector", lambda vv: vv.tensor_copy(out=kbb[:], in_=kw2[:]))
            em.wait("tensor", "vector", vcst)
            em.wait("vector", "sync", em.n["sync"])
            for h in range(H):
                pb = (h % 2) * 64
                tv = em.step("tensor", lambda tt, h=h: tt.transpose(
                    tp[0:64, 0:128], kbb[:, h * HD:(h + 1) * HD], idb[:]))
                em.wait("vector", "tensor", tv)
                ev = em.step("vector", lambda vv, h=h, pb=pb: vv.tensor_copy(
                    out=ktc[pb:pb + 64, (h // 2) * 128:(h // 2) * 128 + 128],
                    in_=tp[0:64, 0:128]))
                em.wait("tensor", "vector", ev)
            qv_ = em.n["vector"]
            em.wait("sync", "vector", qv_)
            for kc in range(KC):
                em.dma("sync", lambda s, kc=kc, c=c: s.dma_start(
                    out=qts_d[:, kc * RK + c * 128:kc * RK + (c + 1) * 128],
                    in_=ktc[:, kc * 128:(kc + 1) * 128]))
            em.wait("tensor", "vector", em.n["vector"])
            mv = proj(pk, wqiT, HI * DI, 0)
            em.wait("vector", "tensor", mv)
            vq = em.step("vector", lambda vv: vv.tensor_copy(out=kbb[:, 0:HI * DI], in_=pk[:, 0:HI * DI]))
            em.wait("tensor", "vector", vq)
            em.wait("vector", "sync", em.n["sync"])
            for hi in range(HI):
                pb = (hi % 2) * 64
                tv = em.step("tensor", lambda tt, hi=hi: tt.transpose(
                    tp[0:64, 0:128], kbb[:, hi * DI:(hi + 1) * DI], idb[:]))
                em.wait("vector", "tensor", tv)
                ev = em.step("vector", lambda vv, hi=hi, pb=pb: vv.tensor_copy(
                    out=qitc[pb:pb + 64, (hi // 2) * 128:(hi // 2) * 128 + 128],
                    in_=tp[0:64, 0:128]))
                em.wait("tensor", "vector", ev)
            qiv_ = em.n["vector"]
            em.wait("sync", "vector", qiv_)
            for blk in range(HI // 2):
                em.dma("sync", lambda s, blk=blk, c=c: s.dma_start(
                    out=qits_d[:, blk * RK + c * 128:blk * RK + (c + 1) * 128],
                    in_=qitc[:, blk * 128:(blk + 1) * 128]))
            em.wait("tensor", "vector", em.n["vector"])
            mv = proj(pk, wwiT, HI, 0)
            em.wait("vector", "tensor", mv)
            em.wait("vector", "sync", em.n["sync"])
            wv_ = em.step("vector", lambda vv: vv.tensor_copy(
                out=kw2[:, 0:HI], in_=pk[:, 0:HI]))
            em.wait("sync", "vector", wv_)
            em.dma("sync", lambda s, c=c: s.dma_start(
                out=wis_d[:, c * HI:(c + 1) * HI], in_=kw2[:, 0:HI]))

        em.barrier(("tensor", "vector", "scalar", "sync"))

    def _gather_switch(eng, sems):
        for arm in nc.switch_core_id(engines=[nc.sync], n=128):
            pcore = arm.logical % 2
            for rcg in range(M):
                ab = 2 * rcg + pcore
                eng.dma_start(
                    out=qtb.ap().rearrange("q (kc n) -> q kc n", kc=KC)[:, :, rcg * 128:(rcg + 1) * 128],
                    in_=qts_d.rearrange("q (kc n) -> q kc n", kc=KC)[:, :, ab * 128:(ab + 1) * 128],
                ).then_inc(sems["sync"], 16)
                eng.dma_start(
                    out=qitb.ap().rearrange("q (b n) -> q b n", b=HI // 2)[:, :, rcg * 128:(rcg + 1) * 128],
                    in_=qits_d.rearrange("q (b n) -> q b n", b=HI // 2)[:, :, ab * 128:(ab + 1) * 128],
                ).then_inc(sems["sync"], 16)
            eng.dma_start(
                out=wis.ap().rearrange("q (a b) -> q a b", b=HI)[:, :, :],
                in_=wis_d.rearrange("q (a two b) -> q a two b", two=2, b=HI)[:, :, pcore, :],
            ).then_inc(sems["sync"], 16)
    em.wait("sync", "sync", em.n["sync"])
    em.raw("sync", _gather_switch, sem="sync", inc=16 * (2 * M + 1))
    em.barrier(("tensor", "vector", "scalar", "sync"))


    lo = mini[:, 0:1]; hi_ = mini[:, 1:2]; mid = mini[:, 2:3]
    negmid = mini[:, 3:4]; Fst = mini[:, 4:5]
    c_hi = mini[:, 7:8]; c_gt = mini[:, 8:9]; needf = mini[:, 9:10]
    cm = mini[:, 10:11]; mnew = mini[:, 11:12]; sco = mini[:, 12:13]
    bsum = mini[:, 13:14]; negmn = mini[:, 14:15]; rec = mini[:, 15:16]

    idx = sb("idx", [128, 4096], F32)
    giB = sb("giB", [128, 4096], I32)
    giC = sb("giC", [128, 4096], I32)
    giE = sb("giE", [128, 4096], I32)
    ktile = sb("ktile", [128, 2 * KC * 512], BF16)
    vtile = sb("vtile", [128, 2 * 4 * 1040], BF16)
    accS = sb("accS", [128, H * 65], F32)
    em.step("vector", lambda vv: vv.memset(vtile[:], 1.0))
    em.wait("gpsimd", "vector", em.n["vector"])

    def Wof(rc):
        return 512 if rc < 2 else min(((2 * rc + 2) * 128 + 511) // 512 * 512, RK)

    fl = dict(ktb=[0, 0], vtb=[0, 0], exp=[0, 0], av_evac=[0] * H,
              tp_ev=0, aot_ev=0, madd=[0, 0], wo_ev=0, y1_dma=0,
              mskT_rd=[0, 0])
    sel = dict(relu2=[0, 0], stt2=[0, 0], cam=0, sign=0, mk=[0, 0])

    def gen_selection(rc):
        W = Wof(rc)
        NSC = W // 512
        mb0 = (rc % 2) * 4096
        jj = 0
        for sc in range(NSC):
            for hi in range(HI):
                pb = (hi % 2) * 64
                b0 = (jj % 2) * 512
                if sel["relu2"][jj % 2]:
                    em.wait("tensor", "scalar", sel["relu2"][jj % 2])
                mv = em.step("tensor", lambda tt, hi=hi, pb=pb, rc=rc, sc=sc, b0=b0: tt.matmul(
                    psc1[:, b0:b0 + 512],
                    qitb[pb:pb + 64, (hi // 2) * T + rc * 128:(hi // 2) * T + rc * 128 + 128],
                    kitb[pb:pb + 64, sc * 512:(sc + 1) * 512],
                    start=True, stop=True))
                em.wait("scalar", "tensor", mv)
                if sel["stt2"][jj % 2]:
                    em.wait("scalar", "vector", sel["stt2"][jj % 2])
                sel["relu2"][jj % 2] = em.step("scalar", lambda sc_, b0=b0: sc_.activation(
                    kw[:, b0:b0 + 512], psc1[:, b0:b0 + 512], AF.Relu, bias=0.0, scale=1.0))
                em.wait("vector", "scalar", sel["relu2"][jj % 2])
                if hi == 0:
                    sel["stt2"][jj % 2] = em.step("vector", lambda vv, rc=rc, sc=sc, b0=b0: vv.tensor_scalar(
                        idx[:, sc * 512:(sc + 1) * 512], kw[:, b0:b0 + 512],
                        wis[:, rc * HI:rc * HI + 1], None, op0=ALU.mult))
                else:
                    sel["stt2"][jj % 2] = em.step("vector", lambda vv, rc=rc, sc=sc, hi=hi, b0=b0: vv.scalar_tensor_tensor(
                        idx[:, sc * 512:(sc + 1) * 512], kw[:, b0:b0 + 512],
                        wis[:, rc * HI + hi:rc * HI + hi + 1],
                        idx[:, sc * 512:(sc + 1) * 512],
                        op0=ALU.mult, op1=ALU.add))
                jj += 1
                yield "idx"
        if sel["cam"]:
            em.wait("sync", "vector", sel["cam"])
        if sel["sign"]:
            em.wait("sync", "scalar", sel["sign"])
        cav = em.dma("sync", lambda s, rc=rc, W=W: s.dma_start(
            out=catile[:, 0:W], in_=camask[rc * 128:(rc + 1) * 128, 0:W]))
        em.wait("vector", "sync", cav)
        sel["cam"] = em.step("vector", lambda vv, W=W: vv.tensor_tensor(
            idx[:, 0:W], idx[:, 0:W], catile[:, 0:W], op=ALU.add))
        Wfloor = max((2 * rc * 128 + 1) // 512 * 512, 512)
        em.step("vector", lambda vv, Wfloor=Wfloor: vv.tensor_reduce(
            lo, idx[:, 0:Wfloor], AX.X, ALU.min))
        em.step("vector", lambda vv: vv.tensor_scalar(lo, lo, 1.0, None, op0=ALU.subtract))
        em.step("vector", lambda vv, W=W: vv.tensor_reduce(
            hi_, idx[:, 0:W], AX.X, ALU.max))
        em.step("vector", lambda vv: vv.tensor_scalar(hi_, hi_, 1.0, None, op0=ALU.add))
        yield
        thr = float(1024 - W)
        for it in range(NB):
            vv1 = em.step("vector", lambda vv: vv.tensor_add(mid, lo, hi_))
            vv1 = em.step("vector", lambda vv: vv.tensor_scalar(mid, mid, 0.5, None, op0=ALU.mult))
            vv1 = em.step("vector", lambda vv: vv.tensor_scalar(negmid, mid, -1.0, None, op0=ALU.mult))
            em.wait("scalar", "vector", vv1)
            sv = em.step("scalar", lambda sc_, W=W: sc_.activation(
                catile[:, 0:W], idx[:, 0:W], AF.Sign, bias=negmid, scale=1.0,
                accum_out=Fst))
            sel["sign"] = sv
            em.wait("vector", "scalar", sv)
            em.step("vector", lambda vv, thr=thr: vv.tensor_scalar(
                fint[:, 0:1], Fst, thr, 1, op0=ALU.is_ge, op1=ALU.mult))
            em.step("vector", lambda vv, thr=thr: vv.tensor_scalar(
                fint[:, 1:2], Fst, thr, 1, op0=ALU.is_lt, op1=ALU.mult))
            em.step("vector", lambda vv: vv.select(lo, fint[:, 0:1], mid, lo))
            em.step("vector", lambda vv: vv.select(hi_, fint[:, 1:2], mid, hi_))
            yield
        em.step("vector", lambda vv, W=W: vv.tensor_scalar(
            giB[:, 0:W], idx[:, 0:W], lo, 1, op0=ALU.is_ge, op1=ALU.mult))
        em.step("vector", lambda vv, W=W: vv.tensor_scalar(
            giC[:, 0:W], idx[:, 0:W], hi_, 1, op0=ALU.is_ge, op1=ALU.mult))
        em.step("vector", lambda vv, W=W: vv.tensor_scalar(
            giE[:, 0:W], idx[:, 0:W], hi_, 1, op0=ALU.is_gt, op1=ALU.mult))
        yield
        em.step("vector", lambda vv, W=W: vv.tensor_tensor_scan(
            idx[:, 0:W], giC[:, 0:W], giC[:, 0:W], 0, op0=ALU.add, op1=ALU.bypass))
        em.step("vector", lambda vv, W=W: vv.tensor_copy(out=c_hi, in_=idx[:, W - 1:W]))
        em.step("vector", lambda vv, W=W: vv.tensor_tensor_scan(
            idx[:, 0:W], giE[:, 0:W], giE[:, 0:W], 0, op0=ALU.add, op1=ALU.bypass))
        em.step("vector", lambda vv, W=W: vv.tensor_copy(out=c_gt, in_=idx[:, W - 1:W]))
        yield
        em.step("vector", lambda vv: vv.tensor_scalar(
            fint[:, 0:1], c_hi, 512.0, 1, op0=ALU.is_ge, op1=ALU.mult))
        em.step("vector", lambda vv, W=W: vv.select(
            giB[:, 0:W], fint[:, 0:1].to_broadcast([128, W]), giC[:, 0:W], giB[:, 0:W]))
        em.step("vector", lambda vv, W=W: vv.select(
            giC[:, 0:W], fint[:, 0:1].to_broadcast([128, W]), giE[:, 0:W], giC[:, 0:W]))
        em.step("vector", lambda vv, W=W: vv.tensor_sub(giB[:, 0:W], giB[:, 0:W], giC[:, 0:W]))
        yield
        em.step("vector", lambda vv, W=W: vv.tensor_tensor_scan(
            idx[:, 0:W], giB[:, 0:W], giB[:, 0:W], 0, op0=ALU.add, op1=ALU.bypass))
        em.step("vector", lambda vv: vv.tensor_scalar(
            fint[:, 1:2], c_hi, 512.0, 1, op0=ALU.is_lt, op1=ALU.mult))
        em.step("vector", lambda vv: vv.select(c_gt, fint[:, 1:2], c_hi, c_gt))
        em.step("vector", lambda vv: vv.tensor_scalar(
            needf, c_gt, 512.0, -1.0, op0=ALU.subtract, op1=ALU.mult))
        yield
        em.step("vector", lambda vv, W=W: vv.tensor_scalar(
            giE[:, 0:W], idx[:, 0:W], needf, 1, op0=ALU.is_le, op1=ALU.mult))
        em.step("vector", lambda vv, W=W: vv.tensor_mul(giB[:, 0:W], giB[:, 0:W], giE[:, 0:W]))
        em.step("vector", lambda vv, W=W: vv.tensor_add(giC[:, 0:W], giC[:, 0:W], giB[:, 0:W]))
        yield
        if fl["mskT_rd"][rc % 2]:
            em.wait("vector", "tensor", fl["mskT_rd"][rc % 2])
        sel["mk"][rc % 2] = em.step("vector", lambda vv, W=W, mb0=mb0: vv.tensor_scalar(
            mask01[:, mb0:mb0 + W], giC[:, 0:W], 1.0, 1e30,
            op0=ALU.subtract, op1=ALU.mult))

    def gen_flash(rc):
        W = Wof(rc)
        special = rc < 2
        nsb = W // 512
        em.step("vector", lambda vv: vv.memset(accS[:], 0.0))
        v_lds = {}

        def block_prologue(sb_):
            tb = sb_ % 2
            kb0 = tb * KC * 512
            vb0 = tb * 4 * 1040
            if fl["ktb"][tb]:
                em.wait("sync", "tensor", fl["ktb"][tb])
            for kc in range(KC):
                em.dma("sync", lambda s, kc=kc, sb_=sb_, kb0=kb0: s.dma_start(
                    out=ktile[:, kb0 + kc * 512:kb0 + (kc + 1) * 512],
                    in_=kts[:, kc * RK + sb_ * 512:kc * RK + (sb_ + 1) * 512]))
            k_ld = em.n["sync"]
            if fl["vtb"][tb]:
                em.wait("gpsimd", "tensor", fl["vtb"][tb])
            for s2 in range(4):
                em.dma("gpsimd", lambda g, s2=s2, sb_=sb_, vb0=vb0: g.dma_start(
                    out=vtile.ap()[:, vb0 + s2 * 1040:vb0 + (s2 + 1) * 1040].rearrange(
                        "p (h c) -> p h c", h=H)[:, :, 0:HD],
                    in_=vds[sb_ * 512 + s2 * 128:sb_ * 512 + (s2 + 1) * 128, :].rearrange(
                        "p (h c) -> p h c", h=H),
                    allow_slow_non_contiguous=True))
            v_lds[sb_] = em.n["gdma"]
            if not special:
                if sb_ == 0:
                    em.wait("tensor", "vector", sel["mk"][rc % 2])
                    if fl["aot_ev"]:
                        em.wait("tensor", "vector", fl["aot_ev"])
                if fl["tp_ev"]:
                    em.wait("tensor", "scalar", fl["tp_ev"])
                if fl["madd"][tb]:
                    em.wait("scalar", "tensor", fl["madd"][tb])
                tv = None
                for c in range(4):
                    tv = em.step("tensor", lambda tt, c=c, sb_=sb_, rc=rc: tt.transpose(
                        tp[:, c * 128:(c + 1) * 128],
                        mask01[:, (rc % 2) * 4096 + sb_ * 512 + c * 128:
                               (rc % 2) * 4096 + sb_ * 512 + c * 128 + 128],
                        idb[:]))
                fl["mskT_rd"][rc % 2] = tv
                em.wait("scalar", "tensor", tv)
                fl["tp_ev"] = em.step("scalar", lambda sc_, tb=tb: sc_.activation(
                    mtb[:, tb * 512:tb * 512 + 512], tp[:, 0:512],
                    AF.Copy, bias=0.0, scale=1.0))
            em.wait("tensor", "sync", k_ld)

        def emit_scores(u, sb_, h):
            tb = sb_ % 2
            kb0 = tb * KC * 512
            pb = (h % 2) * 64
            p0 = (u % 2) * 512
            if fl["exp"][u % 2]:
                em.wait("tensor", "scalar", fl["exp"][u % 2])
            if u < 2 and fl["wo_ev"]:
                em.wait("tensor", "vector", fl["wo_ev"])
            if not special and h == 0:
                em.wait("tensor", "scalar", fl["tp_ev"])
            mv = None
            for c in range(4):
                mv = em.step("tensor", lambda tt, h=h, pb=pb, rc=rc, c=c, p0=p0, kb0=kb0, special=special: tt.matmul(
                    psc0[:, p0 + c * 128:p0 + c * 128 + 128],
                    ktile[pb:pb + 64, kb0 + (h // 2) * 512 + c * 128:kb0 + (h // 2) * 512 + c * 128 + 128],
                    qtb[pb:pb + 64, (h // 2) * T + rc * 128:(h // 2) * T + rc * 128 + 128],
                    start=True, stop=special))
                if not special:
                    mv = em.step("tensor", lambda tt, c=c, tb=tb, p0=p0: tt.matmul(
                        psc0[:, p0 + c * 128:p0 + c * 128 + 128],
                        idb[:],
                        mtb[:, tb * 512 + c * 128:tb * 512 + c * 128 + 128],
                        start=False, stop=True))
            if h == H - 1:
                fl["ktb"][tb] = mv
                if not special:
                    fl["madd"][tb] = mv
            em.wait("scalar", "tensor", mv)
            fl["exp"][u % 2] = em.step("scalar", lambda sc_, u=u, p0=p0: sc_.activation(
                probsT[:, (u % 3) * 512:(u % 3) * 512 + 512], psc0[:, p0:p0 + 512],
                AF.Exp, bias=0.0, scale=0.125))

        def emit_av(u, sb_, h):
            tb = sb_ % 2
            vb0 = tb * 4 * 1040
            em.wait("tensor", "scalar", fl["exp"][u % 2])
            if h == 0:
                em.wait("tensor", "gdma", v_lds[sb_])
            reg = (h // 7) * 512 + (h % 7) * 65
            if fl["av_evac"][h]:
                em.wait("tensor", "vector", fl["av_evac"][h])
            mv = None
            for c in range(4):
                mv = em.step("tensor", lambda tt, c=c, h=h, u=u, reg=reg, vb0=vb0: tt.matmul(
                    pav[:, reg:reg + 65],
                    probsT[:, (u % 3) * 512 + c * 128:(u % 3) * 512 + c * 128 + 128],
                    vtile[:, vb0 + c * 1040 + h * 65:vb0 + c * 1040 + h * 65 + 65],
                    start=(c == 0), stop=(c == 3)))
            if h == H - 1:
                fl["vtb"][tb] = mv
            em.wait("vector", "tensor", mv)
            fl["av_evac"][h] = em.step("vector", lambda vv, h=h, reg=reg: vv.tensor_add(
                accS[:, h * 65:(h + 1) * 65],
                accS[:, h * 65:(h + 1) * 65], pav[:, reg:reg + 65]))

        units = [(sb_, h) for sb_ in range(nsb) for h in range(H)]
        pend = []
        for u, (sb_, h) in enumerate(units):
            if h == 0:
                block_prologue(sb_)
            emit_scores(u, sb_, h)
            pend.append((u, sb_, h))
            if len(pend) > PIPE_AV:
                emit_av(*pend.pop(0))
            if u % 2 == 1 and not special:
                yield
        while pend:
            emit_av(*pend.pop(0))
        for h in range(H):
            em.step("vector", lambda vv, h=h: vv.reciprocal(rec, accS[:, h * 65 + 64:h * 65 + 65]))
            em.step("vector", lambda vv, h=h: vv.tensor_scalar(
                aob[:, h * HD:(h + 1) * HD], accS[:, h * 65:h * 65 + 64],
                rec, None, op0=ALU.mult))
        av = em.n["vector"]
        em.wait("tensor", "vector", av)
        if fl["tp_ev"]:
            em.wait("tensor", "scalar", fl["tp_ev"])
        for g in range(2):
            tv = None
            for j in range(4):
                jj = g * 4 + j
                tv = em.step("tensor", lambda tt, jj=jj, j=j: tt.transpose(
                    tp[:, j * 128:(j + 1) * 128],
                    aob[:, jj * 128:(jj + 1) * 128], idb[:]))
            em.wait("vector", "tensor", tv)
            ev = None
            for j in range(4):
                jj = g * 4 + j
                ev = em.step("vector", lambda vv, jj=jj, j=j: vv.tensor_copy(
                    out=aotT[:, jj * 128:(jj + 1) * 128],
                    in_=tp[:, j * 128:(j + 1) * 128]))
            em.wait("tensor", "vector", ev)
        fl["aot_ev"] = em.n["vector"]
        yield
        em.wait("tensor", "scalar", max(fl["exp"]))
        if fl["y1_dma"]:
            em.wait("vector", "sync", fl["y1_dma"])
        for half in range(2):
            mv = None
            for kc in range(KC):
                mv = em.step("tensor", lambda tt, kc=kc, half=half: tt.matmul(
                    psc0[:, half * 512:half * 512 + 512],
                    aotT[:, kc * 128:(kc + 1) * 128],
                    woTb[:, kc * D + half * 512:kc * D + half * 512 + 512],
                    start=(kc == 0), stop=(kc == KC - 1)))
            em.wait("vector", "tensor", mv)
            em.step("vector", lambda vv, half=half: vv.tensor_copy(
                out=yo[:, half * 512:half * 512 + 512],
                in_=psc0[:, half * 512:half * 512 + 512]))
        fl["wo_ev"] = em.n["vector"]
        em.wait("sync", "vector", fl["wo_ev"])
        fl["y1_dma"] = em.dma("sync", lambda s, rc=rc: s.dma_start(
            out=y1_d[rc * 128:(rc + 1) * 128, :], in_=yo[:]))

    for i in range(M):
        fgen = gen_flash(i)
        sgen = gen_selection(i + 1) if 2 <= i + 1 < M else None
        if INTERLEAVE:
            done_f = done_s = sgen is None
            done_f = False
            while not (done_f and done_s):
                if not done_s:
                    try:
                        tag = next(sgen)
                        if tag == "idx" and not IL_IDX:
                            while tag == "idx":
                                tag = next(sgen)
                    except StopIteration:
                        done_s = True
                for _ in range(3):
                    if not done_f:
                        try:
                            next(fgen)
                        except StopIteration:
                            done_f = True
        else:
            for _ in fgen:
                pass
            if sgen is not None:
                for _ in sgen:
                    pass
    em.wait("sync", "sync", em.n["sync"])
    em.wait("sync", "gdma", em.n["gdma"])



def build_launchD(p):
    RK = 2048 if p == 0 else 4096
    MK = RK // 128
    BASE = p * 2048
    HI = 4
    DI = 64
    NB = 27

    nc = bass.Bass("TRN2", target_bir_lowering=False, detect_race_conditions=False)
    x1o = nc.dram_tensor("x1o", [T, D], F32, kind="ExternalInput").ap()
    x1s = nc.dram_tensor("x1s", [T, D], F32, kind="ExternalInput").ap()
    wqkv = nc.dram_tensor("wqkv", [3 * D, D], F32, kind="ExternalInput").ap()
    wo = nc.dram_tensor("wo", [D, D], F32, kind="ExternalInput").ap()
    wqi = nc.dram_tensor("wqi", [HI * DI, D], F32, kind="ExternalInput").ap()
    wki = nc.dram_tensor("wki", [DI, D], F32, kind="ExternalInput").ap()
    wwi = nc.dram_tensor("wwi", [HI, D], F32, kind="ExternalInput").ap()
    iden = nc.dram_tensor("iden", [128, 128], F32, kind="ExternalInput").ap()
    cosf = nc.dram_tensor("cosf", [4096, D], F32, kind="ExternalInput").ap()
    sinf = nc.dram_tensor("sinf", [4096, D], F32, kind="ExternalInput").ap()
    tri = nc.dram_tensor("tri", [128, 128], F32, kind="ExternalInput").ap()
    y_out = nc.dram_tensor("y_out", [T, D], F32, kind="ExternalOutput").ap()
    kts = nc.dram_tensor("kts", [128, KC * RK], BF16, kind="Internal").ap()
    vds = nc.dram_tensor("vds", [RK, D], BF16, kind="Internal").ap()

    em = Em()
    with contextlib.ExitStack() as st:
        def sb(name, shape, dtype):
            return st.enter_context(nc.sbuf_tensor(name, shape, dtype))

        def ps(name, shape, dtype):
            return _ps_pad(nc, st, name, shape, dtype)

        qtb = sb("qtb", [128, (H // 2) * T], BF16)
        qitb = sb("qitb", [128, (HI // 2) * T], BF16)
        kitb = sb("kitb", [128, RK], BF16)
        wis = sb("wis", [128, M * HI], F32)
        woTb = sb("woTb", [128, KC * D], BF16)
        idf = sb("idf", [128, 128], F32)
        idb = sb("idb", [128, 128], BF16)
        tris = sb("tris", [128, 128], F32)
        mini = sb("mini", [128, 16], F32)
        fint = sb("fint", [128, 2], I32)
        kw = sb("kw", [128, D], F32)
        macc = sb("macc", [128, H], F32)
        lacc = sb("lacc", [128, H], F32)
        acc = sb("acc", [128, H * HD], F32)
        aob = sb("aob", [128, D], BF16)
        aotT = sb("aotT", [128, KC * 128], BF16)
        yo = sb("yo", [128, D], F32)
        mask01 = sb("mask01", [128, 4096], BF16)
        probs = sb("probs", [128, 1024], BF16)
        ptile = sb("ptile", [128, 512], BF16)

        tp = ps("tp", [128, 1024], BF16)
        pk = ps("pk", [128, 512], F32)
        pk2 = ps("pk2", [128, 512], F32)
        psc = ps("psc", [128, 1024], F32)
        pav = ps("pav", [128, 512], F32)

        bufs = dict(mini=mini, tp=tp, idb=idb,
                    eps=mini[:, 6:7], xt_stride=128, xt_off=0)
        em.step("vector", lambda vv: vv.memset(mini[:], 0.0))
        em.step("vector", lambda vv: vv.memset(mini[:, 6:7], 1e-5))
        vid = em.dma("sync", lambda s: s.dma_start(out=idf[:], in_=iden[:, :]))
        vtr = em.dma("sync", lambda s: s.dma_start(out=tris[:], in_=tri[:, :]))
        gid = em.dma("gpsimd", lambda g: g.dma_start(out=idb[:], in_=iden[:, :]))
        em.wait("vector", "sync", vtr)
        em.wait("vector", "gdma", gid)
        em.wait("tensor", "gdma", gid)

        with contextlib.ExitStack() as stw:
            wqT = stw.enter_context(nc.sbuf_tensor("wqT", [128, KC * D], BF16))
            wkT = stw.enter_context(nc.sbuf_tensor("wkT", [128, KC * D], BF16))
            wvT = stw.enter_context(nc.sbuf_tensor("wvT", [128, KC * D], BF16))
            wqiT = stw.enter_context(nc.sbuf_tensor("wqiT", [128, KC * HI * DI], BF16))
            wkiT = stw.enter_context(nc.sbuf_tensor("wkiT", [128, KC * DI], BF16))
            wwiT = stw.enter_context(nc.sbuf_tensor("wwiT", [128, KC * HI], BF16))
            wtmp = stw.enter_context(nc.sbuf_tensor("wtmp", [128, KC * D], BF16))
            xchunk = stw.enter_context(nc.sbuf_tensor("xchunk", [128, D], F32))
            sq = stw.enter_context(nc.sbuf_tensor("sq", [128, D], F32))
            xnb = stw.enter_context(nc.sbuf_tensor("xnb", [128, D], BF16))
            xnt = stw.enter_context(nc.sbuf_tensor("xnt", [128, KC * 128], BF16))
            kw2 = stw.enter_context(nc.sbuf_tensor("kw2", [128, D], F32))
            swp = stw.enter_context(nc.sbuf_tensor("swp", [128, D], F32))
            kbb = stw.enter_context(nc.sbuf_tensor("kbb", [128, D], BF16))
            ktc = stw.enter_context(nc.sbuf_tensor("ktc", [128, KC * 128], BF16))
            ctile = stw.enter_context(nc.sbuf_tensor("ctile", [128, D], F32))
            stile = stw.enter_context(nc.sbuf_tensor("stile", [128, D], F32))
            bufs.update(sq=sq, xnb=xnb, xchunk=xchunk, xt=xnt)

            def wT_simple(src_ap, rows, dst):
                rr = min(rows, 128)
                rcn = max(rows // 128, 1)
                em.wait("gpsimd", "vector", em.n["vector"])
                for rc in range(rcn):
                    em.dma("gpsimd", lambda g, rc=rc, rr=rr: g.dma_start(
                        out=wtmp[0:rr, rc * D:(rc + 1) * D],
                        in_=src_ap[rc * rr:(rc + 1) * rr, :]))
                gv = em.n["gdma"]
                em.wait("tensor", "gdma", gv)
                em.wait("vector", "gdma", gv)
                for ci in range(KC):
                    for rc in range(rcn):
                        tv = em.step("tensor", lambda tt, ci=ci, rc=rc, rr=rr: tt.transpose(
                            tp[0:128, 0:rr],
                            wtmp[0:rr, rc * D + ci * 128:rc * D + ci * 128 + 128],
                            idb[0:rr, 0:rr]))
                        em.wait("vector", "tensor", tv)
                        ev = em.step("vector", lambda vv, ci=ci, rc=rc, rr=rr: vv.tensor_copy(
                            out=dst[:, ci * rows + rc * rr:ci * rows + rc * rr + rr],
                            in_=tp[0:128, 0:rr]))
                        em.wait("tensor", "vector", ev)

            wT_simple(wqkv[0:D, :], D, wqT)
            wT_simple(wqkv[D:2 * D, :], D, wkT)
            wT_simple(wqkv[2 * D:3 * D, :], D, wvT)
            wT_simple(wo, D, woTb)
            wT_simple(wqi, HI * DI, wqiT)
            wT_simple(wki, DI, wkiT)
            wT_simple(wwi, HI, wwiT)
            em.barrier(("tensor", "vector", "scalar", "gpsimd", "sync"))

            wT_colstride = {id(wqT): D, id(wkT): D, id(wvT): D, id(woTb): D,
                            id(wqiT): HI * DI, id(wkiT): DI, id(wwiT): HI}

            def proj(dst_psum, wT, ncols, half):
                stride = wT_colstride[id(wT)]
                mv = None
                for kc in range(KC):
                    mv = em.step("tensor", lambda tt, kc=kc, half=half, dst_psum=dst_psum,
                                 wT=wT, ncols=ncols, stride=stride: tt.matmul(
                        dst_psum[:, 0:ncols],
                        xnt[:, kc * 128:(kc + 1) * 128],
                        wT[:, kc * stride + half * 512:kc * stride + half * 512 + ncols],
                        start=(kc == 0), stop=(kc == KC - 1)))
                return mv

            def emit_rope(dst, src, n_sc, n_v, tile_c, tile_s):
                sv1 = em.step("vector", lambda vv: vv.tensor_copy(
                    out=swp.ap().rearrange("q (a two) -> q a two", two=2)[:, :, 0],
                    in_=src.ap().rearrange("q (a two) -> q a two", two=2)[:, :, 1]))
                sv2 = em.step("vector", lambda vv: vv.tensor_copy(
                    out=swp.ap().rearrange("q (a two) -> q a two", two=2)[:, :, 1],
                    in_=src.ap().rearrange("q (a two) -> q a two", two=2)[:, :, 0]))
                em.step("vector", lambda vv: vv.tensor_mul(dst[:], src[:], tile_c[:]))
                em.step("vector", lambda vv: vv.tensor_mul(swp[:], swp[:], tile_s[:]))
                return em.step("vector", lambda vv: vv.tensor_add(dst[:], dst[:], swp[:]))

            for c in range(MK):
                if (p == 0 and c < 16) or (p == 1 and c >= 16):
                    src, mm = x1o, c % 16
                else:
                    src, mm = x1s, c % 16
                own = (BASE // 128) <= c < (BASE // 128) + 16
                mo = c - BASE // 128
                emit_ln_xnT(em, bufs, mm, x_src_ap=src)
                em.wait("sync", "vector", em.n["vector"])
                vc1 = em.dma("sync", lambda s, c=c: s.dma_start(
                    out=ctile[:], in_=cosf[c * 128:(c + 1) * 128, :]))
                vc2 = em.dma("sync", lambda s, c=c: s.dma_start(
                    out=stile[:], in_=sinf[c * 128:(c + 1) * 128, :]))
                em.wait("vector", "sync", vc2)

                for half in range(2):
                    pp = pk if half == 0 else pk2
                    em.wait("tensor", "vector", em.n["vector"])
                    mv = proj(pp, wkT, 512, half)
                    em.wait("vector", "tensor", mv)
                    em.step("vector", lambda vv, half=half, pp=pp: vv.tensor_copy(
                        out=kw[:, half * 512:half * 512 + 512], in_=pp[:, 0:512]))
                rv_ = emit_rope(kw2, kw, 0, 0, ctile, stile)
                vcst = em.step("vector", lambda vv: vv.tensor_copy(out=kbb[:], in_=kw2[:]))
                em.wait("tensor", "vector", vcst)
                em.wait("vector", "sync", em.n["sync"])
                for h in range(H):
                    pb = (h % 2) * 64
                    tv = em.step("tensor", lambda tt, h=h: tt.transpose(
                        tp[0:64, 0:128], kbb[:, h * HD:(h + 1) * HD], idb[:]))
                    em.wait("vector", "tensor", tv)
                    ev = em.step("vector", lambda vv, h=h, pb=pb: vv.tensor_copy(
                        out=ktc[pb:pb + 64, (h // 2) * 128:(h // 2) * 128 + 128],
                        in_=tp[0:64, 0:128]))
                    em.wait("tensor", "vector", ev)
                kv_ = em.n["vector"]
                em.wait("sync", "vector", kv_)
                for kc in range(KC):
                    em.dma("sync", lambda s, kc=kc, c=c: s.dma_start(
                        out=kts[:, kc * RK + c * 128:kc * RK + (c + 1) * 128],
                        in_=ktc[:, kc * 128:(kc + 1) * 128]))
                for half in range(2):
                    pp = pk if half == 0 else pk2
                    em.wait("tensor", "vector", em.n["vector"])
                    em.wait("tensor", "sync", em.n["sync"])
                    mv = proj(pp, wvT, 512, half)
                    em.wait("vector", "tensor", mv)
                    em.step("vector", lambda vv, half=half, pp=pp: vv.tensor_copy(
                        out=kbb[:, half * 512:half * 512 + 512], in_=pp[:, 0:512]))
                vv_ = em.n["vector"]
                em.wait("sync", "vector", vv_)
                em.dma("sync", lambda s, c=c: s.dma_start(
                    out=vds[c * 128:(c + 1) * 128, :], in_=kbb[:]))
                em.wait("vector", "sync", em.n["sync"])
                em.wait("tensor", "vector", em.n["vector"])
                mv = proj(pk, wkiT, DI, 0)
                em.wait("vector", "tensor", mv)
                vki0 = em.step("vector", lambda vv: vv.tensor_copy(out=kw2[:, 0:DI], in_=pk[:, 0:DI]))
                vki = em.step("vector", lambda vv: vv.tensor_copy(out=kbb[:, 0:DI], in_=kw2[:, 0:DI]))
                em.wait("tensor", "vector", vki)
                tv = em.step("tensor", lambda tt: tt.transpose(tp[0:64, 0:128], kbb[:, 0:DI], idb[:]))
                em.wait("vector", "tensor", tv)
                em.step("vector", lambda vv, c=c: vv.tensor_copy(
                    out=kitb[0:64, c * 128:(c + 1) * 128], in_=tp[0:64, 0:128]))
                ev = em.step("vector", lambda vv, c=c: vv.tensor_copy(
                    out=kitb[64:128, c * 128:(c + 1) * 128], in_=tp[0:64, 0:128]))
                em.wait("tensor", "vector", ev)

                if own:
                    for half in range(2):
                        pp = pk if half == 0 else pk2
                        em.wait("tensor", "vector", em.n["vector"])
                        mv = proj(pp, wqT, 512, half)
                        em.wait("vector", "tensor", mv)
                        em.step("vector", lambda vv, half=half, pp=pp: vv.tensor_copy(
                            out=kw[:, half * 512:half * 512 + 512], in_=pp[:, 0:512]))
                    rv_ = emit_rope(kw2, kw, 0, 0, ctile, stile)
                    vcst = em.step("vector", lambda vv: vv.tensor_copy(out=kbb[:], in_=kw2[:]))
                    em.wait("tensor", "vector", vcst)
                    for h in range(H):
                        pb = (h % 2) * 64
                        tv = em.step("tensor", lambda tt, h=h: tt.transpose(
                            tp[0:64, 0:128], kbb[:, h * HD:(h + 1) * HD], idb[:]))
                        em.wait("vector", "tensor", tv)
                        ev = em.step("vector", lambda vv, h=h, pb=pb, mo=mo: vv.tensor_copy(
                            out=qtb[pb:pb + 64, (h // 2) * T + mo * 128:(h // 2) * T + mo * 128 + 128],
                            in_=tp[0:64, 0:128]))
                        em.wait("tensor", "vector", ev)
                    em.wait("tensor", "vector", em.n["vector"])
                    mv = proj(pk, wqiT, HI * DI, 0)
                    em.wait("vector", "tensor", mv)
                    vq = em.step("vector", lambda vv: vv.tensor_copy(out=kbb[:, 0:HI * DI], in_=pk[:, 0:HI * DI]))
                    em.wait("tensor", "vector", vq)
                    for hi in range(HI):
                        pb = (hi % 2) * 64
                        tv = em.step("tensor", lambda tt, hi=hi: tt.transpose(
                            tp[0:64, 0:128], kbb[:, hi * DI:(hi + 1) * DI], idb[:]))
                        em.wait("vector", "tensor", tv)
                        ev = em.step("vector", lambda vv, hi=hi, pb=pb, mo=mo: vv.tensor_copy(
                            out=qitb[pb:pb + 64, (hi // 2) * T + mo * 128:(hi // 2) * T + mo * 128 + 128],
                            in_=tp[0:64, 0:128]))
                        em.wait("tensor", "vector", ev)
                    em.wait("tensor", "vector", em.n["vector"])
                    mv = proj(pk, wwiT, HI, 0)
                    em.wait("vector", "tensor", mv)
                    em.step("vector", lambda vv, mo=mo: vv.tensor_copy(
                        out=wis[:, mo * HI:(mo + 1) * HI], in_=pk[:, 0:HI]))

            em.barrier(("tensor", "vector", "scalar", "sync"))

        lo = mini[:, 0:1]; hi_ = mini[:, 1:2]; mid = mini[:, 2:3]
        negmid = mini[:, 3:4]; Fst = mini[:, 4:5]
        c_hi = mini[:, 7:8]; c_gt = mini[:, 8:9]; needf = mini[:, 9:10]
        cm = mini[:, 10:11]; mnew = mini[:, 11:12]; sco = mini[:, 12:13]
        bsum = mini[:, 13:14]; negmn = mini[:, 14:15]; rec = mini[:, 15:16]

        idx = sb("idx", [128, 4096], F32)
        giB = sb("giB", [128, 4096], I32)
        giC = sb("giC", [128, 4096], I32)
        giE = sb("giE", [128, 4096], I32)
        ktile = sb("ktile", [128, KC * 1024], BF16)
        vtile = sb("vtile", [128, 8 * D], BF16)
        for rc in range(M):
            abs_blk = BASE // 128 + rc
            special = abs_blk < 4
            if special:
                W = 512
            else:
                W = min(((abs_blk + 1) * 128 + 511) // 512 * 512, RK)
            NSC = W // 512
            if not special:
                for sc in range(NSC):
                    for hi in range(HI):
                        pb = (hi % 2) * 64
                        em.wait("tensor", "vector", em.n["vector"])
                        em.wait("tensor", "scalar", em.n["scalar"])
                        mv = em.step("tensor", lambda tt, hi=hi, pb=pb, rc=rc, sc=sc: tt.matmul(
                            psc[:, 0:512],
                            qitb[pb:pb + 64, (hi // 2) * T + rc * 128:(hi // 2) * T + rc * 128 + 128],
                            kitb[pb:pb + 64, sc * 512:(sc + 1) * 512],
                            start=True, stop=True))
                        em.wait("scalar", "tensor", mv)
                        sv = em.step("scalar", lambda sc_: sc_.activation(
                            kw[:, 0:512], psc[:, 0:512], AF.Relu, bias=0.0, scale=1.0))
                        em.wait("vector", "scalar", sv)
                        if hi == 0:
                            em.step("vector", lambda vv, rc=rc, sc=sc: vv.tensor_scalar(
                                idx[:, sc * 512:(sc + 1) * 512], kw[:, 0:512],
                                wis[:, rc * HI:rc * HI + 1], None, op0=ALU.mult))
                        else:
                            em.step("vector", lambda vv, rc=rc, sc=sc, hi=hi: vv.scalar_tensor_tensor(
                                idx[:, sc * 512:(sc + 1) * 512], kw[:, 0:512],
                                wis[:, rc * HI + hi:rc * HI + hi + 1],
                                idx[:, sc * 512:(sc + 1) * 512],
                                op0=ALU.mult, op1=ALU.add))
                em.step("vector", lambda vv, ab=abs_blk: vv.tensor_add(
                    idx[:, ab * 128:(ab + 1) * 128],
                    idx[:, ab * 128:(ab + 1) * 128], tris[:]))
                if (abs_blk + 1) * 128 < W:
                    em.step("vector", lambda vv, ab=abs_blk, W=W: vv.memset(
                        idx[:, (ab + 1) * 128:W], NEG))
                Wfloor = max((abs_blk * 128 + 1) // 512 * 512, 512)
                em.step("vector", lambda vv, Wfloor=Wfloor: vv.tensor_reduce(
                    lo, idx[:, 0:Wfloor], AX.X, ALU.min))
                em.step("vector", lambda vv: vv.tensor_scalar(lo, lo, 1.0, None, op0=ALU.subtract))
                em.step("vector", lambda vv, W=W: vv.tensor_reduce(
                    hi_, idx[:, 0:W], AX.X, ALU.max))
                em.step("vector", lambda vv: vv.tensor_scalar(hi_, hi_, 1.0, None, op0=ALU.add))
                thr = float(1024 - W)
                for it in range(NB):
                    vv1 = em.step("vector", lambda vv: vv.tensor_add(mid, lo, hi_))
                    vv1 = em.step("vector", lambda vv: vv.tensor_scalar(mid, mid, 0.5, None, op0=ALU.mult))
                    vv1 = em.step("vector", lambda vv: vv.tensor_scalar(negmid, mid, -1.0, None, op0=ALU.mult))
                    em.wait("scalar", "vector", vv1)
                    sv = em.step("scalar", lambda sc_, W=W: sc_.activation(
                        mask01[:, 0:W], idx[:, 0:W], AF.Sign, bias=negmid, scale=1.0,
                        accum_out=Fst))
                    em.wait("vector", "scalar", sv)
                    em.step("vector", lambda vv, thr=thr: vv.tensor_scalar(
                        fint[:, 0:1], Fst, thr, 1, op0=ALU.is_ge, op1=ALU.mult))
                    em.step("vector", lambda vv, thr=thr: vv.tensor_scalar(
                        fint[:, 1:2], Fst, thr, 1, op0=ALU.is_lt, op1=ALU.mult))
                    em.step("vector", lambda vv: vv.select(lo, fint[:, 0:1], mid, lo))
                    em.step("vector", lambda vv: vv.select(hi_, fint[:, 1:2], mid, hi_))
                em.step("vector", lambda vv, W=W: vv.tensor_scalar(
                    giB[:, 0:W], idx[:, 0:W], lo, 1, op0=ALU.is_ge, op1=ALU.mult))
                em.step("vector", lambda vv, W=W: vv.tensor_scalar(
                    giC[:, 0:W], idx[:, 0:W], hi_, 1, op0=ALU.is_ge, op1=ALU.mult))
                em.step("vector", lambda vv, W=W: vv.tensor_scalar(
                    giE[:, 0:W], idx[:, 0:W], hi_, 1, op0=ALU.is_gt, op1=ALU.mult))
                em.step("vector", lambda vv, W=W: vv.tensor_tensor_scan(
                    idx[:, 0:W], giC[:, 0:W], giC[:, 0:W], 0, op0=ALU.add, op1=ALU.bypass))
                em.step("vector", lambda vv, W=W: vv.tensor_copy(out=c_hi, in_=idx[:, W - 1:W]))
                em.step("vector", lambda vv, W=W: vv.tensor_tensor_scan(
                    idx[:, 0:W], giE[:, 0:W], giE[:, 0:W], 0, op0=ALU.add, op1=ALU.bypass))
                em.step("vector", lambda vv, W=W: vv.tensor_copy(out=c_gt, in_=idx[:, W - 1:W]))
                em.step("vector", lambda vv: vv.tensor_scalar(
                    fint[:, 0:1], c_hi, 512.0, 1, op0=ALU.is_ge, op1=ALU.mult))
                em.step("vector", lambda vv, W=W: vv.select(
                    giB[:, 0:W], fint[:, 0:1].to_broadcast([128, W]), giC[:, 0:W], giB[:, 0:W]))
                em.step("vector", lambda vv, W=W: vv.select(
                    giC[:, 0:W], fint[:, 0:1].to_broadcast([128, W]), giE[:, 0:W], giC[:, 0:W]))
                em.step("vector", lambda vv, W=W: vv.tensor_sub(giB[:, 0:W], giB[:, 0:W], giC[:, 0:W]))
                em.step("vector", lambda vv, W=W: vv.tensor_tensor_scan(
                    idx[:, 0:W], giB[:, 0:W], giB[:, 0:W], 0, op0=ALU.add, op1=ALU.bypass))
                em.step("vector", lambda vv: vv.tensor_scalar(
                    fint[:, 1:2], c_hi, 512.0, 1, op0=ALU.is_lt, op1=ALU.mult))
                em.step("vector", lambda vv: vv.select(c_gt, fint[:, 1:2], c_hi, c_gt))
                em.step("vector", lambda vv: vv.tensor_scalar(
                    needf, c_gt, 512.0, -1.0, op0=ALU.subtract, op1=ALU.mult))
                em.step("vector", lambda vv, W=W: vv.tensor_scalar(
                    giE[:, 0:W], idx[:, 0:W], needf, 1, op0=ALU.is_le, op1=ALU.mult))
                em.step("vector", lambda vv, W=W: vv.tensor_mul(giB[:, 0:W], giB[:, 0:W], giE[:, 0:W]))
                em.step("vector", lambda vv, W=W: vv.tensor_add(giC[:, 0:W], giC[:, 0:W], giB[:, 0:W]))
                mk_ = em.step("vector", lambda vv, W=W: vv.tensor_copy(out=mask01[:, 0:W], in_=giC[:, 0:W]))
            else:
                em.step("vector", lambda vv: vv.memset(mask01[:, 0:512], 1.0))

            em.step("vector", lambda vv: vv.memset(macc[:], NEG))
            em.step("vector", lambda vv: vv.memset(lacc[:], 0.0))
            em.step("vector", lambda vv: vv.memset(acc[:], 0.0))
            nsb = (W + 1023) // 1024
            for sb in range(nsb):
                blk = min(1024, W - sb * 1024)
                em.wait("sync", "tensor", em.n["tensor"])
                em.wait("sync", "vector", em.n["vector"])
                em.wait("sync", "scalar", em.n["scalar"])
                for kc in range(KC):
                    em.dma("sync", lambda s, kc=kc, sb=sb, blk=blk: s.dma_start(
                        out=ktile[:, kc * 1024:kc * 1024 + blk],
                        in_=kts[:, kc * RK + sb * 1024:kc * RK + sb * 1024 + blk]))
                for s2 in range(blk // 128):
                    em.dma("sync", lambda s, s2=s2, sb=sb: s.dma_start(
                        out=vtile[:, s2 * D:(s2 + 1) * D],
                        in_=vds[sb * 1024 + s2 * 128:sb * 1024 + (s2 + 1) * 128, :]))
                ldv = em.n["sync"]
                em.wait("tensor", "sync", ldv)
                for h in range(H):
                    pb = (h % 2) * 64
                    em.wait("tensor", "vector", em.n["vector"])
                    em.wait("tensor", "scalar", em.n["scalar"])
                    mv = None
                    for half in range((blk + 511) // 512):
                        nn = min(512, blk - half * 512)
                        mv = em.step("tensor", lambda tt, h=h, pb=pb, rc=rc, half=half, nn=nn: tt.matmul(
                            psc[:, half * 512:half * 512 + nn],
                            qtb[pb:pb + 64, (h // 2) * T + rc * 128:(h // 2) * T + rc * 128 + 128],
                            ktile[pb:pb + 64, (h // 2) * 1024 + half * 512:(h // 2) * 1024 + half * 512 + nn],
                            start=True, stop=True))
                    em.wait("vector", "tensor", mv)
                    em.step("vector", lambda vv, blk=blk: vv.tensor_reduce(
                        cm, psc[:, 0:blk], AX.X, ALU.max))
                    em.step("vector", lambda vv: vv.tensor_scalar(cm, cm, 0.125, None, op0=ALU.mult))
                    em.step("vector", lambda vv, h=h: vv.tensor_max(mnew, macc[:, h:h + 1], cm))
                    vv1 = em.step("vector", lambda vv, h=h: vv.tensor_sub(sco, macc[:, h:h + 1], mnew))
                    vv2 = em.step("vector", lambda vv: vv.tensor_scalar(negmn, mnew, -1.0, None, op0=ALU.mult))
                    em.wait("scalar", "vector", vv2)
                    s1 = em.step("scalar", lambda sc_: sc_.activation(sco, sco, AF.Exp, bias=0.0, scale=1.0))
                    s2_ = em.step("scalar", lambda sc_, blk=blk: sc_.activation(
                        probs[:, 0:blk], psc[:, 0:blk], AF.Exp, bias=negmn, scale=0.125))
                    em.wait("vector", "scalar", s2_)
                    em.step("vector", lambda vv, blk=blk, sb=sb: vv.scalar_tensor_tensor(
                        probs[:, 0:blk], probs[:, 0:blk], 1.0,
                        mask01[:, sb * 1024:sb * 1024 + blk],
                        op0=ALU.mult, op1=ALU.mult, accum_out=bsum))
                    em.step("vector", lambda vv, h=h: vv.tensor_scalar(
                        lacc[:, h:h + 1], lacc[:, h:h + 1], sco, None, op0=ALU.mult))
                    vv3 = em.step("vector", lambda vv, h=h: vv.tensor_add(
                        lacc[:, h:h + 1], lacc[:, h:h + 1], bsum))
                    em.step("vector", lambda vv, h=h: vv.tensor_copy(out=macc[:, h:h + 1], in_=mnew))
                    ngr = (blk + 511) // 512
                    for g in range(ngr):
                        nsub = min(4, (blk - g * 512) // 128)
                        em.wait("tensor", "vector", em.n["vector"])
                        tv = None
                        for j in range(nsub):
                            s2 = g * 4 + j
                            tv = em.step("tensor", lambda tt, s2=s2, j=j: tt.transpose(
                                tp[:, j * 128:(j + 1) * 128],
                                probs[:, s2 * 128:(s2 + 1) * 128], idb[:]))
                        em.wait("vector", "tensor", tv)
                        ev = em.step("vector", lambda vv, nsub=nsub: vv.tensor_copy(
                            out=ptile[:, 0:nsub * 128], in_=tp[:, 0:nsub * 128]))
                        em.wait("tensor", "vector", ev)
                        mv = None
                        for j in range(nsub):
                            s2 = g * 4 + j
                            stt_ = (g == 0 and j == 0)
                            spp_ = (g == ngr - 1 and j == nsub - 1)
                            mv = em.step("tensor", lambda tt, s2=s2, j=j, h=h, stt_=stt_, spp_=spp_: tt.matmul(
                                pav[:, 0:64],
                                ptile[:, j * 128:(j + 1) * 128],
                                vtile[:, s2 * D + h * HD:s2 * D + (h + 1) * HD],
                                start=stt_, stop=spp_))
                    em.wait("vector", "tensor", mv)
                    em.step("vector", lambda vv, h=h: vv.scalar_tensor_tensor(
                        acc[:, h * HD:(h + 1) * HD], acc[:, h * HD:(h + 1) * HD],
                        sco, pav[:, 0:64], op0=ALU.mult, op1=ALU.add))
            for h in range(H):
                em.step("vector", lambda vv, h=h: vv.reciprocal(rec, lacc[:, h:h + 1]))
                em.step("vector", lambda vv, h=h: vv.tensor_scalar(
                    aob[:, h * HD:(h + 1) * HD], acc[:, h * HD:(h + 1) * HD],
                    rec, None, op0=ALU.mult))
            av = em.n["vector"]
            em.wait("tensor", "vector", av)
            for g in range(2):
                tv = None
                for j in range(4):
                    jj = g * 4 + j
                    tv = em.step("tensor", lambda tt, jj=jj, j=j: tt.transpose(
                        tp[:, j * 128:(j + 1) * 128],
                        aob[:, jj * 128:(jj + 1) * 128], idb[:]))
                em.wait("vector", "tensor", tv)
                ev = None
                for j in range(4):
                    jj = g * 4 + j
                    ev = em.step("vector", lambda vv, jj=jj, j=j: vv.tensor_copy(
                        out=aotT[:, jj * 128:(jj + 1) * 128],
                        in_=tp[:, j * 128:(j + 1) * 128]))
                em.wait("tensor", "vector", ev)
            for half in range(2):
                pp = pk if half == 0 else pk2
                em.wait("tensor", "vector", em.n["vector"])
                mv = None
                for kc in range(KC):
                    mv = em.step("tensor", lambda tt, kc=kc, half=half, pp=pp: tt.matmul(
                        pp[:, 0:512],
                        aotT[:, kc * 128:(kc + 1) * 128],
                        woTb[:, kc * D + half * 512:kc * D + half * 512 + 512],
                        start=(kc == 0), stop=(kc == KC - 1)))
                em.wait("vector", "tensor", mv)
                em.step("vector", lambda vv, half=half, pp=pp: vv.tensor_copy(
                    out=yo[:, half * 512:half * 512 + 512], in_=pp[:, 0:512]))
            yv = em.n["vector"]
            em.wait("sync", "vector", yv)
            em.dma("sync", lambda s, rc=rc: s.dma_start(
                out=y_out[rc * 128:(rc + 1) * 128, :], in_=yo[:]))
            em.wait("vector", "sync", em.n["sync"])

        em.wait("sync", "sync", em.n["sync"])
        em.finalize(nc)
    return nc



def emit_phaseE(nc, em, st, sb, ps, x, ya, wg, ws1, ws3, ws2, we1, we3, we2, iden_in, out_f):
    E1 = 8
    FF = 2048
    FQ = 512
    NQ = FF // FQ
    xt = sb("xt", [128, KC * T], BF16)
    acc = sb("acc", [128, M * D], F32)
    w1b = [sb(f"w1b{i}", [128, KC * FQ], BF16) for i in range(2)]
    w3b = [sb(f"w3b{i}", [128, KC * FQ], BF16) for i in range(2)]
    w2b = [sb(f"w2b{i}", [128, (FQ // 128) * D], BF16) for i in range(2)]
    wgb = sb("wgb", [128, KC * E1], BF16)
    idb = sb("idb", [128, 128], BF16)
    sq = sb("sq", [128, D], F32)
    xnb = sb("xnb", [128, D], BF16)
    xchunk = sb("xchunk", [128, D], F32)
    mini = sb("mini", [128, 8], F32)
    lg = sb("lg", [128, 8], F32)
    g8 = sb("g8", [128, 8], F32)
    m8s = sb("m8s", [128, 8], F32)
    ge8 = sb("ge8", [128, 8], I32)
    ts8 = sb("ts8", [128, 8], F32)
    zs8 = sb("zs8", [128, 8], F32)
    g_all = sb("g_all", [128, M * 9], F32)
    hs1 = sb("hs1", [128, 2 * 512], F32)
    hb = sb("hb", [128, 4 * 512], BF16)

    tp = ps("tp", [128, 512], BF16)
    py0 = ps("py0", [128, 1024], F32)
    py1 = ps("py1", [128, 1024], F32)
    ph1a = ps("ph1a", [128, 512], F32)
    ph1b = ps("ph1b", [128, 512], F32)
    ph3a = ps("ph3a", [128, 512], F32)
    ph3b = ph3a
    pr = ph1a

    bufs = dict(xt=xt, sq=sq, mini=mini, xnb=xnb, tp=tp, idb=idb,
                xchunk=xchunk, eps=mini[:, 6:7])
    em.step("vector", lambda vv: vv.memset(mini[:, 6:7], 1e-5))
    em.step("vector", lambda vv: vv.memset(zs8[:], 0.0))

    gid = em.dma("gpsimd", lambda g: g.dma_start(out=idb[:], in_=iden_in[:, :]))
    for kc in range(KC):
        em.dma("gpsimd", lambda g, kc=kc: g.dma_start(
            out=wgb[:, kc * E1:(kc + 1) * E1], in_=wg[kc * 128:(kc + 1) * 128, :]))
    vya = em.dma("sync", lambda s: s.dma_start(
        out=acc.ap().rearrange("p (m n) -> p m n", m=M),
        in_=ya.rearrange("(m p) n -> p m n", p=128)))

    for m in range(M):
        emit_ln_xnT(em, bufs, m, x_src_ap=x)
        em.step("vector", lambda vv, m=m: vv.tensor_add(
            acc[:, m * D:(m + 1) * D], xchunk[:], acc[:, m * D:(m + 1) * D]))
    em.barrier(("tensor", "vector", "scalar", "gpsimd"))

    for m in range(M):
        mv = None
        em.wait("tensor", "vector", em.n["vector"])
        for kc in range(KC):
            mv = em.step("tensor", lambda tt, kc=kc, m=m: tt.matmul(
                pr[:, 0:E1],
                xt[:, kc * T + m * 128:kc * T + (m + 1) * 128],
                wgb[:, kc * E1:(kc + 1) * E1],
                start=(kc == 0), stop=(kc == KC - 1)))
        em.wait("vector", "tensor", mv)
        em.step("vector", lambda vv: vv.tensor_copy(out=lg[:], in_=pr[:, 0:E1]))
        sv = em.n["vector"]
        em.wait("scalar", "vector", sv)
        s1 = em.step("scalar", lambda sc: sc.activation(
            g8[:], lg[:], AF.Sigmoid, bias=0.0, scale=1.0))
        em.wait("vector", "scalar", s1)
        em.step("vector", lambda vv: vv.max(out=m8s[:], in_=g8[:]))
        em.step("vector", lambda vv: vv.tensor_scalar(
            ge8[:], g8[:], m8s[:, 1:2], 1, op0=ALU.is_ge, op1=ALU.mult))
        em.step("vector", lambda vv: vv.tensor_add(mini[:, 0:1], m8s[:, 0:1], m8s[:, 1:2]))
        em.step("vector", lambda vv: vv.reciprocal(mini[:, 1:2], mini[:, 0:1]))
        em.step("vector", lambda vv: vv.tensor_scalar(
            mini[:, 2:3], mini[:, 1:2], 1.0 / 3.0, None, op0=ALU.mult))
        em.step("vector", lambda vv: vv.select(ts8[:], ge8[:], g8[:], zs8[:]))
        em.step("vector", lambda vv, m=m: vv.tensor_scalar(
            g_all[:, m * 9:m * 9 + 8], ts8[:], mini[:, 2:3], None, op0=ALU.mult))
        em.step("vector", lambda vv, m=m: vv.memset(
            g_all[:, m * 9 + 8:m * 9 + 9], 1.0 / 3.0))

    em.barrier(("tensor", "vector", "scalar"))

    NU = 9 * NQ

    def srcs(u):
        e, q = u // NQ, u % NQ
        if e < E1:
            return (we1[e, :, q * FQ:(q + 1) * FQ],
                    we3[e, :, q * FQ:(q + 1) * FQ],
                    we2[e, q * FQ:(q + 1) * FQ, :], e)
        return (ws1[:, q * FQ:(q + 1) * FQ],
                ws3[:, q * FQ:(q + 1) * FQ],
                ws2[q * FQ:(q + 1) * FQ, :], 8)

    def emit_loads(u):
        s1a, s3a, s2a, _ = srcs(u)
        bi = u % 2
        for kc in range(KC):
            em.dma("gpsimd", lambda g, kc=kc, bi=bi, s1a=s1a: g.dma_start(
                out=w1b[bi][:, kc * FQ:(kc + 1) * FQ],
                in_=s1a[kc * 128:(kc + 1) * 128, :]))
            em.dma("gpsimd", lambda g, kc=kc, bi=bi, s3a=s3a: g.dma_start(
                out=w3b[bi][:, kc * FQ:(kc + 1) * FQ],
                in_=s3a[kc * 128:(kc + 1) * 128, :]))
        for kc2 in range(FQ // 128):
            em.dma("gpsimd", lambda g, kc2=kc2, bi=bi, s2a=s2a: g.dma_start(
                out=w2b[bi][:, kc2 * D:(kc2 + 1) * D],
                in_=s2a[kc2 * 128:(kc2 + 1) * 128, :]))
        return em.n["gdma"]

    load_end = {}
    compute_end = {}
    em.wait("gpsimd", "vector", em.n["vector"])
    load_end[0] = emit_loads(0)
    silu_cnt = [0, 0]
    mul_tile = [0] * 4
    py_last = [0, 0]
    pidx = 0
    for u in range(NU):
        e, q = u // NQ, u % NQ
        _, _, _, gcol = srcs(u)
        bi = u % 2
        if u + 1 < NU:
            if u >= 1:
                em.wait("gpsimd", "tensor", compute_end[u - 1])
            load_end[u + 1] = emit_loads(u + 1)
        em.wait("tensor", "gdma", load_end[u])
        for g4 in range(4):
            for s in range(4):
                p1 = ph1a if s % 2 == 0 else ph1b
                p3 = ph3a if s % 2 == 0 else ph3b
                if silu_cnt[s % 2]:
                    em.wait("tensor", "scalar", silu_cnt[s % 2])
                for kc in range(KC):
                    em.step("tensor", lambda tt, kc=kc, s=s, g4=g4, bi=bi, p1=p1: tt.matmul(
                        p1[:, 0:512],
                        w1b[bi][:, kc * FQ + s * 128:kc * FQ + s * 128 + 128],
                        xt[:, kc * T + g4 * 512:kc * T + g4 * 512 + 512],
                        start=(kc == 0), stop=(kc == KC - 1)))
                prev3 = mul_tile[s - 1] if s >= 1 else mul_tile[3]
                if prev3:
                    em.wait("tensor", "vector", prev3)
                mvh = None
                for kc in range(KC):
                    mvh = em.step("tensor", lambda tt, kc=kc, s=s, g4=g4, bi=bi, p3=p3: tt.matmul(
                        p3[:, 0:512],
                        w3b[bi][:, kc * FQ + s * 128:kc * FQ + s * 128 + 128],
                        xt[:, kc * T + g4 * 512:kc * T + g4 * 512 + 512],
                        start=(kc == 0), stop=(kc == KC - 1)))
                prevm = mul_tile[s - 2] if s >= 2 else mul_tile[2 + s]
                if prevm:
                    em.wait("scalar", "vector", prevm)
                em.wait("scalar", "tensor", mvh)
                sv = em.step("scalar", lambda sc, s=s, p1=p1: sc.activation(
                    hs1[:, (s % 2) * 512:(s % 2) * 512 + 512], p1[:, 0:512],
                    AF.Silu, bias=0.0, scale=1.0))
                silu_cnt[s % 2] = sv
                em.wait("vector", "scalar", sv)
                mul_tile[s] = em.step("vector", lambda vv, s=s, p3=p3: vv.tensor_mul(
                    hb[:, s * 512:(s + 1) * 512],
                    hs1[:, (s % 2) * 512:(s % 2) * 512 + 512], p3[:, 0:512]))
            for ts_ in range(4):
                pp = pidx % 2
                p_y = py0 if pp == 0 else py1
                pidx += 1
                if py_last[pp]:
                    em.wait("tensor", "vector", py_last[pp])
                mv = None
                for s in range(4):
                    if ts_ == 0:
                        em.wait("tensor", "vector", mul_tile[s])
                    for nh in range(2):
                        stf = (s == 0)
                        spf = (s == 3)
                        mv = em.step("tensor", lambda tt, s=s, ts_=ts_, nh=nh, p=p_y, bi=bi, stf=stf, spf=spf: tt.matmul(
                            p[:, nh * 512:nh * 512 + 512],
                            hb[:, s * 512 + ts_ * 128:s * 512 + ts_ * 128 + 128],
                            w2b[bi][:, s * D + nh * 512:s * D + nh * 512 + 512],
                            start=stf, stop=spf))
                em.wait("vector", "tensor", mv)
                mcol = g4 * 4 + ts_
                py_last[pp] = em.step("vector", lambda vv, mcol=mcol, gcol=gcol, p=p_y: vv.scalar_tensor_tensor(
                    acc[:, mcol * D:(mcol + 1) * D],
                    p[:, 0:1024],
                    g_all[:, mcol * 9 + gcol:mcol * 9 + gcol + 1],
                    acc[:, mcol * D:(mcol + 1) * D],
                    op0=ALU.mult, op1=ALU.add))
        compute_end[u] = em.n["tensor"]

    em.barrier(("tensor", "vector", "scalar", "sync", "gpsimd"))
    for m in range(M):
        eng = "sync" if m % 2 == 0 else "gpsimd"
        em.dma(eng, lambda en, m=m: en.dma_start(
            out=out_f[m * 128:(m + 1) * 128, :], in_=acc[:, m * D:(m + 1) * D]))
    em.wait("sync", "sync", em.n["sync"])
    em.wait("sync", "gdma", em.n["gdma"])



def build_launchE():
    E1 = 8
    FF = 2048
    FQ = 512
    NQ = FF // FQ
    nc = bass.Bass("TRN2", target_bir_lowering=False, detect_race_conditions=False)
    x = nc.dram_tensor("x", [T, D], F32, kind="ExternalInput").ap()
    ya = nc.dram_tensor("ya", [T, D], F32, kind="ExternalInput").ap()
    wg = nc.dram_tensor("wg", [D, E1], F32, kind="ExternalInput").ap()
    ws1 = nc.dram_tensor("ws1", [D, FF], F32, kind="ExternalInput").ap()
    ws3 = nc.dram_tensor("ws3", [D, FF], F32, kind="ExternalInput").ap()
    ws2 = nc.dram_tensor("ws2", [FF, D], F32, kind="ExternalInput").ap()
    we1 = nc.dram_tensor("we1", [E1, D, FF], F32, kind="ExternalInput").ap()
    we3 = nc.dram_tensor("we3", [E1, D, FF], F32, kind="ExternalInput").ap()
    we2 = nc.dram_tensor("we2", [E1, FF, D], F32, kind="ExternalInput").ap()
    iden_in = nc.dram_tensor("iden", [128, 128], F32, kind="ExternalInput").ap()
    out_f = nc.dram_tensor("out_f", [T, D], F32, kind="ExternalOutput").ap()

    em = Em()
    with contextlib.ExitStack() as st:
        def sb(name, shape, dtype):
            return st.enter_context(nc.sbuf_tensor(name, shape, dtype))

        def ps(name, shape, dtype):
            return _ps_pad(nc, st, name, shape, dtype)

        emit_phaseE(nc, em, st, sb, ps, x, ya, wg, ws1, ws3, ws2, we1, we3, we2, iden_in, out_f)
        em.finalize(nc)
    return nc



def build_fused():
    HI = 4
    DI = 64
    nc = bass.Bass("TRN2", target_bir_lowering=False, detect_race_conditions=False)
    x = nc.dram_tensor("x", [T, D], F32, kind="ExternalInput").ap()
    wk = nc.dram_tensor("wk", [D, D], F32, kind="ExternalInput").ap()
    wv = nc.dram_tensor("wv", [D, D], F32, kind="ExternalInput").ap()
    wq0 = nc.dram_tensor("wq0", [D, D], F32, kind="ExternalInput").ap()
    wout0 = nc.dram_tensor("wout0", [D, D], F32, kind="ExternalInput").ap()
    proj = nc.dram_tensor("proj", [H, F2, HD], F32, kind="ExternalInput").ap()
    iden = nc.dram_tensor("iden", [128, 128], F32, kind="ExternalInput").ap()
    rw = nc.dram_tensor("rw", [D, 8], F32, kind="ExternalInput").ap()
    rb = nc.dram_tensor("rb", [1, 8], F32, kind="ExternalInput").ap()
    w1 = nc.dram_tensor("w1", [8, D, 4096], F32, kind="ExternalInput").ap()
    b1t = nc.dram_tensor("b1t", [8, 128, 32], F32, kind="ExternalInput").ap()
    w2 = nc.dram_tensor("w2", [8, 4096, D], F32, kind="ExternalInput").ap()
    b2 = nc.dram_tensor("b2", [8, D], F32, kind="ExternalInput").ap()
    wqkv1 = nc.dram_tensor("wqkv1", [3 * D, D], F32, kind="ExternalInput").ap()
    wo1 = nc.dram_tensor("wo1", [D, D], F32, kind="ExternalInput").ap()
    wqi = nc.dram_tensor("wqi", [HI * DI, D], F32, kind="ExternalInput").ap()
    wki = nc.dram_tensor("wki", [DI, D], F32, kind="ExternalInput").ap()
    wwi = nc.dram_tensor("wwi", [HI, D], F32, kind="ExternalInput").ap()
    cosf = nc.dram_tensor("cosf", [4096, D], F32, kind="ExternalInput").ap()
    sinf = nc.dram_tensor("sinf", [4096, D], F32, kind="ExternalInput").ap()
    camask = nc.dram_tensor("camask", [T, 4096], BF16, kind="ExternalInput").ap()
    wg = nc.dram_tensor("wg", [D, 8], F32, kind="ExternalInput").ap()
    ws1 = nc.dram_tensor("ws1", [D, 2048], F32, kind="ExternalInput").ap()
    ws3 = nc.dram_tensor("ws3", [D, 2048], F32, kind="ExternalInput").ap()
    ws2 = nc.dram_tensor("ws2", [2048, D], F32, kind="ExternalInput").ap()
    we1 = nc.dram_tensor("we1", [8, D, 2048], F32, kind="ExternalInput").ap()
    we3 = nc.dram_tensor("we3", [8, D, 2048], F32, kind="ExternalInput").ap()
    we2 = nc.dram_tensor("we2", [8, 2048, D], F32, kind="ExternalInput").ap()
    out_f = nc.dram_tensor("out_f", [T, D], F32, kind="ExternalOutput").ap()
    kvsrc = nc.dram_tensor("kvsrc", [128, 32 * 65 + 1], F32, kind="Internal").ap()
    kvg = nc.dram_tensor("kvg", [256, 32 * 65 + 1], F32, kind="Internal").ap()
    ya_d = nc.dram_tensor("ya_d", [T, D], F32, kind="Internal").ap()
    x1_d = nc.dram_tensor("x1_d", [T, D], F32, kind="Internal").ap()
    x1g = nc.dram_tensor("x1g", [4096, D], F32, kind="Internal").ap()
    y1_d = nc.dram_tensor("y1_d", [T, D], F32, kind="Internal").ap()
    kts = nc.dram_tensor("kts", [128, KC * 4096], BF16, kind="Internal").ap()
    vds = nc.dram_tensor("vds", [4096, D], BF16, kind="Internal").ap()
    qts_d = nc.dram_tensor("qts_d", [128, KC * 4096], BF16, kind="Internal").ap()
    qits_d = nc.dram_tensor("qits_d", [128, (HI // 2) * 4096], BF16, kind="Internal").ap()
    wis_d = nc.dram_tensor("wis_d", [128, 32 * HI], F32, kind="Internal").ap()

    em = Em()
    with contextlib.ExitStack() as st0:
        def mk_sb(pfx, stx):
            def sb(name, shape, dtype):
                return stx.enter_context(nc.sbuf_tensor(pfx + name, shape, dtype))
            def ps(name, shape, dtype):
                return _ps_pad(nc, stx, pfx + name, shape, dtype)
            return sb, ps

        zpad = st0.enter_context(nc.sbuf_tensor("zpad", [128, 1], F32))
        em.step("vector", lambda vv: vv.memset(zpad[:], 0.0))

        with contextlib.ExitStack() as stA:
            sbA, psA = mk_sb("A_", stA)
            emit_phaseA(nc, em, stA, sbA, psA, x, wk, wv, proj, iden,
                        kvsrc[:, 0:2080], kvsrc[0:16, 2080:2081])
            zv = em.n["vector"]
            em.wait("sync", "vector", zv)
            em.dma("sync", lambda s: s.dma_start(
                out=kvsrc[16:128, 2080:2081], in_=zpad[16:128, 0:1],
                allow_slow_non_contiguous=True))
            em.wait("sync", "sync", em.n["sync"])
            em.barrier()
        def _coll1(eng, sems):
            eng.collective_compute(
                "AllGather", ALU.bypass,
                replica_groups=[[0, 1], [2, 3], [4, 5], [6, 7]],
                ins=[kvsrc[:, :]], outs=[kvg[:, :]],
            ).then_inc(sems["coll"], 1)
        em.raw("gpsimd", _coll1, sem="coll", inc=1)
        for e in ("sync", "tensor", "vector", "scalar"):
            em.wait(e, "coll", 1)
        em.wait("gpsimd", "coll", 1)

        with contextlib.ExitStack() as stB:
            sbB, psB = mk_sb("B_", stB)
            emit_phaseB(nc, em, stB, sbB, psB, x, wq0, wout0, proj, iden,
                        kvg[0:128, 0:2080], kvg[128:256, 0:2080],
                        kvg[0:16, 2080:2081], kvg[128:144, 2080:2081], ya_d)
            em.barrier()

        with contextlib.ExitStack() as stC:
            sbC, psC = mk_sb("C_", stC)
            emit_phaseC(nc, em, stC, sbC, psC, x, ya_d, rw, rb, w1, b1t, w2, b2,
                        iden, x1_d)
            em.barrier()

        def _coll2(eng, sems):
            eng.collective_compute(
                "AllGather", ALU.bypass,
                replica_groups=[[0, 1], [2, 3], [4, 5], [6, 7]],
                ins=[x1_d[:, :]], outs=[x1g[:, :]],
            ).then_inc(sems["coll"], 1)
        em.raw("gpsimd", _coll2, sem="coll", inc=1)
        for e in ("sync", "tensor", "vector", "scalar"):
            em.wait(e, "coll", 2)
        em.wait("gpsimd", "coll", 2)

        with contextlib.ExitStack() as stD:
            sbD, psD = mk_sb("D_", stD)
            emit_phaseD_uniform(nc, em, stD, sbD, psD, x1g, wqkv1, wo1, wqi, wki,
                                wwi, iden, cosf, sinf, camask, y1_d, kts, vds,
                                qts_d, qits_d, wis_d)
            em.barrier()

        with contextlib.ExitStack() as stE:
            sbE, psE = mk_sb("E_", stE)
            emit_phaseE(nc, em, stE, sbE, psE, x1_d, y1_d, wg, ws1, ws3, ws2,
                        we1, we3, we2, iden, out_f)

        em.wait("sync", "sync", em.n["sync"])
        em.finalize(nc)
    return nc


def _parity_rows(p):
    idx = []
    for j in range(16):
        blk = 2 * j + p
        idx.extend(range(blk * 128, (blk + 1) * 128))
    return np.array(idx)


def _build_camask(p):
    import ml_dtypes as mld
    cm = np.zeros((T, 4096), np.float32)
    for i in range(16):
        ab = 2 * i + p
        if ab < 4:
            continue
        W = min(((2 * i + 2) * 128 + 511) // 512 * 512, 4096)
        for r in range(128):
            t_abs = ab * 128 + r
            cm[i * 128 + r, t_abs + 1:W] = NEG
    return cm.astype(mld.bfloat16)


def run_fused(inputs):
    x_all = inputs["L0_attn_x"]
    wqkv = inputs["L0_attn_W_qkv"]
    comC = _prepC_common(inputs)
    cosf, sinf = _rope_tables()
    com = dict(
        wk=np.ascontiguousarray(wqkv[:, D:2 * D]),
        wv=np.ascontiguousarray(wqkv[:, 2 * D:3 * D]),
        wq0=np.ascontiguousarray(wqkv[:, 0:D]),
        wout0=np.ascontiguousarray(inputs["L0_attn_W_out"]),
        proj=np.ascontiguousarray(inputs["L0_attn_proj"]),
        iden=np.eye(128, dtype=np.float32),
        rw=comC["rw"], rb=comC["rb"], w1=comC["w1"], b1t=comC["b1t"],
        w2=comC["w2"], b2=comC["b2"],
        wqkv1=np.ascontiguousarray(inputs["L1_attn_w_qkv"]),
        wo1=np.ascontiguousarray(inputs["L1_attn_w_o"]),
        wqi=np.ascontiguousarray(inputs["L1_attn_w_qi"]),
        wki=np.ascontiguousarray(inputs["L1_attn_w_ki"]),
        wwi=np.ascontiguousarray(inputs["L1_attn_w_wi"]),
        cosf=cosf, sinf=sinf,
        wg=np.ascontiguousarray(inputs["L1_ffn_Wg"]),
        ws1=np.ascontiguousarray(inputs["L1_ffn_Ws1"]),
        ws3=np.ascontiguousarray(inputs["L1_ffn_Ws3"]),
        ws2=np.ascontiguousarray(inputs["L1_ffn_Ws2"]),
        we1=inputs["L1_ffn_We1"], we3=inputs["L1_ffn_We3"],
        we2=inputs["L1_ffn_We2"])
    cams = [_build_camask(0), _build_camask(1)]
    rows = [_parity_rows(0), _parity_rows(1)]
    nc = build_fused()
    maps = []
    for c in range(NCORES):
        b, p = c // 2, c % 2
        xsh = np.ascontiguousarray(x_all[b][rows[p]])
        maps.append(dict(x=xsh, camask=cams[p], **com))
    res = run_bass_kernel_spmd(nc, maps, core_ids=list(range(NCORES)))
    out = np.empty((4, 4096, D), np.float32)
    for c in range(NCORES):
        b, p = c // 2, c % 2
        out[b][rows[p]] = res.results[c]["out_f"]
    return out



def _shards(x_all):
    out = []
    for c in range(NCORES):
        b, p = c // 2, c % 2
        out.append(np.ascontiguousarray(x_all[b, p * T:(p + 1) * T, :]))
    return out


def run_A(inputs):
    x_all = inputs["L0_attn_x"]
    wqkv = inputs["L0_attn_W_qkv"]
    wk = np.ascontiguousarray(wqkv[:, D:2 * D])
    wv = np.ascontiguousarray(wqkv[:, 2 * D:3 * D])
    proj = np.ascontiguousarray(inputs["L0_attn_proj"])
    iden = np.eye(128, dtype=np.float32)
    nc = build_launchA()
    maps = []
    for c, xsh in enumerate(_shards(x_all)):
        maps.append(dict(x=xsh, wk=wk, wv=wv, proj=proj, iden=iden))
    res = run_bass_kernel_spmd(nc, maps, core_ids=list(range(NCORES)))
    return [dict(kv=r["kv_out"], stab=r["stab_out"]) for r in res.results]


def run_B(inputs, resA):
    x_all = inputs["L0_attn_x"]
    wqkv = inputs["L0_attn_W_qkv"]
    wq = np.ascontiguousarray(wqkv[:, 0:D])
    wout = np.ascontiguousarray(inputs["L0_attn_W_out"])
    proj = np.ascontiguousarray(inputs["L0_attn_proj"])
    iden = np.eye(128, dtype=np.float32)
    nc = build_launchB()
    maps = []
    for c, xsh in enumerate(_shards(x_all)):
        sib = c ^ 1
        lo, hi = (c, sib) if c < sib else (sib, c)
        maps.append(dict(x=xsh, wq=wq, wout=wout, proj=proj, iden=iden,
                         kv0=resA[lo]["kv"], kv1=resA[hi]["kv"],
                         st0=resA[lo]["stab"], st1=resA[hi]["stab"]))
    res = run_bass_kernel_spmd(nc, maps, core_ids=list(range(NCORES)))
    return [r["y_out"] for r in res.results]


def _prepC_common(inputs):
    b1 = inputs["L0_ffn_b1"]
    b1t = np.ascontiguousarray(b1.reshape(8, 32, 128).transpose(0, 2, 1))
    return dict(
        rw=np.ascontiguousarray(inputs["L0_ffn_router_w"]),
        rb=np.ascontiguousarray(inputs["L0_ffn_router_b"][None, :]),
        w1=inputs["L0_ffn_w1"], b1t=b1t,
        w2=inputs["L0_ffn_w2"],
        b2=np.ascontiguousarray(inputs["L0_ffn_b2"]),
        iden=np.eye(128, dtype=np.float32))


def run_C(inputs, resB):
    x_all = inputs["L0_attn_x"]
    com = _prepC_common(inputs)
    nc = build_launchC()
    maps = []
    for c, xsh in enumerate(_shards(x_all)):
        maps.append(dict(x=xsh, ya=resB[c], **com))
    res = run_bass_kernel_spmd(nc, maps, core_ids=list(range(NCORES)))
    return [r["x1_out"] for r in res.results]


def _rope_tables():
    S = 4096
    inv = 1.0 / 10000.0 ** (np.arange(0, HD, 2, dtype=np.float32) / HD)
    ang = np.arange(S, dtype=np.float32)[:, None] * inv
    cos_h = np.zeros((S, HD), np.float32)
    sin_h = np.zeros((S, HD), np.float32)
    cos_h[:, 0::2] = np.cos(ang)
    cos_h[:, 1::2] = np.cos(ang)
    sin_h[:, 0::2] = -np.sin(ang)
    sin_h[:, 1::2] = np.sin(ang)
    cosf = np.tile(cos_h, (1, H))
    sinf = np.tile(sin_h, (1, H))
    return np.ascontiguousarray(cosf), np.ascontiguousarray(sinf)


def _tri_tile():
    t = np.zeros((128, 128), np.float32)
    for i in range(128):
        t[i, i + 1:] = NEG
    return t


def _prepD_common(inputs):
    cosf, sinf = _rope_tables()
    return dict(
        wqkv=np.ascontiguousarray(inputs["L1_attn_w_qkv"]),
        wo=np.ascontiguousarray(inputs["L1_attn_w_o"]),
        wqi=np.ascontiguousarray(inputs["L1_attn_w_qi"]),
        wki=np.ascontiguousarray(inputs["L1_attn_w_ki"]),
        wwi=np.ascontiguousarray(inputs["L1_attn_w_wi"]),
        iden=np.eye(128, dtype=np.float32),
        cosf=cosf, sinf=sinf, tri=_tri_tile())


def run_D(inputs, resC):
    com = _prepD_common(inputs)
    outs = [None] * NCORES
    for p in (0, 1):
        nc = build_launchD(p)
        cores = [c for c in range(NCORES) if c % 2 == p]
        maps = []
        for c in cores:
            maps.append(dict(x1o=resC[c], x1s=resC[c ^ 1], **com))
        res = run_bass_kernel_spmd(nc, maps, core_ids=cores)
        for i, c in enumerate(cores):
            outs[c] = res.results[i]["y_out"]
    return outs


def run_E(inputs, resC, resD):
    com = dict(
        wg=np.ascontiguousarray(inputs["L1_ffn_Wg"]),
        ws1=np.ascontiguousarray(inputs["L1_ffn_Ws1"]),
        ws3=np.ascontiguousarray(inputs["L1_ffn_Ws3"]),
        ws2=np.ascontiguousarray(inputs["L1_ffn_Ws2"]),
        we1=inputs["L1_ffn_We1"], we3=inputs["L1_ffn_We3"], we2=inputs["L1_ffn_We2"],
        iden=np.eye(128, dtype=np.float32))
    nc = build_launchE()
    maps = []
    for c in range(NCORES):
        maps.append(dict(x=resC[c], ya=resD[c], **com))
    res = run_bass_kernel_spmd(nc, maps, core_ids=list(range(NCORES)))
    return [r["out_f"] for r in res.results]


def kernel(**inputs):
    inputs = {k: np.asarray(v) for k, v in inputs.items()}
    return run_fused(inputs)


# version 09 (final)
# speedup vs baseline: 1.1294x; 1.1294x over previous
import contextlib
import numpy as np
import ml_dtypes
import concourse.bass as bass
import concourse.mybir as mybir
from concourse.bass_utils import run_bass_kernel_spmd

F32 = mybir.dt.float32
BF16 = mybir.dt.bfloat16
I32 = mybir.dt.int32
AF = mybir.ActivationFunctionType
ALU = mybir.AluOpType
AX = mybir.AxisListType

NCORES = 8
D = 1024
T = 2048
M = T // 128
KC = D // 128
H = 16
HD = 64
F2 = 128
NEG = -1e30

ENGS = ("sync", "gpsimd", "tensor", "vector", "scalar")
SEMS = ("sync", "gpsimd", "tensor", "vector", "scalar", "gdma", "coll")


class Em:

    def __init__(self):
        self.q = {e: [] for e in ENGS}
        self.n = {s: 0 for s in SEMS}

    def do(self, e, fn, sem, inc):
        self.q[e].append(("op", fn, sem, inc))
        if inc:
            self.n[sem] += inc
        return self.n[sem] if sem else 0

    def dma(self, e, fn):
        sem = "gdma" if e == "gpsimd" else "sync"
        return self.do(e, fn, sem, 16)

    def step(self, e, fn, fence=None):
        assert e != "sync"
        if fence is None:
            fence = e in ("vector", "scalar", "gpsimd")
        if fence and self.n[e] > 0:
            self.q[e].append(("wait", e, self.n[e]))
        return self.do(e, fn, e, 1)

    def wait(self, e, prod, val):
        assert val <= self.n[prod], (e, prod, val, self.n[prod])
        self.q[e].append(("wait", prod, val))

    def barrier(self, engines=ENGS):
        vals = {e: self.n[e] for e in engines}
        gd = self.n["gdma"]
        for e in engines:
            for o in engines:
                if o != e and vals[o] > 0:
                    self.wait(e, o, vals[o])
            if gd > 0 and e != "gpsimd":
                self.wait(e, "gdma", gd)

    def raw(self, e, fn, sem=None, inc=0):
        self.q[e].append(("raw", fn, None, 0))
        if sem:
            self.n[sem] += inc
        return self.n[sem] if sem else 0

    def finalize(self, nc):
        with contextlib.ExitStack() as st:
            sems = {s: st.enter_context(nc.semaphore("sem_" + s)) for s in SEMS}
            self.sems = sems
            block = st.enter_context(nc.Block())

            def mk(name):
                def run(eng):
                    for item in self.q[name]:
                        if item[0] == "op":
                            _, a, sem, b = item
                            ins = a(eng)
                            if b:
                                ins.then_inc(sems[sem], b)
                        elif item[0] == "raw":
                            item[1](eng, sems)
                        else:
                            _, prod, v = item
                            eng.wait_ge(sems[prod], v)
                return run

            block.sync(mk("sync"))
            block.gpsimd(mk("gpsimd"))
            block.tensor(mk("tensor"))
            block.vector(mk("vector"))
            block.scalar(mk("scalar"))



def emit_cast_load(em, dst, src_ap, rows, cols):
    n = rows // 128
    for kc in range(n):
        em.dma("gpsimd", lambda g, kc=kc: g.dma_start(
            out=dst[:, kc * cols:(kc + 1) * cols],
            in_=src_ap[kc * 128:(kc + 1) * 128, :]))
    return em.n["gdma"]



def _ps_pad(nc, st, name, shape, dtype):
    esz = 2 if dtype == BF16 else 4
    cols = shape[1]
    per_bank = 2048 // esz
    padded = ((cols + per_bank - 1) // per_bank) * per_bank
    h = st.enter_context(nc.psum_tensor(name, [shape[0], padded], dtype))
    return h


def emit_ln_xnT(em, bufs, m, x_src_ap=None, ready=None):
    if x_src_ap is not None:
        ws = bufs.get("war_scalar")
        wv = bufs.get("war_vector")
        em.wait("sync", "scalar", em.n["scalar"] if ws is None else ws)
        em.wait("sync", "vector", em.n["vector"] if wv is None else wv)
        v = em.dma("sync", lambda s: s.dma_start(
            out=bufs["xchunk"][:], in_=x_src_ap[m * 128:(m + 1) * 128, :]))
        xa = bufs["xchunk"]
        em.wait("vector", "sync", v)
        em.wait("scalar", "sync", v)
    else:
        xa = bufs["xs"][:, m * D:(m + 1) * D]
        for e, val in (ready or []):
            pass
    mini = bufs["mini"]
    mean, negm, var, rstd, nmr = (mini[:, i:i + 1] for i in range(5))
    v1 = em.step("vector", lambda vv, xa=xa: vv.tensor_reduce(mean, xa[:], AX.X, ALU.add))
    v2 = em.step("vector", lambda vv: vv.tensor_scalar(negm, mean, -1.0 / D, None, op0=ALU.mult))
    em.step("vector", lambda vv: vv.tensor_scalar(mean, mean, 1.0 / D, None, op0=ALU.mult))
    em.wait("scalar", "vector", v2)
    s1 = em.step("scalar", lambda sc, xa=xa: sc.activation(bufs["sq"][:], xa[:], AF.Square, bias=negm, scale=1.0))
    em.wait("vector", "scalar", s1)
    em.step("vector", lambda vv: vv.tensor_reduce(var, bufs["sq"][:], AX.X, ALU.add))
    v3 = em.step("vector", lambda vv: vv.tensor_scalar(var, var, 1.0 / D, None, op0=ALU.mult))
    em.wait("scalar", "vector", v3)
    s2 = em.step("scalar", lambda sc: sc.activation(rstd, var, AF.Ln, bias=bufs["eps"], scale=1.0))
    s3 = em.step("scalar", lambda sc: sc.activation(rstd, rstd, AF.Exp, bias=0.0, scale=-0.5))
    em.wait("vector", "scalar", s3)
    xw = bufs.get("xnb_war")
    em.wait("vector", "tensor", em.n["tensor"] if xw is None else xw)
    s4 = em.step("vector", lambda vv, xa=xa: vv.tensor_scalar(
        bufs["xnb"][:], xa[:], negm, rstd, op0=ALU.add, op1=ALU.mult))
    em.wait("tensor", "vector", s4)
    stride = bufs.get("xt_stride", T)
    off = bufs.get("xt_off", m * 128)
    eve = bufs.get("evac_eng", "vector")
    for g in range(2):
        tv = None
        for j in range(4):
            jj = g * 4 + j
            tv = em.step("tensor", lambda tt, jj=jj, j=j: tt.transpose(
                bufs["tp"][:, j * 128:(j + 1) * 128],
                bufs["xnb"][:, jj * 128:(jj + 1) * 128], bufs["idb"][:]))
        em.wait(eve, "tensor", tv)
        ev = None
        for j in range(4):
            jj = g * 4 + j
            if eve == "vector":
                ev = em.step("vector", lambda vv, jj=jj, j=j: vv.tensor_copy(
                    out=bufs["xt"][:, jj * stride + off:jj * stride + off + 128],
                    in_=bufs["tp"][:, j * 128:(j + 1) * 128]))
            else:
                ev = em.step("scalar", lambda sc, jj=jj, j=j: sc.activation(
                    bufs["xt"][:, jj * stride + off:jj * stride + off + 128],
                    bufs["tp"][:, j * 128:(j + 1) * 128],
                    AF.Copy, bias=0.0, scale=1.0))
        em.wait("tensor", eve, ev)
    return em.n["vector"], em.n["tensor"]


def host_prep_common(inputs):
    c = {}
    c["iden"] = np.eye(128, dtype=np.float32)
    return c



def emit_phaseA(nc, em, st, sb, ps, x, wk, wv, proj, iden, kv_out, stab_out):
    xt = sb("xt", [128, KC * T], BF16)
    ktb = sb("ktb", [128, (H // 2) * T], BF16)
    vb = sb("vb", [128, M * D], BF16)
    wkb = sb("wkb", [128, KC * D], BF16)
    wvb = sb("wvb", [128, KC * D], BF16)
    idf = sb("idf", [128, 128], F32)
    idb = sb("idb", [128, 128], BF16)
    pmt = sb("pmt", [128, H * F2], BF16)
    projs = sb("projs", [128, H * HD], F32)
    projb = sb("projb", [128, H * HD], BF16)
    kw = sb("kw", [128, D], F32)
    kbb = sb("kbb", [128, D], BF16)
    sq = sb("sq", [128, D], F32)
    xnb = sb("xnb", [128, D], BF16)
    xchunk = sb("xchunk", [128, D], F32)
    mini = sb("mini", [128, 8], F32)
    mini2 = sb("mini2", [128, 8], F32)
    diagk = sb("diagk", [128, M * H], F32)
    stacc = sb("stacc", [128, H], F32)
    stk = sb("stk", [16, 1], F32)
    strow = sb("strow", [1, 16], F32)
    stb = sb("stb", [128, H], F32)
    onesf = sb("onesf", [1, 128], F32)
    onesb = sb("onesb", [128, 1], BF16)
    escr = sb("escr", [128, 2 * 2 * F2], F32)
    kfb = sb("kfb", [128, 2 * 2 * F2], BF16)
    kvs = sb("kvs", [128, 32 * 65], F32)
    bias = sb("biasA", [128, 2], F32)

    tp = ps("tp", [128, 512], BF16)
    pk = ps("pk", [128, 512], F32)
    pk2 = ps("pk2", [128, 512], F32)
    pu = ps("pu", [128, 128], F32)
    pu2 = ps("pu2", [128, 128], F32)
    pkv = ps("pkv", [128, 130], F32)
    pkv2 = ps("pkv2", [128, 130], F32)
    pmisc = ps("pmisc", [128, 128], F32)

    bufs = dict(xt=xt, sq=sq, mini=mini, xnb=xnb, tp=tp, idb=idb,
                xchunk=xchunk, eps=mini[:, 6:7])
    em.step("vector", lambda vv: vv.memset(mini[:, 6:7], 1e-5))

    vid = em.dma("sync", lambda s: s.dma_start(out=idf[:], in_=iden[:, :]))
    vpr = em.dma("sync", lambda s: s.dma_start(
        out=projs.ap().rearrange("p (h d) -> p h d", h=H),
        in_=proj.rearrange("h f d -> f h d")))
    g1 = emit_cast_load(em, wkb, wk, D, D)
    g2 = emit_cast_load(em, wvb, wv, D, D)
    gid = em.dma("gpsimd", lambda g: g.dma_start(out=idb[:], in_=iden[:, :]))

    em.step("vector", lambda vv: vv.memset(onesf[:], 1.0))
    em.step("vector", lambda vv: vv.memset(onesb[:], 1.0))
    em.step("vector", lambda vv: vv.memset(stacc[:], NEG))
    em.wait("vector", "sync", vpr)
    vp = em.step("vector", lambda vv: vv.tensor_scalar(projb[:], projs[:], 0.125, None, op0=ALU.mult))
    em.wait("tensor", "vector", vp)
    em.wait("tensor", "gdma", gid)
    for h in range(H):
        tv = em.step("tensor", lambda tt, h=h: tt.transpose(
            tp[0:64, 0:128], projb[:, h * HD:(h + 1) * HD], idb[:]))
        em.wait("vector", "tensor", tv)
        e1 = em.step("vector", lambda vv, h=h: vv.tensor_copy(
            out=pmt[0:64, h * F2:(h + 1) * F2], in_=tp[0:64, 0:128]))
        e2 = em.step("vector", lambda vv, h=h: vv.tensor_copy(
            out=pmt[64:128, h * F2:(h + 1) * F2], in_=tp[0:64, 0:128]))
        em.wait("tensor", "vector", e2)

    bufsA = []
    for mx in (mini, mini2):
        bb = dict(bufs)
        bb.update(mini=mx, eps=mx[:, 6:7])
        bufsA.append(bb)
    em.step("vector", lambda vv: vv.memset(mini2[:, 6:7], 1e-5))
    em.barrier(("vector", "gpsimd", "sync"))
    em.wait("tensor", "gdma", em.n["gdma"])

    trkA = dict(ln_s=0, ln_v=0, ln_t=0, kev=[0, 0], diag=0, ktr=0,
                red=[0, 0], tpv=0)
    for m in range(M):
        bb = bufsA[m % 2]
        bb["war_scalar"] = trkA["ln_s"]
        bb["war_vector"] = trkA["ln_v"]
        bb["xnb_war"] = trkA["ln_t"]
        if trkA["tpv"]:
            em.wait("tensor", "vector", trkA["tpv"])
        emit_ln_xnT(em, bb, m, x_src_ap=x)
        trkA["ln_s"] = em.n["scalar"]
        trkA["ln_v"] = em.n["vector"]
        trkA["ln_t"] = em.n["tensor"]
        lnv = em.n["vector"]
        for half in range(2):
            p = pk if half == 0 else pk2
            if trkA["kev"][half]:
                em.wait("tensor", "vector", trkA["kev"][half])
            mv = None
            for kc in range(KC):
                mv = em.step("tensor", lambda tt, kc=kc, half=half, m=m, p=p: tt.matmul(
                    p[:, 0:512],
                    xt[:, kc * T + m * 128:kc * T + (m + 1) * 128],
                    wkb[:, kc * D + half * 512:kc * D + half * 512 + 512],
                    start=(kc == 0), stop=(kc == KC - 1)))
            em.wait("vector", "tensor", mv)
            if half == 0 and trkA["diag"]:
                em.wait("vector", "scalar", trkA["diag"])
            if half == 0 and trkA["ktr"]:
                em.wait("vector", "tensor", trkA["ktr"])
            trkA["kev"][half] = em.step("vector", lambda vv, half=half, p=p: vv.tensor_copy(
                out=kw[:, half * 512:half * 512 + 512], in_=p[:, 0:512]))
        for half in range(2):
            p = pk if half == 0 else pk2
            em.wait("tensor", "vector", trkA["kev"][half])
            mv = None
            for kc in range(KC):
                mv = em.step("tensor", lambda tt, kc=kc, half=half, m=m, p=p: tt.matmul(
                    p[:, 0:512],
                    xt[:, kc * T + m * 128:kc * T + (m + 1) * 128],
                    wvb[:, kc * D + half * 512:kc * D + half * 512 + 512],
                    start=(kc == 0), stop=(kc == KC - 1)))
            em.wait("vector", "tensor", mv)
            trkA["kev"][half] = em.step("vector", lambda vv, half=half, m=m, p=p: vv.tensor_copy(
                out=vb[:, m * D + half * 512:m * D + half * 512 + 512], in_=p[:, 0:512]))
        kv_evac = em.n["vector"]
        em.wait("scalar", "vector", kv_evac)
        for h in range(H):
            em.step("scalar", lambda sc, h=h, m=m: sc.activation(
                sq[:, 0:HD], kw[:, h * HD:(h + 1) * HD], AF.Square,
                bias=0.0, scale=1.0, accum_out=diagk[:, m * H + h:m * H + h + 1]))
        trkA["diag"] = em.n["scalar"]
        vc = em.step("vector", lambda vv: vv.tensor_copy(out=kbb[:], in_=kw[:]))
        em.wait("tensor", "vector", vc)
        tv = None
        for h2 in range(KC):
            tv = em.step("tensor", lambda tt, h2=h2: tt.transpose(
                tp[:, (h2 % 4) * 128:(h2 % 4) * 128 + 128],
                kbb[:, h2 * 128:(h2 + 1) * 128], idb[:]))
            if h2 % 4 == 3:
                em.wait("vector", "tensor", tv)
                ev = None
                for j2 in range(h2 - 3, h2 + 1):
                    ev = em.step("vector", lambda vv, j2=j2, m=m: vv.tensor_copy(
                        out=ktb[:, j2 * T + m * 128:j2 * T + (m + 1) * 128],
                        in_=tp[:, (j2 % 4) * 128:(j2 % 4) * 128 + 128]))
                em.wait("tensor", "vector", ev)
        trkA["ktr"] = em.n["tensor"]
        trkA["tpv"] = em.n["vector"]
        for h in range(H):
            p = pu if h % 2 == 0 else pu2
            pb = (h % 2) * 64
            if trkA["red"][h % 2]:
                em.wait("tensor", "vector", trkA["red"][h % 2])
            mv = em.step("tensor", lambda tt, m=m, h=h, p=p, pb=pb: tt.matmul(
                p[:, 0:128],
                ktb[pb:pb + 64, (h // 2) * T + m * 128:(h // 2) * T + (m + 1) * 128],
                pmt[pb:pb + 64, h * F2:(h + 1) * F2],
                start=True, stop=True))
            em.wait("vector", "tensor", mv)
            trkA["red"][h % 2] = em.step("vector", lambda vv, p=p: vv.tensor_reduce(
                mini[:, 5:6], p[:, 0:128], AX.X, ALU.max))
            em.step("vector", lambda vv, h=h: vv.tensor_max(
                stacc[:, h:h + 1], stacc[:, h:h + 1], mini[:, 5:6]))
    em.barrier(("tensor", "vector", "scalar"))

    em.wait("tensor", "vector", em.n["vector"])
    tv = em.step("tensor", lambda tt: tt.transpose(pmisc[0:16, 0:128], stacc[:], idf[:]))
    em.wait("vector", "tensor", tv)
    em.step("vector", lambda vv: vv.tensor_reduce(stk[:], pmisc[0:16, 0:128], AX.X, ALU.max))
    vv1 = em.step("vector", lambda vv: vv.tensor_copy(out=sq[0:16, 0:1], in_=stk[:]))
    em.wait("tensor", "vector", vv1)
    tv = em.step("tensor", lambda tt: tt.transpose(pmisc[0:1, 0:16], sq[0:16, 0:1], idf[0:16, 0:16]))
    em.wait("vector", "tensor", tv)
    vv2 = em.step("vector", lambda vv: vv.tensor_copy(out=strow[:], in_=pmisc[0:1, 0:16]))
    em.wait("tensor", "vector", vv2)
    tv = em.step("tensor", lambda tt: tt.matmul(pmisc[:, 0:16], onesf[:], strow[:], start=True, stop=True))
    em.wait("vector", "tensor", tv)
    em.step("vector", lambda vv: vv.tensor_copy(out=stb[:], in_=pmisc[:, 0:16]))
    em.barrier(("tensor", "vector", "scalar"))

    em.step("vector", lambda vv: vv.memset(kvs[:], 0.0))
    p3 = dict(exp=[0, 0], add=[0, 0], kvm=[0, 0])
    for m in range(M):
        for h in range(H):
            i = m * H + h
            pr = i % 2
            pU = pu if pr == 0 else pu2
            pK = pkv if pr == 0 else pkv2
            e0 = pr * 2 * F2
            pb = (h % 2) * 64
            if p3["exp"][pr]:
                em.wait("tensor", "scalar", p3["exp"][pr])
            mv = em.step("tensor", lambda tt, m=m, h=h, p=pU, pb=pb: tt.matmul(
                p[:, 0:128],
                ktb[pb:pb + 64, (h // 2) * T + m * 128:(h // 2) * T + (m + 1) * 128],
                pmt[pb:pb + 64, h * F2:(h + 1) * F2],
                start=True, stop=True))
            bv = em.step("vector", lambda vv, m=m, h=h, pr=pr: vv.tensor_scalar(
                bias[:, pr:pr + 1], diagk[:, m * H + h:m * H + h + 1], -0.5,
                stb[:, h:h + 1], op0=ALU.mult, op1=ALU.subtract))
            em.wait("scalar", "tensor", mv)
            em.wait("scalar", "vector", bv)
            if p3["kvm"][pr]:
                em.wait("scalar", "tensor", p3["kvm"][pr])
            s1 = em.step("scalar", lambda sc, p=pU, e0=e0, pr=pr: sc.activation(
                escr[:, e0:e0 + F2], p[:, 0:128], AF.Exp,
                bias=bias[:, pr:pr + 1], scale=1.0))
            p3["exp"][pr] = em.step("scalar", lambda sc, p=pU, e0=e0, pr=pr: sc.activation(
                escr[:, e0 + F2:e0 + 2 * F2], p[:, 0:128], AF.Exp,
                bias=bias[:, pr:pr + 1], scale=-1.0))
            em.wait("vector", "scalar", p3["exp"][pr])
            vc = em.step("vector", lambda vv, e0=e0: vv.tensor_scalar(
                kfb[:, e0:e0 + 2 * F2], escr[:, e0:e0 + 2 * F2],
                0.0625, None, op0=ALU.mult))
            em.wait("tensor", "vector", vc)
            if p3["add"][pr]:
                em.wait("tensor", "vector", p3["add"][pr])
            mv2 = None
            for half in range(2):
                mv2 = em.step("tensor", lambda tt, half=half, m=m, h=h, p=pK, e0=e0: tt.matmul(
                    p[:, half * 65:half * 65 + 64],
                    kfb[:, e0 + half * 128:e0 + half * 128 + 128],
                    vb[:, m * D + h * HD:m * D + (h + 1) * HD],
                    start=(half == 0), stop=False))
                mv2 = em.step("tensor", lambda tt, half=half, p=pK, e0=e0: tt.matmul(
                    p[:, half * 65 + 64:half * 65 + 65],
                    kfb[:, e0 + half * 128:e0 + half * 128 + 128],
                    onesb[:],
                    start=False, stop=(half == 1)))
            p3["kvm"][pr] = mv2
            em.wait("vector", "tensor", mv2)
            av = None
            for half in range(2):
                tile = h * 2 + half
                av = em.step("vector", lambda vv, tile=tile, half=half, p=pK: vv.tensor_add(
                    kvs[:, tile * 65:(tile + 1) * 65],
                    kvs[:, tile * 65:(tile + 1) * 65],
                    p[:, half * 65:half * 65 + 65]))
            p3["add"][pr] = av

    fv = em.n["vector"]
    em.wait("sync", "vector", fv)
    o1 = em.dma("sync", lambda s: s.dma_start(out=kv_out[:, :], in_=kvs[:]))
    o2 = em.dma("sync", lambda s: s.dma_start(out=stab_out[:, :], in_=stk[:],
                                              allow_slow_non_contiguous=True))
    em.wait("sync", "sync", o2)




def build_launchA():
    nc = bass.Bass("TRN2", target_bir_lowering=False, detect_race_conditions=False)
    x = nc.dram_tensor("x", [T, D], F32, kind="ExternalInput").ap()
    wk = nc.dram_tensor("wk", [D, D], F32, kind="ExternalInput").ap()
    wv = nc.dram_tensor("wv", [D, D], F32, kind="ExternalInput").ap()
    proj = nc.dram_tensor("proj", [H, F2, HD], F32, kind="ExternalInput").ap()
    iden = nc.dram_tensor("iden", [128, 128], F32, kind="ExternalInput").ap()
    kv_out = nc.dram_tensor("kv_out", [128, 32 * 65], F32, kind="ExternalOutput").ap()
    stab_out = nc.dram_tensor("stab_out", [16, 1], F32, kind="ExternalOutput").ap()

    em = Em()
    with contextlib.ExitStack() as st:
        def sb(name, shape, dtype):
            return st.enter_context(nc.sbuf_tensor(name, shape, dtype))

        def ps(name, shape, dtype):
            return _ps_pad(nc, st, name, shape, dtype)

        emit_phaseA(nc, em, st, sb, ps, x, wk, wv, proj, iden, kv_out, stab_out)
        em.finalize(nc)
    return nc



def emit_phaseB(nc, em, st, sb, ps, x, wq, wout, proj, iden, kv0, kv1, st0, st1, y_out):
    xt = sb("xt", [128, KC * T], BF16)
    t8a = sb("t8a", [128, KC * T], BF16)
    wqb = sb("wqb", [128, KC * D], BF16)
    wob = sb("wob", [128, KC * D], BF16)
    idf = sb("idf", [128, 128], F32)
    idb = sb("idb", [128, 128], BF16)
    pmt = sb("pmt", [128, H * F2], BF16)
    projs = sb("projs", [128, H * HD], F32)
    projb = sb("projb", [128, H * HD], BF16)
    kv0s = sb("kv0s", [128, 32 * 65], F32)
    kv1s = sb("kv1s", [128, 32 * 65], F32)
    kvc = sb("kvc", [128, 32 * 65], F32)
    kvcb = sb("kvcb", [128, 32 * 65], BF16)
    st0s = sb("st0s", [16, 1], F32)
    st1s = sb("st1s", [16, 1], F32)
    smax = sb("smax", [16, 1], F32)
    sd = sb("sd", [16, 2], F32)
    srow = sb("srow", [1, 32], F32)
    mbc = sb("mbc", [128, 32], F32)
    kw = sb("kw", [128, D], F32)
    kbb = sb("kbb", [128, D], BF16)
    sq = sb("sq", [128, D], F32)
    xnb = sb("xnb", [128, D], BF16)
    xchunk = sb("xchunk", [128, D], F32)
    mini = sb("mini", [128, 8], F32)
    mini2 = sb("mini2", [128, 8], F32)
    diagq = sb("diagq", [128, H], F32)
    qts = sb("qts", [128, (H // 2) * 128], BF16)
    escr = sb("escr", [128, 2 * 2 * F2], F32)
    qfb = sb("qfb", [128, 2 * 2 * F2], BF16)
    qftb = sb("qftb", [128, 2 * 2 * F2], BF16)
    aob = sb("aob", [128, D], BF16)
    yo = sb("yo", [128, D], F32)
    onesf = sb("onesf", [1, 128], F32)

    tp = ps("tp", [128, 512], BF16)
    pk = ps("pk", [128, 512], F32)
    pk2 = ps("pk2", [128, 512], F32)
    pu = ps("pu", [128, 128], F32)
    pu2 = ps("pu2", [128, 128], F32)
    po = ps("po", [128, 65], F32)
    po2 = ps("po2", [128, 65], F32)
    pmisc = ps("pmisc", [128, 128], F32)

    bufs = dict(xt=xt, sq=sq, mini=mini, xnb=xnb, tp=tp, idb=idb,
                xchunk=xchunk, eps=mini[:, 6:7])
    em.step("vector", lambda vv: vv.memset(mini[:, 6:7], 1e-5))

    vid = em.dma("sync", lambda s: s.dma_start(out=idf[:], in_=iden[:, :]))
    vpr = em.dma("sync", lambda s: s.dma_start(
        out=projs.ap().rearrange("p (h d) -> p h d", h=H),
        in_=proj.rearrange("h f d -> f h d")))
    vkv = em.dma("sync", lambda s: s.dma_start(out=kv0s[:], in_=kv0[:, :]))
    vkv = em.dma("sync", lambda s: s.dma_start(out=kv1s[:], in_=kv1[:, :]))
    vst = em.dma("sync", lambda s: s.dma_start(out=st0s[:], in_=st0[:, :],
                                               allow_slow_non_contiguous=True))
    vst = em.dma("sync", lambda s: s.dma_start(out=st1s[:], in_=st1[:, :],
                                               allow_slow_non_contiguous=True))
    g1 = emit_cast_load(em, wqb, wq, D, D)
    g2 = emit_cast_load(em, wob, wout, D, D)
    gid = em.dma("gpsimd", lambda g: g.dma_start(out=idb[:], in_=iden[:, :]))

    em.step("vector", lambda vv: vv.memset(onesf[:], 1.0))
    em.wait("vector", "sync", vpr)
    vp = em.step("vector", lambda vv: vv.tensor_scalar(projb[:], projs[:], 0.125, None, op0=ALU.mult))
    em.wait("tensor", "vector", vp)
    em.wait("tensor", "gdma", gid)
    for h in range(H):
        tv = em.step("tensor", lambda tt, h=h: tt.transpose(
            tp[0:64, 0:128], projb[:, h * HD:(h + 1) * HD], idb[:]))
        em.wait("vector", "tensor", tv)
        e1 = em.step("vector", lambda vv, h=h: vv.tensor_copy(
            out=pmt[0:64, h * F2:(h + 1) * F2], in_=tp[0:64, 0:128]))
        e2 = em.step("vector", lambda vv, h=h: vv.tensor_copy(
            out=pmt[64:128, h * F2:(h + 1) * F2], in_=tp[0:64, 0:128]))
        em.wait("tensor", "vector", e2)

    em.wait("vector", "sync", em.n["sync"])
    em.step("vector", lambda vv: vv.tensor_max(smax[:], st0s[:], st1s[:]))
    em.step("vector", lambda vv: vv.tensor_sub(sd[:, 0:1], st0s[:], smax[:]))
    em.step("vector", lambda vv: vv.tensor_sub(sd[:, 1:2], st1s[:], smax[:]))
    sv = em.n["vector"]
    em.wait("scalar", "vector", sv)
    se = em.step("scalar", lambda sc: sc.activation(sd[:, 0:2], sd[:, 0:2], AF.Exp, bias=0.0, scale=1.0))
    em.wait("tensor", "scalar", se)
    em.wait("tensor", "sync", vid)
    tv = em.step("tensor", lambda tt: tt.transpose(pmisc[0:1, 0:16], sd[0:16, 0:1], idf[0:16, 0:16]))
    tv = em.step("tensor", lambda tt: tt.transpose(pmisc[0:1, 16:32], sd[0:16, 1:2], idf[0:16, 0:16]))
    em.wait("vector", "tensor", tv)
    ev = em.step("vector", lambda vv: vv.tensor_copy(out=srow[0:1, 0:16], in_=pmisc[0:1, 0:16]))
    ev = em.step("vector", lambda vv: vv.tensor_copy(out=srow[0:1, 16:32], in_=pmisc[0:1, 16:32]))
    em.wait("tensor", "vector", ev)
    tv = em.step("tensor", lambda tt: tt.matmul(pmisc[:, 0:32], onesf[:], srow[:], start=True, stop=True))
    em.wait("vector", "tensor", tv)
    em.step("vector", lambda vv: vv.tensor_copy(out=mbc[:], in_=pmisc[:, 0:32]))
    for h in range(H):
        for half in range(2):
            tile = h * 2 + half
            sl = slice(tile * 65, (tile + 1) * 65)
            em.step("vector", lambda vv, sl=sl, h=h: vv.tensor_scalar(
                kvc[:, sl], kv1s[:, sl], mbc[:, 16 + h:17 + h], None, op0=ALU.mult))
            em.step("vector", lambda vv, sl=sl, h=h: vv.scalar_tensor_tensor(
                kvc[:, sl], kv0s[:, sl], mbc[:, h:h + 1], kvc[:, sl],
                op0=ALU.mult, op1=ALU.add))
    em.step("vector", lambda vv: vv.tensor_copy(out=kvcb[:], in_=kvc[:]))

    bufsB = []
    for mx in (mini, mini2):
        bb = dict(bufs)
        bb.update(mini=mx, eps=mx[:, 6:7])
        bufsB.append(bb)
    em.step("vector", lambda vv: vv.memset(mini2[:, 6:7], 1e-5))
    em.barrier(("vector", "gpsimd", "sync"))
    em.wait("tensor", "gdma", em.n["gdma"])

    trkB = dict(ln_s=0, ln_v=0, ln_t=0, kev=[0, 0], diag=0, ktr=0, tpv=0)
    p3 = dict(exp=[0, 0], av=[0, 0], kvm=[0, 0], qtr=[0, 0])
    minis = (mini, mini2)
    for m in range(M):
        bb = bufsB[m % 2]
        bb["war_scalar"] = trkB["ln_s"]
        bb["war_vector"] = trkB["ln_v"]
        bb["xnb_war"] = trkB["ln_t"]
        if trkB["tpv"]:
            em.wait("tensor", "vector", trkB["tpv"])
        emit_ln_xnT(em, bb, m, x_src_ap=x)
        trkB["ln_s"] = em.n["scalar"]
        trkB["ln_v"] = em.n["vector"]
        trkB["ln_t"] = em.n["tensor"]
        for half in range(2):
            p = pk if half == 0 else pk2
            if trkB["kev"][half]:
                em.wait("tensor", "vector", trkB["kev"][half])
            mv = None
            for kc in range(KC):
                mv = em.step("tensor", lambda tt, kc=kc, half=half, m=m, p=p: tt.matmul(
                    p[:, 0:512],
                    xt[:, kc * T + m * 128:kc * T + (m + 1) * 128],
                    wqb[:, kc * D + half * 512:kc * D + half * 512 + 512],
                    start=(kc == 0), stop=(kc == KC - 1)))
            em.wait("vector", "tensor", mv)
            if half == 0 and trkB["diag"]:
                em.wait("vector", "scalar", trkB["diag"])
            if half == 0 and trkB["ktr"]:
                em.wait("vector", "tensor", trkB["ktr"])
            trkB["kev"][half] = em.step("vector", lambda vv, half=half, p=p: vv.tensor_copy(
                out=kw[:, half * 512:half * 512 + 512], in_=p[:, 0:512]))
        q_evac = em.n["vector"]
        em.wait("scalar", "vector", q_evac)
        for h in range(H):
            em.step("scalar", lambda sc, h=h: sc.activation(
                sq[:, 0:HD], kw[:, h * HD:(h + 1) * HD], AF.Square,
                bias=0.0, scale=1.0, accum_out=diagq[:, h:h + 1]))
        trkB["diag"] = em.n["scalar"]
        vc = em.step("vector", lambda vv: vv.tensor_copy(out=kbb[:], in_=kw[:]))
        em.wait("tensor", "vector", vc)
        tv = None
        for h2 in range(KC):
            tv = em.step("tensor", lambda tt, h2=h2: tt.transpose(
                tp[:, (h2 % 4) * 128:(h2 % 4) * 128 + 128],
                kbb[:, h2 * 128:(h2 + 1) * 128], idb[:]))
            if h2 % 4 == 3:
                em.wait("vector", "tensor", tv)
                ev = None
                for j2 in range(h2 - 3, h2 + 1):
                    ev = em.step("vector", lambda vv, j2=j2: vv.tensor_copy(
                        out=qts[:, j2 * 128:(j2 + 1) * 128],
                        in_=tp[:, (j2 % 4) * 128:(j2 % 4) * 128 + 128]))
                em.wait("tensor", "vector", ev)
        trkB["ktr"] = em.n["tensor"]
        em.wait("tensor", "scalar", trkB["diag"])
        for h in range(H):
            i = m * H + h
            pr = i % 2
            pU = pu if pr == 0 else pu2
            pO = po if pr == 0 else po2
            mxh = minis[pr]
            e0 = pr * 2 * F2
            pb = (h % 2) * 64
            if p3["exp"][pr]:
                em.wait("tensor", "scalar", p3["exp"][pr])
            mv = em.step("tensor", lambda tt, h=h, p=pU, pb=pb: tt.matmul(
                p[:, 0:128],
                qts[pb:pb + 64, (h // 2) * 128:(h // 2) * 128 + 128],
                pmt[pb:pb + 64, h * F2:(h + 1) * F2],
                start=True, stop=True))
            em.wait("vector", "tensor", mv)
            rv = em.step("vector", lambda vv, p=pU, mxh=mxh: vv.tensor_reduce(
                mxh[:, 5:6], p[:, 0:128], AX.X, ALU.max))
            bv = em.step("vector", lambda vv, h=h, mxh=mxh: vv.tensor_scalar(
                mxh[:, 7:8], diagq[:, h:h + 1], -0.5, mxh[:, 5:6],
                op0=ALU.mult, op1=ALU.subtract))
            em.wait("scalar", "vector", bv)
            if p3["kvm"][pr]:
                em.wait("scalar", "tensor", p3["kvm"][pr])
            s1 = em.step("scalar", lambda sc, p=pU, e0=e0, mxh=mxh: sc.activation(
                escr[:, e0:e0 + F2], p[:, 0:128], AF.Exp, bias=mxh[:, 7:8], scale=1.0))
            p3["exp"][pr] = em.step("scalar", lambda sc, p=pU, e0=e0, mxh=mxh: sc.activation(
                escr[:, e0 + F2:e0 + 2 * F2], p[:, 0:128], AF.Exp, bias=mxh[:, 7:8], scale=-1.0))
            em.wait("vector", "scalar", p3["exp"][pr])
            vc = em.step("vector", lambda vv, e0=e0: vv.tensor_scalar(
                qfb[:, e0:e0 + 2 * F2], escr[:, e0:e0 + 2 * F2],
                0.0625, None, op0=ALU.mult))
            em.wait("tensor", "vector", vc)
            if p3["qtr"][pr]:
                em.wait("tensor", "vector", p3["qtr"][pr])
            tv = em.step("tensor", lambda tt, e0=e0, pr=pr: tt.transpose(
                tp[:, pr * 256:pr * 256 + 128], qfb[:, e0:e0 + 128], idb[:]))
            tv = em.step("tensor", lambda tt, e0=e0, pr=pr: tt.transpose(
                tp[:, pr * 256 + 128:pr * 256 + 256], qfb[:, e0 + 128:e0 + 256], idb[:]))
            em.wait("vector", "tensor", tv)
            ev = em.step("vector", lambda vv, e0=e0, pr=pr: vv.tensor_copy(
                out=qftb[:, e0:e0 + 256], in_=tp[:, pr * 256:pr * 256 + 256]))
            p3["qtr"][pr] = ev
            em.wait("tensor", "vector", ev)
            if p3["av"][pr]:
                em.wait("tensor", "vector", p3["av"][pr])
            mo = None
            for c in range(2):
                tile = h * 2 + c
                mo = em.step("tensor", lambda tt, c=c, tile=tile, p=pO, e0=e0: tt.matmul(
                    p[:, 0:65],
                    qftb[:, e0 + c * 128:e0 + (c + 1) * 128],
                    kvcb[:, tile * 65:(tile + 1) * 65],
                    start=(c == 0), stop=(c == 1)))
            p3["kvm"][pr] = mo
            em.wait("vector", "tensor", mo)
            zv = em.step("vector", lambda vv, p=pO, mxh=mxh: vv.tensor_scalar(
                mxh[:, 5:6], p[:, 64:65], 1e-6, None, op0=ALU.add))
            rv = em.step("vector", lambda vv, mxh=mxh: vv.reciprocal(mxh[:, 7:8], mxh[:, 5:6]))
            av = em.step("vector", lambda vv, h=h, p=pO, mxh=mxh: vv.tensor_scalar(
                aob[:, h * HD:(h + 1) * HD], p[:, 0:64], mxh[:, 7:8], None, op0=ALU.mult))
            p3["av"][pr] = av
        em.wait("tensor", "vector", em.n["vector"])
        if p3["qtr"][0]:
            p3["qtr"] = [em.n["vector"], em.n["vector"]]
        for g in range(2):
            tv = None
            for j in range(4):
                jj = g * 4 + j
                tv = em.step("tensor", lambda tt, jj=jj, j=j: tt.transpose(
                    tp[:, j * 128:(j + 1) * 128],
                    aob[:, jj * 128:(jj + 1) * 128], idb[:]))
            em.wait("vector", "tensor", tv)
            ev = None
            for j in range(4):
                jj = g * 4 + j
                ev = em.step("vector", lambda vv, jj=jj, j=j, m=m: vv.tensor_copy(
                    out=t8a[:, jj * T + m * 128:jj * T + (m + 1) * 128],
                    in_=tp[:, j * 128:(j + 1) * 128]))
            em.wait("tensor", "vector", ev)
        for half in range(2):
            p = pk if half == 0 else pk2
            if trkB["kev"][half]:
                em.wait("tensor", "vector", trkB["kev"][half])
            mv = None
            for kc in range(KC):
                mv = em.step("tensor", lambda tt, kc=kc, half=half, m=m, p=p: tt.matmul(
                    p[:, 0:512],
                    t8a[:, kc * T + m * 128:kc * T + (m + 1) * 128],
                    wob[:, kc * D + half * 512:kc * D + half * 512 + 512],
                    start=(kc == 0), stop=(kc == KC - 1)))
            em.wait("vector", "tensor", mv)
            if half == 0 and m > 0:
                em.wait("vector", "sync", trkB.get("ydma", 0))
            trkB["kev"][half] = em.step("vector", lambda vv, half=half, p=p: vv.tensor_copy(
                out=yo[:, half * 512:half * 512 + 512], in_=p[:, 0:512]))
        yv = em.n["vector"]
        em.wait("sync", "vector", yv)
        trkB["ydma"] = em.dma("sync", lambda s, m=m: s.dma_start(
            out=y_out[m * 128:(m + 1) * 128, :], in_=yo[:]))

    em.wait("sync", "sync", em.n["sync"])



def build_launchB():
    nc = bass.Bass("TRN2", target_bir_lowering=False, detect_race_conditions=False)
    x = nc.dram_tensor("x", [T, D], F32, kind="ExternalInput").ap()
    wq = nc.dram_tensor("wq", [D, D], F32, kind="ExternalInput").ap()
    wout = nc.dram_tensor("wout", [D, D], F32, kind="ExternalInput").ap()
    proj = nc.dram_tensor("proj", [H, F2, HD], F32, kind="ExternalInput").ap()
    iden = nc.dram_tensor("iden", [128, 128], F32, kind="ExternalInput").ap()
    kv0 = nc.dram_tensor("kv0", [128, 32 * 65], F32, kind="ExternalInput").ap()
    kv1 = nc.dram_tensor("kv1", [128, 32 * 65], F32, kind="ExternalInput").ap()
    st0 = nc.dram_tensor("st0", [16, 1], F32, kind="ExternalInput").ap()
    st1 = nc.dram_tensor("st1", [16, 1], F32, kind="ExternalInput").ap()
    y_out = nc.dram_tensor("y_out", [T, D], F32, kind="ExternalOutput").ap()

    em = Em()
    with contextlib.ExitStack() as st:
        def sb(name, shape, dtype):
            return st.enter_context(nc.sbuf_tensor(name, shape, dtype))

        def ps(name, shape, dtype):
            return _ps_pad(nc, st, name, shape, dtype)

        emit_phaseB(nc, em, st, sb, ps, x, wq, wout, proj, iden, kv0, kv1, st0, st1, y_out)
        em.finalize(nc)
    return nc



def emit_phaseC(nc, em, st, sb, ps, x, ya, rw, rb, w1, b1t, w2, b2, iden_in, x1_out):
    E0 = 8
    HID = 4096
    NQ = 4
    xt = sb("xt", [128, KC * T], BF16)
    acc = sb("acc", [128, M * D], F32)
    w1b = [sb(f"w1b{i}", [128, KC * 1024], BF16) for i in range(2)]
    w2b = [sb(f"w2b{i}", [128, KC * 1024], BF16) for i in range(2)]
    b1s = [sb(f"b1s{i}", [128, HID // 128], F32) for i in range(2)]
    b2r = [sb(f"b2r{i}", [1, D], BF16) for i in range(2)]
    rwb = sb("rwb", [128, KC * 8], BF16)
    rbb = sb("rbb", [1, 8], BF16)
    onesr = sb("onesr", [1, 128], BF16)
    idb = sb("idb", [128, 128], BF16)
    sq = sb("sq", [128, D], F32)
    xnb = sb("xnb", [128, D], BF16)
    xchunk = sb("xchunk", [128, D], F32)
    mini = sb("mini", [128, 8], F32)
    lg = sb("lg", [128, 8], F32)
    m8s = sb("m8s", [128, 8], F32)
    ge8 = sb("ge8", [128, 8], I32)
    es8 = sb("es8", [128, 8], F32)
    ts8 = sb("ts8", [128, 8], F32)
    zs8 = sb("zs8", [128, 8], F32)
    g_all = sb("g_all", [128, M * 8], F32)
    hb = sb("hb", [128, 8 * 512], BF16)

    tp = ps("tp", [128, 512], BF16)
    py0 = ps("py0", [128, 1024], F32)
    py1 = ps("py1", [128, 1024], F32)
    ph = ps("ph", [128, 512], F32)
    ph2 = ps("ph2", [128, 512], F32)
    pr = ps("pr", [128, 8], F32)

    bufs = dict(xt=xt, sq=sq, mini=mini, xnb=xnb, tp=tp, idb=idb,
                xchunk=xchunk, eps=mini[:, 6:7])
    em.step("vector", lambda vv: vv.memset(mini[:, 6:7], 1e-5))
    em.step("vector", lambda vv: vv.memset(onesr[:], 1.0))
    em.step("vector", lambda vv: vv.memset(zs8[:], 0.0))

    gid = em.dma("gpsimd", lambda g: g.dma_start(out=idb[:], in_=iden_in[:, :]))
    for kc in range(KC):
        em.dma("gpsimd", lambda g, kc=kc: g.dma_start(
            out=rwb[:, kc * 8:(kc + 1) * 8], in_=rw[kc * 128:(kc + 1) * 128, :]))
    grb = em.dma("gpsimd", lambda g: g.dma_start(out=rbb[:], in_=rb[:, :]))
    vya = em.dma("sync", lambda s: s.dma_start(
        out=acc.ap().rearrange("p (m n) -> p m n", m=M),
        in_=ya.rearrange("(m p) n -> p m n", p=128)))

    for m in range(M):
        emit_ln_xnT(em, bufs, m, x_src_ap=x)
        em.step("vector", lambda vv, m=m: vv.tensor_add(
            acc[:, m * D:(m + 1) * D], xchunk[:], acc[:, m * D:(m + 1) * D]))
    em.barrier(("tensor", "vector", "scalar", "gpsimd"))

    for m in range(M):
        mv = None
        em.wait("tensor", "vector", em.n["vector"])
        for kc in range(KC):
            mv = em.step("tensor", lambda tt, kc=kc, m=m: tt.matmul(
                pr[:, 0:8],
                xt[:, kc * T + m * 128:kc * T + (m + 1) * 128],
                rwb[:, kc * 8:(kc + 1) * 8],
                start=(kc == 0), stop=False))
        mv = em.step("tensor", lambda tt: tt.matmul(
            pr[:, 0:8], onesr[:], rbb[:], start=False, stop=True))
        em.wait("vector", "tensor", mv)
        em.step("vector", lambda vv: vv.tensor_copy(out=lg[:], in_=pr[:, 0:8]))
        em.step("vector", lambda vv: vv.max(out=m8s[:], in_=lg[:]))
        em.step("vector", lambda vv: vv.tensor_scalar(
            mini[:, 0:1], m8s[:, 0:1], -1.0, None, op0=ALU.mult))
        em.step("vector", lambda vv: vv.tensor_scalar(
            ge8[:], lg[:], m8s[:, 1:2], 1, op0=ALU.is_ge, op1=ALU.mult))
        sv = em.n["vector"]
        em.wait("scalar", "vector", sv)
        s1 = em.step("scalar", lambda sc: sc.activation(
            es8[:], lg[:], AF.Exp, bias=mini[:, 0:1], scale=1.0))
        em.wait("vector", "scalar", s1)
        em.step("vector", lambda vv: vv.select(ts8[:], ge8[:], es8[:], zs8[:]))
        em.step("vector", lambda vv: vv.tensor_reduce(mini[:, 1:2], ts8[:], AX.X, ALU.add))
        em.step("vector", lambda vv: vv.reciprocal(mini[:, 2:3], mini[:, 1:2]))
        em.step("vector", lambda vv, m=m: vv.tensor_scalar(
            g_all[:, m * 8:(m + 1) * 8], ts8[:], mini[:, 2:3], None, op0=ALU.mult))

    em.barrier(("tensor", "vector", "scalar"))

    NU = E0 * NQ

    def emit_loads(u):
        e, q = u // NQ, u % NQ
        bi = u % 2
        for kc in range(KC):
            em.dma("gpsimd", lambda g, e=e, q=q, kc=kc, bi=bi: g.dma_start(
                out=w1b[bi][:, kc * 1024:(kc + 1) * 1024],
                in_=w1[e, kc * 128:(kc + 1) * 128, q * 1024:(q + 1) * 1024]))
        for kc in range(KC):
            em.dma("gpsimd", lambda g, e=e, q=q, kc=kc, bi=bi: g.dma_start(
                out=w2b[bi][:, kc * 1024:(kc + 1) * 1024],
                in_=w2[e, q * 1024 + kc * 128:q * 1024 + (kc + 1) * 128, :]))
        if q == 0:
            ei = (u // NQ) % 2
            em.dma("gpsimd", lambda g, e=e, ei=ei: g.dma_start(
                out=b1s[ei][:], in_=b1t[e]))
            em.dma("gpsimd", lambda g, e=e, ei=ei: g.dma_start(
                out=b2r[ei][:], in_=b2[e:e + 1, :]))
        return (em.n["gdma"], em.n["sync"])

    load_end = {}
    compute_end = {0: 0}
    em.wait("gpsimd", "vector", em.n["vector"])
    load_end[0] = emit_loads(0)
    evac_tile = [0] * 8
    py_last = [0, 0]
    pidx = 0
    for u in range(NU):
        e, q = u // NQ, u % NQ
        bi = u % 2
        ei = (u // NQ) % 2
        if u + 1 < NU:
            if u >= 1:
                em.wait("gpsimd", "tensor", compute_end[u - 1])
            load_end[u + 1] = emit_loads(u + 1)
        em.wait("tensor", "gdma", load_end[u][0])
        for g4 in range(4):
            for s in range(8):
                pb = ph if s % 2 == 0 else ph2
                prev = evac_tile[s - 2] if s >= 2 else evac_tile[6 + s]
                if prev:
                    em.wait("tensor", "scalar", prev)
                mv = None
                for kc in range(KC):
                    mv = em.step("tensor", lambda tt, kc=kc, s=s, g4=g4, bi=bi, pb=pb: tt.matmul(
                        pb[:, 0:512],
                        w1b[bi][:, kc * 1024 + s * 128:kc * 1024 + s * 128 + 128],
                        xt[:, kc * T + g4 * 512:kc * T + g4 * 512 + 512],
                        start=(kc == 0), stop=(kc == KC - 1)))
                em.wait("scalar", "tensor", mv)
                bc = q * 8 + s
                evac_tile[s] = em.step("scalar", lambda sc, s=s, bc=bc, pb=pb, ei=ei: sc.activation(
                    hb[:, s * 512:(s + 1) * 512],
                    pb[:, 0:512],
                    AF.Relu, bias=b1s[ei][:, bc:bc + 1], scale=1.0))
            for ts_ in range(4):
                pp = pidx % 2
                p_y = py0 if pp == 0 else py1
                pidx += 1
                if py_last[pp]:
                    em.wait("tensor", "vector", py_last[pp])
                if q == 0:
                    for nh in range(2):
                        em.step("tensor", lambda tt, nh=nh, p=p_y, ei=ei: tt.matmul(
                            p[:, nh * 512:nh * 512 + 512],
                            onesr[:], b2r[ei][0:1, nh * 512:nh * 512 + 512],
                            start=True, stop=False))
                mv = None
                for s in range(8):
                    if ts_ == 0:
                        em.wait("tensor", "scalar", evac_tile[s])
                    for nh in range(2):
                        stf = (q > 0 and s == 0)
                        spf = (s == 7)
                        mv = em.step("tensor", lambda tt, s=s, ts_=ts_, nh=nh, p=p_y, bi=bi, stf=stf, spf=spf: tt.matmul(
                            p[:, nh * 512:nh * 512 + 512],
                            hb[:, s * 512 + ts_ * 128:s * 512 + ts_ * 128 + 128],
                            w2b[bi][:, s * 1024 + nh * 512:s * 1024 + nh * 512 + 512],
                            start=stf, stop=spf))
                em.wait("vector", "tensor", mv)
                mcol = g4 * 4 + ts_
                py_last[pp] = em.step("vector", lambda vv, mcol=mcol, e=e, p=p_y: vv.scalar_tensor_tensor(
                    acc[:, mcol * D:(mcol + 1) * D],
                    p[:, 0:1024],
                    g_all[:, mcol * 8 + e:mcol * 8 + e + 1],
                    acc[:, mcol * D:(mcol + 1) * D],
                    op0=ALU.mult, op1=ALU.add))
        compute_end[u] = em.n["tensor"]

    em.barrier(("tensor", "vector", "scalar", "sync", "gpsimd"))
    for m in range(M):
        eng = "sync" if m % 2 == 0 else "gpsimd"
        em.dma(eng, lambda en, m=m: en.dma_start(
            out=x1_out[m * 128:(m + 1) * 128, :], in_=acc[:, m * D:(m + 1) * D]))
    em.wait("sync", "sync", em.n["sync"])
    em.wait("sync", "gdma", em.n["gdma"])



def build_launchC():
    E0 = 8
    HID = 4096
    NQ = 4
    nc = bass.Bass("TRN2", target_bir_lowering=False, detect_race_conditions=False)
    x = nc.dram_tensor("x", [T, D], F32, kind="ExternalInput").ap()
    ya = nc.dram_tensor("ya", [T, D], F32, kind="ExternalInput").ap()
    rw = nc.dram_tensor("rw", [D, 8], F32, kind="ExternalInput").ap()
    rb = nc.dram_tensor("rb", [1, 8], F32, kind="ExternalInput").ap()
    w1 = nc.dram_tensor("w1", [E0, D, HID], F32, kind="ExternalInput").ap()
    b1t = nc.dram_tensor("b1t", [E0, 128, HID // 128], F32, kind="ExternalInput").ap()
    w2 = nc.dram_tensor("w2", [E0, HID, D], F32, kind="ExternalInput").ap()
    b2 = nc.dram_tensor("b2", [E0, D], F32, kind="ExternalInput").ap()
    x1_out = nc.dram_tensor("x1_out", [T, D], F32, kind="ExternalOutput").ap()

    iden_in = nc.dram_tensor("iden", [128, 128], F32, kind="ExternalInput").ap()
    em = Em()
    with contextlib.ExitStack() as st:
        def sb(name, shape, dtype):
            return st.enter_context(nc.sbuf_tensor(name, shape, dtype))

        def ps(name, shape, dtype):
            return _ps_pad(nc, st, name, shape, dtype)

        emit_phaseC(nc, em, st, sb, ps, x, ya, rw, rb, w1, b1t, w2, b2, iden_in, x1_out)
        em.finalize(nc)
    return nc



PIPE_AV = 1
INTERLEAVE = True
IL_IDX = True


def emit_phaseD_uniform(nc, em, st, sb, ps, x1g, wqkv, wo, wqi, wki, wwi, iden,
                        cosf, sinf, camask, y1_d, kts, vds, qts_d, qits_d, wis_d):
    RK = 4096
    MK = 32
    HI = 4
    DI = 64
    NB = 27

    qtb = sb("qtb", [128, (H // 2) * T], BF16)
    qitb = sb("qitb", [128, (HI // 2) * T], BF16)
    kitb = sb("kitb", [128, RK], BF16)
    wis = sb("wis", [128, M * HI], F32)
    woTb = sb("woTb", [128, KC * D], BF16)
    idf = sb("idf", [128, 128], F32)
    idb = sb("idb", [128, 128], BF16)
    mini = sb("mini", [128, 16], F32)
    fint = sb("fint", [128, 2], I32)
    kw = sb("kw", [128, D], F32)
    aob = sb("aob", [128, D], BF16)
    aotT = sb("aotT", [128, KC * 128], BF16)
    yo = sb("yo", [128, D], F32)
    mask01 = sb("mask01", [128, 2 * 4096], BF16)
    catile = sb("catile", [128, 4096], BF16)
    qitc = sb("qitc", [128, 2 * 128], BF16)
    probsT = sb("probsT", [128, 3 * 512], BF16)
    mtb = sb("mtb", [128, 2 * 512], BF16)

    tp = ps("tp", [128, 1024], BF16)
    psc0 = ps("psc0", [128, 1024], F32)
    psc1 = ps("psc1", [128, 1024], F32)
    pav = ps("pav", [128, 1536], F32)
    pk = psc0
    pk2 = psc1

    bufs = dict(mini=mini, tp=tp, idb=idb,
                eps=mini[:, 6:7], xt_stride=128, xt_off=0)
    em.step("vector", lambda vv: vv.memset(mini[:], 0.0))
    em.step("vector", lambda vv: vv.memset(mini[:, 6:7], 1e-5))
    vid = em.dma("sync", lambda s: s.dma_start(out=idf[:], in_=iden[:, :]))
    gid = em.dma("gpsimd", lambda g: g.dma_start(out=idb[:], in_=iden[:, :]))
    em.wait("vector", "sync", vid)
    em.wait("vector", "gdma", gid)
    em.wait("tensor", "gdma", gid)

    with contextlib.ExitStack() as stw:
        wqT = stw.enter_context(nc.sbuf_tensor("wqT", [128, KC * D], BF16))
        wkT = stw.enter_context(nc.sbuf_tensor("wkT", [128, KC * D], BF16))
        wvT = stw.enter_context(nc.sbuf_tensor("wvT", [128, KC * D], BF16))
        wqiT = stw.enter_context(nc.sbuf_tensor("wqiT", [128, KC * HI * DI], BF16))
        wkiT = stw.enter_context(nc.sbuf_tensor("wkiT", [128, KC * DI], BF16))
        wwiT = stw.enter_context(nc.sbuf_tensor("wwiT", [128, KC * HI], BF16))

        with contextlib.ExitStack() as stw0:
            wtmp = stw0.enter_context(nc.sbuf_tensor("wtmp", [128, KC * D], BF16))

            def wT_simple(src_ap, rows, dst):
                rr = min(rows, 128)
                rcn = max(rows // 128, 1)
                em.wait("gpsimd", "vector", em.n["vector"])
                for rc in range(rcn):
                    em.dma("gpsimd", lambda g, rc=rc, rr=rr, src_ap=src_ap: g.dma_start(
                        out=wtmp[0:rr, rc * D:(rc + 1) * D],
                        in_=src_ap[rc * rr:(rc + 1) * rr, :]))
                gv = em.n["gdma"]
                em.wait("tensor", "gdma", gv)
                em.wait("vector", "gdma", gv)
                for ci in range(KC):
                    for rc in range(rcn):
                        tv = em.step("tensor", lambda tt, ci=ci, rc=rc, rr=rr: tt.transpose(
                            tp[0:128, 0:rr],
                            wtmp[0:rr, rc * D + ci * 128:rc * D + ci * 128 + 128],
                            idb[0:rr, 0:rr]))
                        em.wait("vector", "tensor", tv)
                        ev = em.step("vector", lambda vv, ci=ci, rc=rc, rr=rr, dst=dst, rows=rows: vv.tensor_copy(
                            out=dst[:, ci * rows + rc * rr:ci * rows + rc * rr + rr],
                            in_=tp[0:128, 0:rr]))
                        em.wait("tensor", "vector", ev)

            wT_simple(wqkv[0:D, :], D, wqT)
            wT_simple(wqkv[D:2 * D, :], D, wkT)
            wT_simple(wqkv[2 * D:3 * D, :], D, wvT)
            wT_simple(wo, D, woTb)
            wT_simple(wqi, HI * DI, wqiT)
            wT_simple(wki, DI, wkiT)
            wT_simple(wwi, HI, wwiT)
            em.barrier(("tensor", "vector", "scalar", "gpsimd", "sync"))

        xchunk = stw.enter_context(nc.sbuf_tensor("xchunk", [128, D], F32))
        sq = stw.enter_context(nc.sbuf_tensor("sq", [128, D], F32))
        xnb = stw.enter_context(nc.sbuf_tensor("xnb", [128, D], BF16))
        xnt = stw.enter_context(nc.sbuf_tensor("xnt", [128, KC * 128], BF16))
        miniA = stw.enter_context(nc.sbuf_tensor("miniA", [128, 8], F32))
        miniB = stw.enter_context(nc.sbuf_tensor("miniB", [128, 8], F32))
        swp = stw.enter_context(nc.sbuf_tensor("swp", [128, D], F32))
        kbbk = stw.enter_context(nc.sbuf_tensor("kbbk", [128, D], BF16))
        kbbq = stw.enter_context(nc.sbuf_tensor("kbbq", [128, D], BF16))
        kctc = stw.enter_context(nc.sbuf_tensor("kctc", [128, KC * 128], BF16))
        qctc = stw.enter_context(nc.sbuf_tensor("qctc", [128, KC * 128], BF16))
        kib = stw.enter_context(nc.sbuf_tensor("kib", [128, DI], BF16))
        qib = stw.enter_context(nc.sbuf_tensor("qib", [128, HI * DI], BF16))
        wib = stw.enter_context(nc.sbuf_tensor("wib", [128, 8], F32))
        ctile = stw.enter_context(nc.sbuf_tensor("ctile", [128, D], F32))
        stile = stw.enter_context(nc.sbuf_tensor("stile", [128, D], F32))
        em.step("vector", lambda vv: vv.memset(miniA[:, 6:7], 1e-5))
        em.step("vector", lambda vv: vv.memset(miniB[:, 6:7], 1e-5))
        bufsP = []
        for mx in (miniA, miniB):
            bb = dict(bufs)
            bb.update(sq=sq, xnb=xnb, xchunk=xchunk, xt=xnt, mini=mx,
                      eps=mx[:, 6:7], evac_eng="scalar")
            bufsP.append(bb)

        wT_colstride = {id(wqT): D, id(wkT): D, id(wvT): D, id(woTb): D,
                        id(wqiT): HI * DI, id(wkiT): DI, id(wwiT): HI}
        slots = [(psc0, 0), (psc0, 512), (psc1, 0), (psc1, 512)]
        slot_ev = [("vector", 0)] * 4
        sidx = [0]

        def proj2(wT, ncols, half, xt_ready=None):
            si = sidx[0] % 4
            sp, off = slots[si]
            sidx[0] += 1
            eng, cnt = slot_ev[si]
            if cnt:
                em.wait("tensor", eng, cnt)
            if xt_ready:
                em.wait("tensor", "scalar", xt_ready)
            stride = wT_colstride[id(wT)]
            mv = None
            for kc in range(KC):
                mv = em.step("tensor", lambda tt, kc=kc, half=half, sp=sp, off=off,
                             wT=wT, ncols=ncols, stride=stride: tt.matmul(
                    sp[:, off:off + ncols],
                    xnt[:, kc * 128:(kc + 1) * 128],
                    wT[:, kc * stride + half * 512:kc * stride + half * 512 + ncols],
                    start=(kc == 0), stop=(kc == KC - 1)))
            return si, sp, off, mv

        def rope_evac(halves, dst):
            for half, (si, sp, off, mv) in enumerate(halves):
                em.wait("vector", "tensor", mv)
                ev = em.step("vector", lambda vv, half=half, sp=sp, off=off: vv.tensor_copy(
                    out=kw[:, half * 512:half * 512 + 512], in_=sp[:, off:off + 512]))
                slot_ev[si] = ("vector", ev)
            em.step("vector", lambda vv: vv.tensor_copy(
                out=swp.ap().rearrange("q (a two) -> q a two", two=2)[:, :, 0],
                in_=kw.ap().rearrange("q (a two) -> q a two", two=2)[:, :, 1]))
            em.step("vector", lambda vv: vv.tensor_copy(
                out=swp.ap().rearrange("q (a two) -> q a two", two=2)[:, :, 1],
                in_=kw.ap().rearrange("q (a two) -> q a two", two=2)[:, :, 0]))
            em.step("vector", lambda vv: vv.tensor_mul(swp[:], swp[:], stile[:]))
            em.step("vector", lambda vv: vv.tensor_mul(kw[:], kw[:], ctile[:]))
            return em.step("vector", lambda vv, dst=dst: vv.tensor_add(dst[:], kw[:], swp[:]))

        def pair_transposes(src, n2, dsts, eng):
            tv = None
            for j in range(n2):
                tv = em.step("tensor", lambda tt, j=j, src=src: tt.transpose(
                    tp[:, j * 128:(j + 1) * 128],
                    src[:, j * 128:(j + 1) * 128], idb[:]))
            em.wait(eng, "tensor", tv)
            ev = None
            for j in range(n2):
                dst = dsts(j)
                if eng == "scalar":
                    ev = em.step("scalar", lambda sc, j=j, dst=dst: sc.activation(
                        dst, tp[:, j * 128:(j + 1) * 128],
                        AF.Copy, bias=0.0, scale=1.0))
                else:
                    ev = em.step("vector", lambda vv, j=j, dst=dst: vv.tensor_copy(
                        out=dst, in_=tp[:, j * 128:(j + 1) * 128]))
            return tv, ev

        trk = dict(rope_v=0, kts=0, qts=0, vds=0, wis=0, qits=0,
                   ktr=0, qtr=0, ln_s=0, ln_v=0, ln_t=0,
                   tp_s=0, tp_v=0)

        for c in range(MK):
            mm = (c % 2) * 16 + (c // 2)
            bb = bufsP[c % 2]
            bb["war_scalar"] = trk["ln_s"]
            bb["war_vector"] = trk["ln_v"]
            bb["xnb_war"] = trk["ln_t"]
            if trk["tp_s"]:
                em.wait("tensor", "scalar", trk["tp_s"])
            if trk["tp_v"]:
                em.wait("tensor", "vector", trk["tp_v"])
            emit_ln_xnT(em, bb, mm, x_src_ap=x1g)
            trk["ln_s"] = em.n["scalar"]
            trk["ln_v"] = em.n["vector"]
            trk["ln_t"] = em.n["tensor"]
            lnxt = em.n["scalar"]
            if trk["rope_v"]:
                em.wait("sync", "vector", trk["rope_v"])
            em.dma("sync", lambda s, c=c: s.dma_start(
                out=ctile[:], in_=cosf[c * 128:(c + 1) * 128, :]))
            sd = em.dma("sync", lambda s, c=c: s.dma_start(
                out=stile[:], in_=sinf[c * 128:(c + 1) * 128, :]))

            kh = [proj2(wkT, 512, 0, lnxt), proj2(wkT, 512, 1)]
            em.wait("vector", "sync", sd)
            if trk["ktr"]:
                em.wait("vector", "tensor", trk["ktr"])
            rvk = rope_evac(kh, kbbk)
            vh = [proj2(wvT, 512, 0), proj2(wvT, 512, 1)]
            svl = None
            for half, (si, sp, off, mv) in enumerate(vh):
                em.wait("scalar", "tensor", mv)
                if half == 0 and trk["vds"]:
                    em.wait("scalar", "sync", trk["vds"])
                svl = em.step("scalar", lambda sc, half=half, sp=sp, off=off: sc.activation(
                    kbbq[:, half * 512:half * 512 + 512], sp[:, off:off + 512],
                    AF.Copy, bias=0.0, scale=1.0))
                slot_ev[si] = ("scalar", svl)
            em.wait("sync", "scalar", svl)
            trk["vds"] = em.dma("sync", lambda s, c=c: s.dma_start(
                out=vds[c * 128:(c + 1) * 128, :], in_=kbbq[:]))

            em.wait("tensor", "vector", rvk)
            if trk["kts"]:
                em.wait("scalar", "sync", trk["kts"])
                em.wait("vector", "sync", trk["kts"])
            tv, ev_ = pair_transposes(
                kbbk, KC, lambda j: kctc[:, j * 128:(j + 1) * 128], "scalar")
            trk["ktr"] = tv
            trk["tp_s"] = ev_
            em.wait("sync", "scalar", ev_)
            for kc in range(KC):
                em.dma("sync", lambda s, kc=kc, c=c: s.dma_start(
                    out=kts[:, kc * RK + c * 128:kc * RK + (c + 1) * 128],
                    in_=kctc[:, kc * 128:(kc + 1) * 128]))
            trk["kts"] = em.n["sync"]

            si, sp, off, mv = proj2(wkiT, DI, 0)
            em.wait("scalar", "tensor", mv)
            skib = em.step("scalar", lambda sc, sp=sp, off=off: sc.activation(
                kib[:, 0:DI], sp[:, off:off + DI], AF.Copy, bias=0.0, scale=1.0))
            slot_ev[si] = ("scalar", skib)

            qh = [proj2(wqT, 512, 0), proj2(wqT, 512, 1)]
            em.wait("vector", "sync", trk["vds"])
            if trk["qtr"]:
                em.wait("vector", "tensor", trk["qtr"])
            rvq = rope_evac(qh, kbbq)
            trk["rope_v"] = em.n["vector"]

            em.wait("tensor", "scalar", skib)
            em.wait("tensor", "scalar", trk["tp_s"])
            em.wait("tensor", "vector", trk["tp_v"])
            tvki = em.step("tensor", lambda tt: tt.transpose(
                tp[0:64, 0:128], kib[:, 0:DI], idb[:]))
            em.wait("vector", "tensor", tvki)
            em.step("vector", lambda vv, c=c: vv.tensor_copy(
                out=kitb[0:64, c * 128:(c + 1) * 128], in_=tp[0:64, 0:128]))
            kv_ = em.step("vector", lambda vv, c=c: vv.tensor_copy(
                out=kitb[64:128, c * 128:(c + 1) * 128], in_=tp[0:64, 0:128]))
            trk["tp_v"] = kv_

            em.wait("tensor", "vector", rvq)
            em.wait("tensor", "scalar", trk["tp_s"])
            em.wait("tensor", "vector", trk["tp_v"])
            if trk["qts"]:
                em.wait("scalar", "sync", trk["qts"])
                em.wait("vector", "sync", trk["qts"])
            tv, ev_ = pair_transposes(
                kbbq, KC, lambda j: qctc[:, j * 128:(j + 1) * 128], "vector")
            trk["qtr"] = tv
            trk["tp_v"] = ev_
            em.wait("sync", "vector", ev_)
            for kc in range(KC):
                em.dma("sync", lambda s, kc=kc, c=c: s.dma_start(
                    out=qts_d[:, kc * RK + c * 128:kc * RK + (c + 1) * 128],
                    in_=qctc[:, kc * 128:(kc + 1) * 128]))
            trk["qts"] = em.n["sync"]

            si, sp, off, mv = proj2(wqiT, HI * DI, 0)
            em.wait("scalar", "tensor", mv)
            sqib = em.step("scalar", lambda sc, sp=sp, off=off: sc.activation(
                qib[:, 0:HI * DI], sp[:, off:off + HI * DI],
                AF.Copy, bias=0.0, scale=1.0))
            slot_ev[si] = ("scalar", sqib)
            em.wait("tensor", "scalar", sqib)
            em.wait("tensor", "scalar", trk["tp_s"])
            em.wait("tensor", "vector", trk["tp_v"])
            if trk["qits"]:
                em.wait("scalar", "sync", trk["qits"])
                em.wait("vector", "sync", trk["qits"])
            tv, ev_ = pair_transposes(
                qib, HI // 2, lambda j: qitc[:, j * 128:(j + 1) * 128], "vector")
            trk["tp_v"] = ev_
            em.wait("sync", "vector", ev_)
            for blk in range(HI // 2):
                em.dma("sync", lambda s, blk=blk, c=c: s.dma_start(
                    out=qits_d[:, blk * RK + c * 128:blk * RK + (c + 1) * 128],
                    in_=qitc[:, blk * 128:(blk + 1) * 128]))
            trk["qits"] = em.n["sync"]

            si, sp, off, mv = proj2(wwiT, HI, 0)
            em.wait("scalar", "tensor", mv)
            if trk["wis"]:
                em.wait("scalar", "sync", trk["wis"])
            swi = em.step("scalar", lambda sc, sp=sp, off=off: sc.activation(
                wib[:, 0:HI], sp[:, off:off + HI], AF.Copy, bias=0.0, scale=1.0))
            slot_ev[si] = ("scalar", swi)
            em.wait("sync", "scalar", swi)
            trk["wis"] = em.dma("sync", lambda s, c=c: s.dma_start(
                out=wis_d[:, c * HI:(c + 1) * HI], in_=wib[:, 0:HI]))

        em.barrier(("tensor", "vector", "scalar", "sync"))

    def _gather_switch(eng, sems):
        for arm in nc.switch_core_id(engines=[nc.sync], n=128):
            pcore = arm.logical % 2
            for rcg in range(M):
                ab = 2 * rcg + pcore
                eng.dma_start(
                    out=qtb.ap().rearrange("q (kc n) -> q kc n", kc=KC)[:, :, rcg * 128:(rcg + 1) * 128],
                    in_=qts_d.rearrange("q (kc n) -> q kc n", kc=KC)[:, :, ab * 128:(ab + 1) * 128],
                ).then_inc(sems["sync"], 16)
                eng.dma_start(
                    out=qitb.ap().rearrange("q (b n) -> q b n", b=HI // 2)[:, :, rcg * 128:(rcg + 1) * 128],
                    in_=qits_d.rearrange("q (b n) -> q b n", b=HI // 2)[:, :, ab * 128:(ab + 1) * 128],
                ).then_inc(sems["sync"], 16)
            eng.dma_start(
                out=wis.ap().rearrange("q (a b) -> q a b", b=HI)[:, :, :],
                in_=wis_d.rearrange("q (a two b) -> q a two b", two=2, b=HI)[:, :, pcore, :],
            ).then_inc(sems["sync"], 16)
    em.wait("sync", "sync", em.n["sync"])
    em.raw("sync", _gather_switch, sem="sync", inc=16 * (2 * M + 1))
    em.barrier(("tensor", "vector", "scalar", "sync"))


    lo = mini[:, 0:1]; hi_ = mini[:, 1:2]; mid = mini[:, 2:3]
    negmid = mini[:, 3:4]; Fst = mini[:, 4:5]
    c_hi = mini[:, 7:8]; c_gt = mini[:, 8:9]; needf = mini[:, 9:10]
    cm = mini[:, 10:11]; mnew = mini[:, 11:12]; sco = mini[:, 12:13]
    bsum = mini[:, 13:14]; negmn = mini[:, 14:15]; rec = mini[:, 15:16]

    idx = sb("idx", [128, 4096], F32)
    giB = sb("giB", [128, 4096], I32)
    giC = sb("giC", [128, 4096], I32)
    giE = sb("giE", [128, 4096], I32)
    ktile = sb("ktile", [128, 2 * KC * 512], BF16)
    vtile = sb("vtile", [128, 2 * 4 * 1040], BF16)
    accS = sb("accS", [128, H * 65], F32)
    em.step("vector", lambda vv: vv.memset(vtile[:], 1.0))
    em.wait("gpsimd", "vector", em.n["vector"])

    def Wof(rc):
        return 512 if rc < 2 else min(((2 * rc + 2) * 128 + 511) // 512 * 512, RK)

    fl = dict(ktb=[0, 0], vtb=[0, 0], exp=[0, 0], av_evac=[0] * H,
              tp_ev=0, aot_ev=0, madd=[0, 0], wo_ev=0, y1_dma=0,
              mskT_rd=[0, 0])
    sel = dict(relu2=[0, 0], stt2=[0, 0], cam=0, sign=0, mk=[0, 0])

    def gen_selection(rc):
        W = Wof(rc)
        NSC = W // 512
        mb0 = (rc % 2) * 4096
        jj = 0
        for sc in range(NSC):
            for hi in range(HI):
                pb = (hi % 2) * 64
                b0 = (jj % 2) * 512
                if sel["relu2"][jj % 2]:
                    em.wait("tensor", "scalar", sel["relu2"][jj % 2])
                mv = em.step("tensor", lambda tt, hi=hi, pb=pb, rc=rc, sc=sc, b0=b0: tt.matmul(
                    psc1[:, b0:b0 + 512],
                    qitb[pb:pb + 64, (hi // 2) * T + rc * 128:(hi // 2) * T + rc * 128 + 128],
                    kitb[pb:pb + 64, sc * 512:(sc + 1) * 512],
                    start=True, stop=True))
                em.wait("scalar", "tensor", mv)
                if sel["stt2"][jj % 2]:
                    em.wait("scalar", "vector", sel["stt2"][jj % 2])
                sel["relu2"][jj % 2] = em.step("scalar", lambda sc_, b0=b0: sc_.activation(
                    kw[:, b0:b0 + 512], psc1[:, b0:b0 + 512], AF.Relu, bias=0.0, scale=1.0))
                em.wait("vector", "scalar", sel["relu2"][jj % 2])
                if hi == 0:
                    sel["stt2"][jj % 2] = em.step("vector", lambda vv, rc=rc, sc=sc, b0=b0: vv.tensor_scalar(
                        idx[:, sc * 512:(sc + 1) * 512], kw[:, b0:b0 + 512],
                        wis[:, rc * HI:rc * HI + 1], None, op0=ALU.mult))
                else:
                    sel["stt2"][jj % 2] = em.step("vector", lambda vv, rc=rc, sc=sc, hi=hi, b0=b0: vv.scalar_tensor_tensor(
                        idx[:, sc * 512:(sc + 1) * 512], kw[:, b0:b0 + 512],
                        wis[:, rc * HI + hi:rc * HI + hi + 1],
                        idx[:, sc * 512:(sc + 1) * 512],
                        op0=ALU.mult, op1=ALU.add))
                jj += 1
                yield "idx"
        if sel["cam"]:
            em.wait("sync", "vector", sel["cam"])
        if sel["sign"]:
            em.wait("sync", "scalar", sel["sign"])
        cav = em.dma("sync", lambda s, rc=rc, W=W: s.dma_start(
            out=catile[:, 0:W], in_=camask[rc * 128:(rc + 1) * 128, 0:W]))
        em.wait("vector", "sync", cav)
        sel["cam"] = em.step("vector", lambda vv, W=W: vv.tensor_tensor(
            idx[:, 0:W], idx[:, 0:W], catile[:, 0:W], op=ALU.add))
        Wfloor = max((2 * rc * 128 + 1) // 512 * 512, 512)
        em.step("vector", lambda vv, Wfloor=Wfloor: vv.tensor_reduce(
            lo, idx[:, 0:Wfloor], AX.X, ALU.min))
        em.step("vector", lambda vv: vv.tensor_scalar(lo, lo, 1.0, None, op0=ALU.subtract))
        em.step("vector", lambda vv, W=W: vv.tensor_reduce(
            hi_, idx[:, 0:W], AX.X, ALU.max))
        em.step("vector", lambda vv: vv.tensor_scalar(hi_, hi_, 1.0, None, op0=ALU.add))
        yield
        thr = float(1024 - W)
        for it in range(NB):
            vv1 = em.step("vector", lambda vv: vv.tensor_add(mid, lo, hi_))
            vv1 = em.step("vector", lambda vv: vv.tensor_scalar(mid, mid, 0.5, None, op0=ALU.mult))
            vv1 = em.step("vector", lambda vv: vv.tensor_scalar(negmid, mid, -1.0, None, op0=ALU.mult))
            em.wait("scalar", "vector", vv1)
            sv = em.step("scalar", lambda sc_, W=W: sc_.activation(
                catile[:, 0:W], idx[:, 0:W], AF.Sign, bias=negmid, scale=1.0,
                accum_out=Fst))
            sel["sign"] = sv
            em.wait("vector", "scalar", sv)
            em.step("vector", lambda vv, thr=thr: vv.tensor_scalar(
                fint[:, 0:1], Fst, thr, 1, op0=ALU.is_ge, op1=ALU.mult))
            em.step("vector", lambda vv, thr=thr: vv.tensor_scalar(
                fint[:, 1:2], Fst, thr, 1, op0=ALU.is_lt, op1=ALU.mult))
            em.step("vector", lambda vv: vv.select(lo, fint[:, 0:1], mid, lo))
            em.step("vector", lambda vv: vv.select(hi_, fint[:, 1:2], mid, hi_))
            yield
        em.step("vector", lambda vv, W=W: vv.tensor_scalar(
            giB[:, 0:W], idx[:, 0:W], lo, 1, op0=ALU.is_ge, op1=ALU.mult))
        em.step("vector", lambda vv, W=W: vv.tensor_scalar(
            giC[:, 0:W], idx[:, 0:W], hi_, 1, op0=ALU.is_ge, op1=ALU.mult))
        em.step("vector", lambda vv, W=W: vv.tensor_scalar(
            giE[:, 0:W], idx[:, 0:W], hi_, 1, op0=ALU.is_gt, op1=ALU.mult))
        yield
        em.step("vector", lambda vv, W=W: vv.tensor_tensor_scan(
            idx[:, 0:W], giC[:, 0:W], giC[:, 0:W], 0, op0=ALU.add, op1=ALU.bypass))
        em.step("vector", lambda vv, W=W: vv.tensor_copy(out=c_hi, in_=idx[:, W - 1:W]))
        em.step("vector", lambda vv, W=W: vv.tensor_tensor_scan(
            idx[:, 0:W], giE[:, 0:W], giE[:, 0:W], 0, op0=ALU.add, op1=ALU.bypass))
        em.step("vector", lambda vv, W=W: vv.tensor_copy(out=c_gt, in_=idx[:, W - 1:W]))
        yield
        em.step("vector", lambda vv: vv.tensor_scalar(
            fint[:, 0:1], c_hi, 512.0, 1, op0=ALU.is_ge, op1=ALU.mult))
        em.step("vector", lambda vv, W=W: vv.select(
            giB[:, 0:W], fint[:, 0:1].to_broadcast([128, W]), giC[:, 0:W], giB[:, 0:W]))
        em.step("vector", lambda vv, W=W: vv.select(
            giC[:, 0:W], fint[:, 0:1].to_broadcast([128, W]), giE[:, 0:W], giC[:, 0:W]))
        em.step("vector", lambda vv, W=W: vv.tensor_sub(giB[:, 0:W], giB[:, 0:W], giC[:, 0:W]))
        yield
        em.step("vector", lambda vv, W=W: vv.tensor_tensor_scan(
            idx[:, 0:W], giB[:, 0:W], giB[:, 0:W], 0, op0=ALU.add, op1=ALU.bypass))
        em.step("vector", lambda vv: vv.tensor_scalar(
            fint[:, 1:2], c_hi, 512.0, 1, op0=ALU.is_lt, op1=ALU.mult))
        em.step("vector", lambda vv: vv.select(c_gt, fint[:, 1:2], c_hi, c_gt))
        em.step("vector", lambda vv: vv.tensor_scalar(
            needf, c_gt, 512.0, -1.0, op0=ALU.subtract, op1=ALU.mult))
        yield
        em.step("vector", lambda vv, W=W: vv.tensor_scalar(
            giE[:, 0:W], idx[:, 0:W], needf, 1, op0=ALU.is_le, op1=ALU.mult))
        em.step("vector", lambda vv, W=W: vv.tensor_mul(giB[:, 0:W], giB[:, 0:W], giE[:, 0:W]))
        em.step("vector", lambda vv, W=W: vv.tensor_add(giC[:, 0:W], giC[:, 0:W], giB[:, 0:W]))
        yield
        if fl["mskT_rd"][rc % 2]:
            em.wait("vector", "tensor", fl["mskT_rd"][rc % 2])
        sel["mk"][rc % 2] = em.step("vector", lambda vv, W=W, mb0=mb0: vv.tensor_scalar(
            mask01[:, mb0:mb0 + W], giC[:, 0:W], 1.0, 1e30,
            op0=ALU.subtract, op1=ALU.mult))

    def gen_flash(rc):
        W = Wof(rc)
        special = rc < 2
        nsb = W // 512
        em.step("vector", lambda vv: vv.memset(accS[:], 0.0))
        v_lds = {}

        def block_prologue(sb_):
            tb = sb_ % 2
            kb0 = tb * KC * 512
            vb0 = tb * 4 * 1040
            if fl["ktb"][tb]:
                em.wait("sync", "tensor", fl["ktb"][tb])
            for kc in range(KC):
                em.dma("sync", lambda s, kc=kc, sb_=sb_, kb0=kb0: s.dma_start(
                    out=ktile[:, kb0 + kc * 512:kb0 + (kc + 1) * 512],
                    in_=kts[:, kc * RK + sb_ * 512:kc * RK + (sb_ + 1) * 512]))
            k_ld = em.n["sync"]
            if fl["vtb"][tb]:
                em.wait("gpsimd", "tensor", fl["vtb"][tb])
            for s2 in range(4):
                em.dma("gpsimd", lambda g, s2=s2, sb_=sb_, vb0=vb0: g.dma_start(
                    out=vtile.ap()[:, vb0 + s2 * 1040:vb0 + (s2 + 1) * 1040].rearrange(
                        "p (h c) -> p h c", h=H)[:, :, 0:HD],
                    in_=vds[sb_ * 512 + s2 * 128:sb_ * 512 + (s2 + 1) * 128, :].rearrange(
                        "p (h c) -> p h c", h=H),
                    allow_slow_non_contiguous=True))
            v_lds[sb_] = em.n["gdma"]
            if not special:
                if sb_ == 0:
                    em.wait("tensor", "vector", sel["mk"][rc % 2])
                    if fl["aot_ev"]:
                        em.wait("tensor", "vector", fl["aot_ev"])
                if fl["tp_ev"]:
                    em.wait("tensor", "scalar", fl["tp_ev"])
                if fl["madd"][tb]:
                    em.wait("scalar", "tensor", fl["madd"][tb])
                tv = None
                for c in range(4):
                    tv = em.step("tensor", lambda tt, c=c, sb_=sb_, rc=rc: tt.transpose(
                        tp[:, c * 128:(c + 1) * 128],
                        mask01[:, (rc % 2) * 4096 + sb_ * 512 + c * 128:
                               (rc % 2) * 4096 + sb_ * 512 + c * 128 + 128],
                        idb[:]))
                fl["mskT_rd"][rc % 2] = tv
                em.wait("scalar", "tensor", tv)
                fl["tp_ev"] = em.step("scalar", lambda sc_, tb=tb: sc_.activation(
                    mtb[:, tb * 512:tb * 512 + 512], tp[:, 0:512],
                    AF.Copy, bias=0.0, scale=1.0))
            em.wait("tensor", "sync", k_ld)

        def emit_scores(u, sb_, h):
            tb = sb_ % 2
            kb0 = tb * KC * 512
            pb = (h % 2) * 64
            p0 = (u % 2) * 512
            if fl["exp"][u % 2]:
                em.wait("tensor", "scalar", fl["exp"][u % 2])
            if u < 2 and fl["wo_ev"]:
                em.wait("tensor", "vector", fl["wo_ev"])
            if not special and h == 0:
                em.wait("tensor", "scalar", fl["tp_ev"])
            mv = None
            for c in range(4):
                mv = em.step("tensor", lambda tt, h=h, pb=pb, rc=rc, c=c, p0=p0, kb0=kb0, special=special: tt.matmul(
                    psc0[:, p0 + c * 128:p0 + c * 128 + 128],
                    ktile[pb:pb + 64, kb0 + (h // 2) * 512 + c * 128:kb0 + (h // 2) * 512 + c * 128 + 128],
                    qtb[pb:pb + 64, (h // 2) * T + rc * 128:(h // 2) * T + rc * 128 + 128],
                    start=True, stop=special))
                if not special:
                    mv = em.step("tensor", lambda tt, c=c, tb=tb, p0=p0: tt.matmul(
                        psc0[:, p0 + c * 128:p0 + c * 128 + 128],
                        idb[:],
                        mtb[:, tb * 512 + c * 128:tb * 512 + c * 128 + 128],
                        start=False, stop=True))
            if h == H - 1:
                fl["ktb"][tb] = mv
                if not special:
                    fl["madd"][tb] = mv
            em.wait("scalar", "tensor", mv)
            fl["exp"][u % 2] = em.step("scalar", lambda sc_, u=u, p0=p0: sc_.activation(
                probsT[:, (u % 3) * 512:(u % 3) * 512 + 512], psc0[:, p0:p0 + 512],
                AF.Exp, bias=0.0, scale=0.125))

        def emit_av(u, sb_, h):
            tb = sb_ % 2
            vb0 = tb * 4 * 1040
            em.wait("tensor", "scalar", fl["exp"][u % 2])
            if h == 0:
                em.wait("tensor", "gdma", v_lds[sb_])
            reg = (h // 7) * 512 + (h % 7) * 65
            if fl["av_evac"][h]:
                em.wait("tensor", "vector", fl["av_evac"][h])
            mv = None
            for c in range(4):
                mv = em.step("tensor", lambda tt, c=c, h=h, u=u, reg=reg, vb0=vb0: tt.matmul(
                    pav[:, reg:reg + 65],
                    probsT[:, (u % 3) * 512 + c * 128:(u % 3) * 512 + c * 128 + 128],
                    vtile[:, vb0 + c * 1040 + h * 65:vb0 + c * 1040 + h * 65 + 65],
                    start=(c == 0), stop=(c == 3)))
            if h == H - 1:
                fl["vtb"][tb] = mv
            em.wait("vector", "tensor", mv)
            fl["av_evac"][h] = em.step("vector", lambda vv, h=h, reg=reg: vv.tensor_add(
                accS[:, h * 65:(h + 1) * 65],
                accS[:, h * 65:(h + 1) * 65], pav[:, reg:reg + 65]))

        units = [(sb_, h) for sb_ in range(nsb) for h in range(H)]
        pend = []
        for u, (sb_, h) in enumerate(units):
            if h == 0:
                block_prologue(sb_)
            emit_scores(u, sb_, h)
            pend.append((u, sb_, h))
            if len(pend) > PIPE_AV:
                emit_av(*pend.pop(0))
            if u % 2 == 1 and not special:
                yield
        while pend:
            emit_av(*pend.pop(0))
        for h in range(H):
            em.step("vector", lambda vv, h=h: vv.reciprocal(rec, accS[:, h * 65 + 64:h * 65 + 65]))
            em.step("vector", lambda vv, h=h: vv.tensor_scalar(
                aob[:, h * HD:(h + 1) * HD], accS[:, h * 65:h * 65 + 64],
                rec, None, op0=ALU.mult))
        av = em.n["vector"]
        em.wait("tensor", "vector", av)
        if fl["tp_ev"]:
            em.wait("tensor", "scalar", fl["tp_ev"])
        for g in range(2):
            tv = None
            for j in range(4):
                jj = g * 4 + j
                tv = em.step("tensor", lambda tt, jj=jj, j=j: tt.transpose(
                    tp[:, j * 128:(j + 1) * 128],
                    aob[:, jj * 128:(jj + 1) * 128], idb[:]))
            em.wait("vector", "tensor", tv)
            ev = None
            for j in range(4):
                jj = g * 4 + j
                ev = em.step("vector", lambda vv, jj=jj, j=j: vv.tensor_copy(
                    out=aotT[:, jj * 128:(jj + 1) * 128],
                    in_=tp[:, j * 128:(j + 1) * 128]))
            em.wait("tensor", "vector", ev)
        fl["aot_ev"] = em.n["vector"]
        yield
        em.wait("tensor", "scalar", max(fl["exp"]))
        if fl["y1_dma"]:
            em.wait("vector", "sync", fl["y1_dma"])
        for half in range(2):
            mv = None
            for kc in range(KC):
                mv = em.step("tensor", lambda tt, kc=kc, half=half: tt.matmul(
                    psc0[:, half * 512:half * 512 + 512],
                    aotT[:, kc * 128:(kc + 1) * 128],
                    woTb[:, kc * D + half * 512:kc * D + half * 512 + 512],
                    start=(kc == 0), stop=(kc == KC - 1)))
            em.wait("vector", "tensor", mv)
            em.step("vector", lambda vv, half=half: vv.tensor_copy(
                out=yo[:, half * 512:half * 512 + 512],
                in_=psc0[:, half * 512:half * 512 + 512]))
        fl["wo_ev"] = em.n["vector"]
        em.wait("sync", "vector", fl["wo_ev"])
        fl["y1_dma"] = em.dma("sync", lambda s, rc=rc: s.dma_start(
            out=y1_d[rc * 128:(rc + 1) * 128, :], in_=yo[:]))

    for i in range(M):
        fgen = gen_flash(i)
        sgen = gen_selection(i + 1) if 2 <= i + 1 < M else None
        if INTERLEAVE:
            done_f = done_s = sgen is None
            done_f = False
            while not (done_f and done_s):
                if not done_s:
                    try:
                        tag = next(sgen)
                        if tag == "idx" and not IL_IDX:
                            while tag == "idx":
                                tag = next(sgen)
                    except StopIteration:
                        done_s = True
                for _ in range(3):
                    if not done_f:
                        try:
                            next(fgen)
                        except StopIteration:
                            done_f = True
        else:
            for _ in fgen:
                pass
            if sgen is not None:
                for _ in sgen:
                    pass
    em.wait("sync", "sync", em.n["sync"])
    em.wait("sync", "gdma", em.n["gdma"])



def build_launchD(p):
    RK = 2048 if p == 0 else 4096
    MK = RK // 128
    BASE = p * 2048
    HI = 4
    DI = 64
    NB = 27

    nc = bass.Bass("TRN2", target_bir_lowering=False, detect_race_conditions=False)
    x1o = nc.dram_tensor("x1o", [T, D], F32, kind="ExternalInput").ap()
    x1s = nc.dram_tensor("x1s", [T, D], F32, kind="ExternalInput").ap()
    wqkv = nc.dram_tensor("wqkv", [3 * D, D], F32, kind="ExternalInput").ap()
    wo = nc.dram_tensor("wo", [D, D], F32, kind="ExternalInput").ap()
    wqi = nc.dram_tensor("wqi", [HI * DI, D], F32, kind="ExternalInput").ap()
    wki = nc.dram_tensor("wki", [DI, D], F32, kind="ExternalInput").ap()
    wwi = nc.dram_tensor("wwi", [HI, D], F32, kind="ExternalInput").ap()
    iden = nc.dram_tensor("iden", [128, 128], F32, kind="ExternalInput").ap()
    cosf = nc.dram_tensor("cosf", [4096, D], F32, kind="ExternalInput").ap()
    sinf = nc.dram_tensor("sinf", [4096, D], F32, kind="ExternalInput").ap()
    tri = nc.dram_tensor("tri", [128, 128], F32, kind="ExternalInput").ap()
    y_out = nc.dram_tensor("y_out", [T, D], F32, kind="ExternalOutput").ap()
    kts = nc.dram_tensor("kts", [128, KC * RK], BF16, kind="Internal").ap()
    vds = nc.dram_tensor("vds", [RK, D], BF16, kind="Internal").ap()

    em = Em()
    with contextlib.ExitStack() as st:
        def sb(name, shape, dtype):
            return st.enter_context(nc.sbuf_tensor(name, shape, dtype))

        def ps(name, shape, dtype):
            return _ps_pad(nc, st, name, shape, dtype)

        qtb = sb("qtb", [128, (H // 2) * T], BF16)
        qitb = sb("qitb", [128, (HI // 2) * T], BF16)
        kitb = sb("kitb", [128, RK], BF16)
        wis = sb("wis", [128, M * HI], F32)
        woTb = sb("woTb", [128, KC * D], BF16)
        idf = sb("idf", [128, 128], F32)
        idb = sb("idb", [128, 128], BF16)
        tris = sb("tris", [128, 128], F32)
        mini = sb("mini", [128, 16], F32)
        fint = sb("fint", [128, 2], I32)
        kw = sb("kw", [128, D], F32)
        macc = sb("macc", [128, H], F32)
        lacc = sb("lacc", [128, H], F32)
        acc = sb("acc", [128, H * HD], F32)
        aob = sb("aob", [128, D], BF16)
        aotT = sb("aotT", [128, KC * 128], BF16)
        yo = sb("yo", [128, D], F32)
        mask01 = sb("mask01", [128, 4096], BF16)
        probs = sb("probs", [128, 1024], BF16)
        ptile = sb("ptile", [128, 512], BF16)

        tp = ps("tp", [128, 1024], BF16)
        pk = ps("pk", [128, 512], F32)
        pk2 = ps("pk2", [128, 512], F32)
        psc = ps("psc", [128, 1024], F32)
        pav = ps("pav", [128, 512], F32)

        bufs = dict(mini=mini, tp=tp, idb=idb,
                    eps=mini[:, 6:7], xt_stride=128, xt_off=0)
        em.step("vector", lambda vv: vv.memset(mini[:], 0.0))
        em.step("vector", lambda vv: vv.memset(mini[:, 6:7], 1e-5))
        vid = em.dma("sync", lambda s: s.dma_start(out=idf[:], in_=iden[:, :]))
        vtr = em.dma("sync", lambda s: s.dma_start(out=tris[:], in_=tri[:, :]))
        gid = em.dma("gpsimd", lambda g: g.dma_start(out=idb[:], in_=iden[:, :]))
        em.wait("vector", "sync", vtr)
        em.wait("vector", "gdma", gid)
        em.wait("tensor", "gdma", gid)

        with contextlib.ExitStack() as stw:
            wqT = stw.enter_context(nc.sbuf_tensor("wqT", [128, KC * D], BF16))
            wkT = stw.enter_context(nc.sbuf_tensor("wkT", [128, KC * D], BF16))
            wvT = stw.enter_context(nc.sbuf_tensor("wvT", [128, KC * D], BF16))
            wqiT = stw.enter_context(nc.sbuf_tensor("wqiT", [128, KC * HI * DI], BF16))
            wkiT = stw.enter_context(nc.sbuf_tensor("wkiT", [128, KC * DI], BF16))
            wwiT = stw.enter_context(nc.sbuf_tensor("wwiT", [128, KC * HI], BF16))
            wtmp = stw.enter_context(nc.sbuf_tensor("wtmp", [128, KC * D], BF16))
            xchunk = stw.enter_context(nc.sbuf_tensor("xchunk", [128, D], F32))
            sq = stw.enter_context(nc.sbuf_tensor("sq", [128, D], F32))
            xnb = stw.enter_context(nc.sbuf_tensor("xnb", [128, D], BF16))
            xnt = stw.enter_context(nc.sbuf_tensor("xnt", [128, KC * 128], BF16))
            kw2 = stw.enter_context(nc.sbuf_tensor("kw2", [128, D], F32))
            swp = stw.enter_context(nc.sbuf_tensor("swp", [128, D], F32))
            kbb = stw.enter_context(nc.sbuf_tensor("kbb", [128, D], BF16))
            ktc = stw.enter_context(nc.sbuf_tensor("ktc", [128, KC * 128], BF16))
            ctile = stw.enter_context(nc.sbuf_tensor("ctile", [128, D], F32))
            stile = stw.enter_context(nc.sbuf_tensor("stile", [128, D], F32))
            bufs.update(sq=sq, xnb=xnb, xchunk=xchunk, xt=xnt)

            def wT_simple(src_ap, rows, dst):
                rr = min(rows, 128)
                rcn = max(rows // 128, 1)
                em.wait("gpsimd", "vector", em.n["vector"])
                for rc in range(rcn):
                    em.dma("gpsimd", lambda g, rc=rc, rr=rr: g.dma_start(
                        out=wtmp[0:rr, rc * D:(rc + 1) * D],
                        in_=src_ap[rc * rr:(rc + 1) * rr, :]))
                gv = em.n["gdma"]
                em.wait("tensor", "gdma", gv)
                em.wait("vector", "gdma", gv)
                for ci in range(KC):
                    for rc in range(rcn):
                        tv = em.step("tensor", lambda tt, ci=ci, rc=rc, rr=rr: tt.transpose(
                            tp[0:128, 0:rr],
                            wtmp[0:rr, rc * D + ci * 128:rc * D + ci * 128 + 128],
                            idb[0:rr, 0:rr]))
                        em.wait("vector", "tensor", tv)
                        ev = em.step("vector", lambda vv, ci=ci, rc=rc, rr=rr: vv.tensor_copy(
                            out=dst[:, ci * rows + rc * rr:ci * rows + rc * rr + rr],
                            in_=tp[0:128, 0:rr]))
                        em.wait("tensor", "vector", ev)

            wT_simple(wqkv[0:D, :], D, wqT)
            wT_simple(wqkv[D:2 * D, :], D, wkT)
            wT_simple(wqkv[2 * D:3 * D, :], D, wvT)
            wT_simple(wo, D, woTb)
            wT_simple(wqi, HI * DI, wqiT)
            wT_simple(wki, DI, wkiT)
            wT_simple(wwi, HI, wwiT)
            em.barrier(("tensor", "vector", "scalar", "gpsimd", "sync"))

            wT_colstride = {id(wqT): D, id(wkT): D, id(wvT): D, id(woTb): D,
                            id(wqiT): HI * DI, id(wkiT): DI, id(wwiT): HI}

            def proj(dst_psum, wT, ncols, half):
                stride = wT_colstride[id(wT)]
                mv = None
                for kc in range(KC):
                    mv = em.step("tensor", lambda tt, kc=kc, half=half, dst_psum=dst_psum,
                                 wT=wT, ncols=ncols, stride=stride: tt.matmul(
                        dst_psum[:, 0:ncols],
                        xnt[:, kc * 128:(kc + 1) * 128],
                        wT[:, kc * stride + half * 512:kc * stride + half * 512 + ncols],
                        start=(kc == 0), stop=(kc == KC - 1)))
                return mv

            def emit_rope(dst, src, n_sc, n_v, tile_c, tile_s):
                sv1 = em.step("vector", lambda vv: vv.tensor_copy(
                    out=swp.ap().rearrange("q (a two) -> q a two", two=2)[:, :, 0],
                    in_=src.ap().rearrange("q (a two) -> q a two", two=2)[:, :, 1]))
                sv2 = em.step("vector", lambda vv: vv.tensor_copy(
                    out=swp.ap().rearrange("q (a two) -> q a two", two=2)[:, :, 1],
                    in_=src.ap().rearrange("q (a two) -> q a two", two=2)[:, :, 0]))
                em.step("vector", lambda vv: vv.tensor_mul(dst[:], src[:], tile_c[:]))
                em.step("vector", lambda vv: vv.tensor_mul(swp[:], swp[:], tile_s[:]))
                return em.step("vector", lambda vv: vv.tensor_add(dst[:], dst[:], swp[:]))

            for c in range(MK):
                if (p == 0 and c < 16) or (p == 1 and c >= 16):
                    src, mm = x1o, c % 16
                else:
                    src, mm = x1s, c % 16
                own = (BASE // 128) <= c < (BASE // 128) + 16
                mo = c - BASE // 128
                emit_ln_xnT(em, bufs, mm, x_src_ap=src)
                em.wait("sync", "vector", em.n["vector"])
                vc1 = em.dma("sync", lambda s, c=c: s.dma_start(
                    out=ctile[:], in_=cosf[c * 128:(c + 1) * 128, :]))
                vc2 = em.dma("sync", lambda s, c=c: s.dma_start(
                    out=stile[:], in_=sinf[c * 128:(c + 1) * 128, :]))
                em.wait("vector", "sync", vc2)

                for half in range(2):
                    pp = pk if half == 0 else pk2
                    em.wait("tensor", "vector", em.n["vector"])
                    mv = proj(pp, wkT, 512, half)
                    em.wait("vector", "tensor", mv)
                    em.step("vector", lambda vv, half=half, pp=pp: vv.tensor_copy(
                        out=kw[:, half * 512:half * 512 + 512], in_=pp[:, 0:512]))
                rv_ = emit_rope(kw2, kw, 0, 0, ctile, stile)
                vcst = em.step("vector", lambda vv: vv.tensor_copy(out=kbb[:], in_=kw2[:]))
                em.wait("tensor", "vector", vcst)
                em.wait("vector", "sync", em.n["sync"])
                for h in range(H):
                    pb = (h % 2) * 64
                    tv = em.step("tensor", lambda tt, h=h: tt.transpose(
                        tp[0:64, 0:128], kbb[:, h * HD:(h + 1) * HD], idb[:]))
                    em.wait("vector", "tensor", tv)
                    ev = em.step("vector", lambda vv, h=h, pb=pb: vv.tensor_copy(
                        out=ktc[pb:pb + 64, (h // 2) * 128:(h // 2) * 128 + 128],
                        in_=tp[0:64, 0:128]))
                    em.wait("tensor", "vector", ev)
                kv_ = em.n["vector"]
                em.wait("sync", "vector", kv_)
                for kc in range(KC):
                    em.dma("sync", lambda s, kc=kc, c=c: s.dma_start(
                        out=kts[:, kc * RK + c * 128:kc * RK + (c + 1) * 128],
                        in_=ktc[:, kc * 128:(kc + 1) * 128]))
                for half in range(2):
                    pp = pk if half == 0 else pk2
                    em.wait("tensor", "vector", em.n["vector"])
                    em.wait("tensor", "sync", em.n["sync"])
                    mv = proj(pp, wvT, 512, half)
                    em.wait("vector", "tensor", mv)
                    em.step("vector", lambda vv, half=half, pp=pp: vv.tensor_copy(
                        out=kbb[:, half * 512:half * 512 + 512], in_=pp[:, 0:512]))
                vv_ = em.n["vector"]
                em.wait("sync", "vector", vv_)
                em.dma("sync", lambda s, c=c: s.dma_start(
                    out=vds[c * 128:(c + 1) * 128, :], in_=kbb[:]))
                em.wait("vector", "sync", em.n["sync"])
                em.wait("tensor", "vector", em.n["vector"])
                mv = proj(pk, wkiT, DI, 0)
                em.wait("vector", "tensor", mv)
                vki0 = em.step("vector", lambda vv: vv.tensor_copy(out=kw2[:, 0:DI], in_=pk[:, 0:DI]))
                vki = em.step("vector", lambda vv: vv.tensor_copy(out=kbb[:, 0:DI], in_=kw2[:, 0:DI]))
                em.wait("tensor", "vector", vki)
                tv = em.step("tensor", lambda tt: tt.transpose(tp[0:64, 0:128], kbb[:, 0:DI], idb[:]))
                em.wait("vector", "tensor", tv)
                em.step("vector", lambda vv, c=c: vv.tensor_copy(
                    out=kitb[0:64, c * 128:(c + 1) * 128], in_=tp[0:64, 0:128]))
                ev = em.step("vector", lambda vv, c=c: vv.tensor_copy(
                    out=kitb[64:128, c * 128:(c + 1) * 128], in_=tp[0:64, 0:128]))
                em.wait("tensor", "vector", ev)

                if own:
                    for half in range(2):
                        pp = pk if half == 0 else pk2
                        em.wait("tensor", "vector", em.n["vector"])
                        mv = proj(pp, wqT, 512, half)
                        em.wait("vector", "tensor", mv)
                        em.step("vector", lambda vv, half=half, pp=pp: vv.tensor_copy(
                            out=kw[:, half * 512:half * 512 + 512], in_=pp[:, 0:512]))
                    rv_ = emit_rope(kw2, kw, 0, 0, ctile, stile)
                    vcst = em.step("vector", lambda vv: vv.tensor_copy(out=kbb[:], in_=kw2[:]))
                    em.wait("tensor", "vector", vcst)
                    for h in range(H):
                        pb = (h % 2) * 64
                        tv = em.step("tensor", lambda tt, h=h: tt.transpose(
                            tp[0:64, 0:128], kbb[:, h * HD:(h + 1) * HD], idb[:]))
                        em.wait("vector", "tensor", tv)
                        ev = em.step("vector", lambda vv, h=h, pb=pb, mo=mo: vv.tensor_copy(
                            out=qtb[pb:pb + 64, (h // 2) * T + mo * 128:(h // 2) * T + mo * 128 + 128],
                            in_=tp[0:64, 0:128]))
                        em.wait("tensor", "vector", ev)
                    em.wait("tensor", "vector", em.n["vector"])
                    mv = proj(pk, wqiT, HI * DI, 0)
                    em.wait("vector", "tensor", mv)
                    vq = em.step("vector", lambda vv: vv.tensor_copy(out=kbb[:, 0:HI * DI], in_=pk[:, 0:HI * DI]))
                    em.wait("tensor", "vector", vq)
                    for hi in range(HI):
                        pb = (hi % 2) * 64
                        tv = em.step("tensor", lambda tt, hi=hi: tt.transpose(
                            tp[0:64, 0:128], kbb[:, hi * DI:(hi + 1) * DI], idb[:]))
                        em.wait("vector", "tensor", tv)
                        ev = em.step("vector", lambda vv, hi=hi, pb=pb, mo=mo: vv.tensor_copy(
                            out=qitb[pb:pb + 64, (hi // 2) * T + mo * 128:(hi // 2) * T + mo * 128 + 128],
                            in_=tp[0:64, 0:128]))
                        em.wait("tensor", "vector", ev)
                    em.wait("tensor", "vector", em.n["vector"])
                    mv = proj(pk, wwiT, HI, 0)
                    em.wait("vector", "tensor", mv)
                    em.step("vector", lambda vv, mo=mo: vv.tensor_copy(
                        out=wis[:, mo * HI:(mo + 1) * HI], in_=pk[:, 0:HI]))

            em.barrier(("tensor", "vector", "scalar", "sync"))

        lo = mini[:, 0:1]; hi_ = mini[:, 1:2]; mid = mini[:, 2:3]
        negmid = mini[:, 3:4]; Fst = mini[:, 4:5]
        c_hi = mini[:, 7:8]; c_gt = mini[:, 8:9]; needf = mini[:, 9:10]
        cm = mini[:, 10:11]; mnew = mini[:, 11:12]; sco = mini[:, 12:13]
        bsum = mini[:, 13:14]; negmn = mini[:, 14:15]; rec = mini[:, 15:16]

        idx = sb("idx", [128, 4096], F32)
        giB = sb("giB", [128, 4096], I32)
        giC = sb("giC", [128, 4096], I32)
        giE = sb("giE", [128, 4096], I32)
        ktile = sb("ktile", [128, KC * 1024], BF16)
        vtile = sb("vtile", [128, 8 * D], BF16)
        for rc in range(M):
            abs_blk = BASE // 128 + rc
            special = abs_blk < 4
            if special:
                W = 512
            else:
                W = min(((abs_blk + 1) * 128 + 511) // 512 * 512, RK)
            NSC = W // 512
            if not special:
                for sc in range(NSC):
                    for hi in range(HI):
                        pb = (hi % 2) * 64
                        em.wait("tensor", "vector", em.n["vector"])
                        em.wait("tensor", "scalar", em.n["scalar"])
                        mv = em.step("tensor", lambda tt, hi=hi, pb=pb, rc=rc, sc=sc: tt.matmul(
                            psc[:, 0:512],
                            qitb[pb:pb + 64, (hi // 2) * T + rc * 128:(hi // 2) * T + rc * 128 + 128],
                            kitb[pb:pb + 64, sc * 512:(sc + 1) * 512],
                            start=True, stop=True))
                        em.wait("scalar", "tensor", mv)
                        sv = em.step("scalar", lambda sc_: sc_.activation(
                            kw[:, 0:512], psc[:, 0:512], AF.Relu, bias=0.0, scale=1.0))
                        em.wait("vector", "scalar", sv)
                        if hi == 0:
                            em.step("vector", lambda vv, rc=rc, sc=sc: vv.tensor_scalar(
                                idx[:, sc * 512:(sc + 1) * 512], kw[:, 0:512],
                                wis[:, rc * HI:rc * HI + 1], None, op0=ALU.mult))
                        else:
                            em.step("vector", lambda vv, rc=rc, sc=sc, hi=hi: vv.scalar_tensor_tensor(
                                idx[:, sc * 512:(sc + 1) * 512], kw[:, 0:512],
                                wis[:, rc * HI + hi:rc * HI + hi + 1],
                                idx[:, sc * 512:(sc + 1) * 512],
                                op0=ALU.mult, op1=ALU.add))
                em.step("vector", lambda vv, ab=abs_blk: vv.tensor_add(
                    idx[:, ab * 128:(ab + 1) * 128],
                    idx[:, ab * 128:(ab + 1) * 128], tris[:]))
                if (abs_blk + 1) * 128 < W:
                    em.step("vector", lambda vv, ab=abs_blk, W=W: vv.memset(
                        idx[:, (ab + 1) * 128:W], NEG))
                Wfloor = max((abs_blk * 128 + 1) // 512 * 512, 512)
                em.step("vector", lambda vv, Wfloor=Wfloor: vv.tensor_reduce(
                    lo, idx[:, 0:Wfloor], AX.X, ALU.min))
                em.step("vector", lambda vv: vv.tensor_scalar(lo, lo, 1.0, None, op0=ALU.subtract))
                em.step("vector", lambda vv, W=W: vv.tensor_reduce(
                    hi_, idx[:, 0:W], AX.X, ALU.max))
                em.step("vector", lambda vv: vv.tensor_scalar(hi_, hi_, 1.0, None, op0=ALU.add))
                thr = float(1024 - W)
                for it in range(NB):
                    vv1 = em.step("vector", lambda vv: vv.tensor_add(mid, lo, hi_))
                    vv1 = em.step("vector", lambda vv: vv.tensor_scalar(mid, mid, 0.5, None, op0=ALU.mult))
                    vv1 = em.step("vector", lambda vv: vv.tensor_scalar(negmid, mid, -1.0, None, op0=ALU.mult))
                    em.wait("scalar", "vector", vv1)
                    sv = em.step("scalar", lambda sc_, W=W: sc_.activation(
                        mask01[:, 0:W], idx[:, 0:W], AF.Sign, bias=negmid, scale=1.0,
                        accum_out=Fst))
                    em.wait("vector", "scalar", sv)
                    em.step("vector", lambda vv, thr=thr: vv.tensor_scalar(
                        fint[:, 0:1], Fst, thr, 1, op0=ALU.is_ge, op1=ALU.mult))
                    em.step("vector", lambda vv, thr=thr: vv.tensor_scalar(
                        fint[:, 1:2], Fst, thr, 1, op0=ALU.is_lt, op1=ALU.mult))
                    em.step("vector", lambda vv: vv.select(lo, fint[:, 0:1], mid, lo))
                    em.step("vector", lambda vv: vv.select(hi_, fint[:, 1:2], mid, hi_))
                em.step("vector", lambda vv, W=W: vv.tensor_scalar(
                    giB[:, 0:W], idx[:, 0:W], lo, 1, op0=ALU.is_ge, op1=ALU.mult))
                em.step("vector", lambda vv, W=W: vv.tensor_scalar(
                    giC[:, 0:W], idx[:, 0:W], hi_, 1, op0=ALU.is_ge, op1=ALU.mult))
                em.step("vector", lambda vv, W=W: vv.tensor_scalar(
                    giE[:, 0:W], idx[:, 0:W], hi_, 1, op0=ALU.is_gt, op1=ALU.mult))
                em.step("vector", lambda vv, W=W: vv.tensor_tensor_scan(
                    idx[:, 0:W], giC[:, 0:W], giC[:, 0:W], 0, op0=ALU.add, op1=ALU.bypass))
                em.step("vector", lambda vv, W=W: vv.tensor_copy(out=c_hi, in_=idx[:, W - 1:W]))
                em.step("vector", lambda vv, W=W: vv.tensor_tensor_scan(
                    idx[:, 0:W], giE[:, 0:W], giE[:, 0:W], 0, op0=ALU.add, op1=ALU.bypass))
                em.step("vector", lambda vv, W=W: vv.tensor_copy(out=c_gt, in_=idx[:, W - 1:W]))
                em.step("vector", lambda vv: vv.tensor_scalar(
                    fint[:, 0:1], c_hi, 512.0, 1, op0=ALU.is_ge, op1=ALU.mult))
                em.step("vector", lambda vv, W=W: vv.select(
                    giB[:, 0:W], fint[:, 0:1].to_broadcast([128, W]), giC[:, 0:W], giB[:, 0:W]))
                em.step("vector", lambda vv, W=W: vv.select(
                    giC[:, 0:W], fint[:, 0:1].to_broadcast([128, W]), giE[:, 0:W], giC[:, 0:W]))
                em.step("vector", lambda vv, W=W: vv.tensor_sub(giB[:, 0:W], giB[:, 0:W], giC[:, 0:W]))
                em.step("vector", lambda vv, W=W: vv.tensor_tensor_scan(
                    idx[:, 0:W], giB[:, 0:W], giB[:, 0:W], 0, op0=ALU.add, op1=ALU.bypass))
                em.step("vector", lambda vv: vv.tensor_scalar(
                    fint[:, 1:2], c_hi, 512.0, 1, op0=ALU.is_lt, op1=ALU.mult))
                em.step("vector", lambda vv: vv.select(c_gt, fint[:, 1:2], c_hi, c_gt))
                em.step("vector", lambda vv: vv.tensor_scalar(
                    needf, c_gt, 512.0, -1.0, op0=ALU.subtract, op1=ALU.mult))
                em.step("vector", lambda vv, W=W: vv.tensor_scalar(
                    giE[:, 0:W], idx[:, 0:W], needf, 1, op0=ALU.is_le, op1=ALU.mult))
                em.step("vector", lambda vv, W=W: vv.tensor_mul(giB[:, 0:W], giB[:, 0:W], giE[:, 0:W]))
                em.step("vector", lambda vv, W=W: vv.tensor_add(giC[:, 0:W], giC[:, 0:W], giB[:, 0:W]))
                mk_ = em.step("vector", lambda vv, W=W: vv.tensor_copy(out=mask01[:, 0:W], in_=giC[:, 0:W]))
            else:
                em.step("vector", lambda vv: vv.memset(mask01[:, 0:512], 1.0))

            em.step("vector", lambda vv: vv.memset(macc[:], NEG))
            em.step("vector", lambda vv: vv.memset(lacc[:], 0.0))
            em.step("vector", lambda vv: vv.memset(acc[:], 0.0))
            nsb = (W + 1023) // 1024
            for sb in range(nsb):
                blk = min(1024, W - sb * 1024)
                em.wait("sync", "tensor", em.n["tensor"])
                em.wait("sync", "vector", em.n["vector"])
                em.wait("sync", "scalar", em.n["scalar"])
                for kc in range(KC):
                    em.dma("sync", lambda s, kc=kc, sb=sb, blk=blk: s.dma_start(
                        out=ktile[:, kc * 1024:kc * 1024 + blk],
                        in_=kts[:, kc * RK + sb * 1024:kc * RK + sb * 1024 + blk]))
                for s2 in range(blk // 128):
                    em.dma("sync", lambda s, s2=s2, sb=sb: s.dma_start(
                        out=vtile[:, s2 * D:(s2 + 1) * D],
                        in_=vds[sb * 1024 + s2 * 128:sb * 1024 + (s2 + 1) * 128, :]))
                ldv = em.n["sync"]
                em.wait("tensor", "sync", ldv)
                for h in range(H):
                    pb = (h % 2) * 64
                    em.wait("tensor", "vector", em.n["vector"])
                    em.wait("tensor", "scalar", em.n["scalar"])
                    mv = None
                    for half in range((blk + 511) // 512):
                        nn = min(512, blk - half * 512)
                        mv = em.step("tensor", lambda tt, h=h, pb=pb, rc=rc, half=half, nn=nn: tt.matmul(
                            psc[:, half * 512:half * 512 + nn],
                            qtb[pb:pb + 64, (h // 2) * T + rc * 128:(h // 2) * T + rc * 128 + 128],
                            ktile[pb:pb + 64, (h // 2) * 1024 + half * 512:(h // 2) * 1024 + half * 512 + nn],
                            start=True, stop=True))
                    em.wait("vector", "tensor", mv)
                    em.step("vector", lambda vv, blk=blk: vv.tensor_reduce(
                        cm, psc[:, 0:blk], AX.X, ALU.max))
                    em.step("vector", lambda vv: vv.tensor_scalar(cm, cm, 0.125, None, op0=ALU.mult))
                    em.step("vector", lambda vv, h=h: vv.tensor_max(mnew, macc[:, h:h + 1], cm))
                    vv1 = em.step("vector", lambda vv, h=h: vv.tensor_sub(sco, macc[:, h:h + 1], mnew))
                    vv2 = em.step("vector", lambda vv: vv.tensor_scalar(negmn, mnew, -1.0, None, op0=ALU.mult))
                    em.wait("scalar", "vector", vv2)
                    s1 = em.step("scalar", lambda sc_: sc_.activation(sco, sco, AF.Exp, bias=0.0, scale=1.0))
                    s2_ = em.step("scalar", lambda sc_, blk=blk: sc_.activation(
                        probs[:, 0:blk], psc[:, 0:blk], AF.Exp, bias=negmn, scale=0.125))
                    em.wait("vector", "scalar", s2_)
                    em.step("vector", lambda vv, blk=blk, sb=sb: vv.scalar_tensor_tensor(
                        probs[:, 0:blk], probs[:, 0:blk], 1.0,
                        mask01[:, sb * 1024:sb * 1024 + blk],
                        op0=ALU.mult, op1=ALU.mult, accum_out=bsum))
                    em.step("vector", lambda vv, h=h: vv.tensor_scalar(
                        lacc[:, h:h + 1], lacc[:, h:h + 1], sco, None, op0=ALU.mult))
                    vv3 = em.step("vector", lambda vv, h=h: vv.tensor_add(
                        lacc[:, h:h + 1], lacc[:, h:h + 1], bsum))
                    em.step("vector", lambda vv, h=h: vv.tensor_copy(out=macc[:, h:h + 1], in_=mnew))
                    ngr = (blk + 511) // 512
                    for g in range(ngr):
                        nsub = min(4, (blk - g * 512) // 128)
                        em.wait("tensor", "vector", em.n["vector"])
                        tv = None
                        for j in range(nsub):
                            s2 = g * 4 + j
                            tv = em.step("tensor", lambda tt, s2=s2, j=j: tt.transpose(
                                tp[:, j * 128:(j + 1) * 128],
                                probs[:, s2 * 128:(s2 + 1) * 128], idb[:]))
                        em.wait("vector", "tensor", tv)
                        ev = em.step("vector", lambda vv, nsub=nsub: vv.tensor_copy(
                            out=ptile[:, 0:nsub * 128], in_=tp[:, 0:nsub * 128]))
                        em.wait("tensor", "vector", ev)
                        mv = None
                        for j in range(nsub):
                            s2 = g * 4 + j
                            stt_ = (g == 0 and j == 0)
                            spp_ = (g == ngr - 1 and j == nsub - 1)
                            mv = em.step("tensor", lambda tt, s2=s2, j=j, h=h, stt_=stt_, spp_=spp_: tt.matmul(
                                pav[:, 0:64],
                                ptile[:, j * 128:(j + 1) * 128],
                                vtile[:, s2 * D + h * HD:s2 * D + (h + 1) * HD],
                                start=stt_, stop=spp_))
                    em.wait("vector", "tensor", mv)
                    em.step("vector", lambda vv, h=h: vv.scalar_tensor_tensor(
                        acc[:, h * HD:(h + 1) * HD], acc[:, h * HD:(h + 1) * HD],
                        sco, pav[:, 0:64], op0=ALU.mult, op1=ALU.add))
            for h in range(H):
                em.step("vector", lambda vv, h=h: vv.reciprocal(rec, lacc[:, h:h + 1]))
                em.step("vector", lambda vv, h=h: vv.tensor_scalar(
                    aob[:, h * HD:(h + 1) * HD], acc[:, h * HD:(h + 1) * HD],
                    rec, None, op0=ALU.mult))
            av = em.n["vector"]
            em.wait("tensor", "vector", av)
            for g in range(2):
                tv = None
                for j in range(4):
                    jj = g * 4 + j
                    tv = em.step("tensor", lambda tt, jj=jj, j=j: tt.transpose(
                        tp[:, j * 128:(j + 1) * 128],
                        aob[:, jj * 128:(jj + 1) * 128], idb[:]))
                em.wait("vector", "tensor", tv)
                ev = None
                for j in range(4):
                    jj = g * 4 + j
                    ev = em.step("vector", lambda vv, jj=jj, j=j: vv.tensor_copy(
                        out=aotT[:, jj * 128:(jj + 1) * 128],
                        in_=tp[:, j * 128:(j + 1) * 128]))
                em.wait("tensor", "vector", ev)
            for half in range(2):
                pp = pk if half == 0 else pk2
                em.wait("tensor", "vector", em.n["vector"])
                mv = None
                for kc in range(KC):
                    mv = em.step("tensor", lambda tt, kc=kc, half=half, pp=pp: tt.matmul(
                        pp[:, 0:512],
                        aotT[:, kc * 128:(kc + 1) * 128],
                        woTb[:, kc * D + half * 512:kc * D + half * 512 + 512],
                        start=(kc == 0), stop=(kc == KC - 1)))
                em.wait("vector", "tensor", mv)
                em.step("vector", lambda vv, half=half, pp=pp: vv.tensor_copy(
                    out=yo[:, half * 512:half * 512 + 512], in_=pp[:, 0:512]))
            yv = em.n["vector"]
            em.wait("sync", "vector", yv)
            em.dma("sync", lambda s, rc=rc: s.dma_start(
                out=y_out[rc * 128:(rc + 1) * 128, :], in_=yo[:]))
            em.wait("vector", "sync", em.n["sync"])

        em.wait("sync", "sync", em.n["sync"])
        em.finalize(nc)
    return nc



def emit_phaseE(nc, em, st, sb, ps, x, ya, wg, ws1, ws3, ws2, we1, we3, we2, iden_in, out_f):
    E1 = 8
    FF = 2048
    FQ = 512
    NQ = FF // FQ
    xt = sb("xt", [128, KC * T], BF16)
    acc = sb("acc", [128, M * D], F32)
    w1b = [sb(f"w1b{i}", [128, KC * FQ], BF16) for i in range(2)]
    w3b = [sb(f"w3b{i}", [128, KC * FQ], BF16) for i in range(2)]
    w2b = [sb(f"w2b{i}", [128, (FQ // 128) * D], BF16) for i in range(2)]
    wgb = sb("wgb", [128, KC * E1], BF16)
    idb = sb("idb", [128, 128], BF16)
    sq = sb("sq", [128, D], F32)
    xnb = sb("xnb", [128, D], BF16)
    xchunk = sb("xchunk", [128, D], F32)
    mini = sb("mini", [128, 8], F32)
    lg = sb("lg", [128, 8], F32)
    g8 = sb("g8", [128, 8], F32)
    m8s = sb("m8s", [128, 8], F32)
    ge8 = sb("ge8", [128, 8], I32)
    ts8 = sb("ts8", [128, 8], F32)
    zs8 = sb("zs8", [128, 8], F32)
    g_all = sb("g_all", [128, M * 9], F32)
    hs1 = sb("hs1", [128, 2 * 512], F32)
    hb = sb("hb", [128, 4 * 512], BF16)

    tp = ps("tp", [128, 512], BF16)
    py0 = ps("py0", [128, 1024], F32)
    py1 = ps("py1", [128, 1024], F32)
    ph1a = ps("ph1a", [128, 512], F32)
    ph1b = ps("ph1b", [128, 512], F32)
    ph3a = ps("ph3a", [128, 512], F32)
    ph3b = ph3a
    pr = ph1a

    bufs = dict(xt=xt, sq=sq, mini=mini, xnb=xnb, tp=tp, idb=idb,
                xchunk=xchunk, eps=mini[:, 6:7])
    em.step("vector", lambda vv: vv.memset(mini[:, 6:7], 1e-5))
    em.step("vector", lambda vv: vv.memset(zs8[:], 0.0))

    gid = em.dma("gpsimd", lambda g: g.dma_start(out=idb[:], in_=iden_in[:, :]))
    for kc in range(KC):
        em.dma("gpsimd", lambda g, kc=kc: g.dma_start(
            out=wgb[:, kc * E1:(kc + 1) * E1], in_=wg[kc * 128:(kc + 1) * 128, :]))
    vya = em.dma("sync", lambda s: s.dma_start(
        out=acc.ap().rearrange("p (m n) -> p m n", m=M),
        in_=ya.rearrange("(m p) n -> p m n", p=128)))

    for m in range(M):
        emit_ln_xnT(em, bufs, m, x_src_ap=x)
        em.step("vector", lambda vv, m=m: vv.tensor_add(
            acc[:, m * D:(m + 1) * D], xchunk[:], acc[:, m * D:(m + 1) * D]))
    em.barrier(("tensor", "vector", "scalar", "gpsimd"))

    for m in range(M):
        mv = None
        em.wait("tensor", "vector", em.n["vector"])
        for kc in range(KC):
            mv = em.step("tensor", lambda tt, kc=kc, m=m: tt.matmul(
                pr[:, 0:E1],
                xt[:, kc * T + m * 128:kc * T + (m + 1) * 128],
                wgb[:, kc * E1:(kc + 1) * E1],
                start=(kc == 0), stop=(kc == KC - 1)))
        em.wait("vector", "tensor", mv)
        em.step("vector", lambda vv: vv.tensor_copy(out=lg[:], in_=pr[:, 0:E1]))
        sv = em.n["vector"]
        em.wait("scalar", "vector", sv)
        s1 = em.step("scalar", lambda sc: sc.activation(
            g8[:], lg[:], AF.Sigmoid, bias=0.0, scale=1.0))
        em.wait("vector", "scalar", s1)
        em.step("vector", lambda vv: vv.max(out=m8s[:], in_=g8[:]))
        em.step("vector", lambda vv: vv.tensor_scalar(
            ge8[:], g8[:], m8s[:, 1:2], 1, op0=ALU.is_ge, op1=ALU.mult))
        em.step("vector", lambda vv: vv.tensor_add(mini[:, 0:1], m8s[:, 0:1], m8s[:, 1:2]))
        em.step("vector", lambda vv: vv.reciprocal(mini[:, 1:2], mini[:, 0:1]))
        em.step("vector", lambda vv: vv.tensor_scalar(
            mini[:, 2:3], mini[:, 1:2], 1.0 / 3.0, None, op0=ALU.mult))
        em.step("vector", lambda vv: vv.select(ts8[:], ge8[:], g8[:], zs8[:]))
        em.step("vector", lambda vv, m=m: vv.tensor_scalar(
            g_all[:, m * 9:m * 9 + 8], ts8[:], mini[:, 2:3], None, op0=ALU.mult))
        em.step("vector", lambda vv, m=m: vv.memset(
            g_all[:, m * 9 + 8:m * 9 + 9], 1.0 / 3.0))

    em.barrier(("tensor", "vector", "scalar"))

    NU = 9 * NQ

    def srcs(u):
        e, q = u // NQ, u % NQ
        if e < E1:
            return (we1[e, :, q * FQ:(q + 1) * FQ],
                    we3[e, :, q * FQ:(q + 1) * FQ],
                    we2[e, q * FQ:(q + 1) * FQ, :], e)
        return (ws1[:, q * FQ:(q + 1) * FQ],
                ws3[:, q * FQ:(q + 1) * FQ],
                ws2[q * FQ:(q + 1) * FQ, :], 8)

    def emit_loads(u):
        s1a, s3a, s2a, _ = srcs(u)
        bi = u % 2
        for kc in range(KC):
            em.dma("gpsimd", lambda g, kc=kc, bi=bi, s1a=s1a: g.dma_start(
                out=w1b[bi][:, kc * FQ:(kc + 1) * FQ],
                in_=s1a[kc * 128:(kc + 1) * 128, :]))
            em.dma("gpsimd", lambda g, kc=kc, bi=bi, s3a=s3a: g.dma_start(
                out=w3b[bi][:, kc * FQ:(kc + 1) * FQ],
                in_=s3a[kc * 128:(kc + 1) * 128, :]))
        for kc2 in range(FQ // 128):
            em.dma("gpsimd", lambda g, kc2=kc2, bi=bi, s2a=s2a: g.dma_start(
                out=w2b[bi][:, kc2 * D:(kc2 + 1) * D],
                in_=s2a[kc2 * 128:(kc2 + 1) * 128, :]))
        return em.n["gdma"]

    load_end = {}
    compute_end = {}
    em.wait("gpsimd", "vector", em.n["vector"])
    load_end[0] = emit_loads(0)
    silu_cnt = [0, 0]
    mul_tile = [0] * 4
    py_last = [0, 0]
    pidx = 0
    for u in range(NU):
        e, q = u // NQ, u % NQ
        _, _, _, gcol = srcs(u)
        bi = u % 2
        if u + 1 < NU:
            if u >= 1:
                em.wait("gpsimd", "tensor", compute_end[u - 1])
            load_end[u + 1] = emit_loads(u + 1)
        em.wait("tensor", "gdma", load_end[u])
        for g4 in range(4):
            for s in range(4):
                p1 = ph1a if s % 2 == 0 else ph1b
                p3 = ph3a if s % 2 == 0 else ph3b
                if silu_cnt[s % 2]:
                    em.wait("tensor", "scalar", silu_cnt[s % 2])
                for kc in range(KC):
                    em.step("tensor", lambda tt, kc=kc, s=s, g4=g4, bi=bi, p1=p1: tt.matmul(
                        p1[:, 0:512],
                        w1b[bi][:, kc * FQ + s * 128:kc * FQ + s * 128 + 128],
                        xt[:, kc * T + g4 * 512:kc * T + g4 * 512 + 512],
                        start=(kc == 0), stop=(kc == KC - 1)))
                prev3 = mul_tile[s - 1] if s >= 1 else mul_tile[3]
                if prev3:
                    em.wait("tensor", "vector", prev3)
                mvh = None
                for kc in range(KC):
                    mvh = em.step("tensor", lambda tt, kc=kc, s=s, g4=g4, bi=bi, p3=p3: tt.matmul(
                        p3[:, 0:512],
                        w3b[bi][:, kc * FQ + s * 128:kc * FQ + s * 128 + 128],
                        xt[:, kc * T + g4 * 512:kc * T + g4 * 512 + 512],
                        start=(kc == 0), stop=(kc == KC - 1)))
                prevm = mul_tile[s - 2] if s >= 2 else mul_tile[2 + s]
                if prevm:
                    em.wait("scalar", "vector", prevm)
                em.wait("scalar", "tensor", mvh)
                sv = em.step("scalar", lambda sc, s=s, p1=p1: sc.activation(
                    hs1[:, (s % 2) * 512:(s % 2) * 512 + 512], p1[:, 0:512],
                    AF.Silu, bias=0.0, scale=1.0))
                silu_cnt[s % 2] = sv
                em.wait("vector", "scalar", sv)
                mul_tile[s] = em.step("vector", lambda vv, s=s, p3=p3: vv.tensor_mul(
                    hb[:, s * 512:(s + 1) * 512],
                    hs1[:, (s % 2) * 512:(s % 2) * 512 + 512], p3[:, 0:512]))
            for ts_ in range(4):
                pp = pidx % 2
                p_y = py0 if pp == 0 else py1
                pidx += 1
                if py_last[pp]:
                    em.wait("tensor", "vector", py_last[pp])
                mv = None
                for s in range(4):
                    if ts_ == 0:
                        em.wait("tensor", "vector", mul_tile[s])
                    for nh in range(2):
                        stf = (s == 0)
                        spf = (s == 3)
                        mv = em.step("tensor", lambda tt, s=s, ts_=ts_, nh=nh, p=p_y, bi=bi, stf=stf, spf=spf: tt.matmul(
                            p[:, nh * 512:nh * 512 + 512],
                            hb[:, s * 512 + ts_ * 128:s * 512 + ts_ * 128 + 128],
                            w2b[bi][:, s * D + nh * 512:s * D + nh * 512 + 512],
                            start=stf, stop=spf))
                em.wait("vector", "tensor", mv)
                mcol = g4 * 4 + ts_
                py_last[pp] = em.step("vector", lambda vv, mcol=mcol, gcol=gcol, p=p_y: vv.scalar_tensor_tensor(
                    acc[:, mcol * D:(mcol + 1) * D],
                    p[:, 0:1024],
                    g_all[:, mcol * 9 + gcol:mcol * 9 + gcol + 1],
                    acc[:, mcol * D:(mcol + 1) * D],
                    op0=ALU.mult, op1=ALU.add))
        compute_end[u] = em.n["tensor"]

    em.barrier(("tensor", "vector", "scalar", "sync", "gpsimd"))
    for m in range(M):
        eng = "sync" if m % 2 == 0 else "gpsimd"
        em.dma(eng, lambda en, m=m: en.dma_start(
            out=out_f[m * 128:(m + 1) * 128, :], in_=acc[:, m * D:(m + 1) * D]))
    em.wait("sync", "sync", em.n["sync"])
    em.wait("sync", "gdma", em.n["gdma"])



def build_launchE():
    E1 = 8
    FF = 2048
    FQ = 512
    NQ = FF // FQ
    nc = bass.Bass("TRN2", target_bir_lowering=False, detect_race_conditions=False)
    x = nc.dram_tensor("x", [T, D], F32, kind="ExternalInput").ap()
    ya = nc.dram_tensor("ya", [T, D], F32, kind="ExternalInput").ap()
    wg = nc.dram_tensor("wg", [D, E1], F32, kind="ExternalInput").ap()
    ws1 = nc.dram_tensor("ws1", [D, FF], F32, kind="ExternalInput").ap()
    ws3 = nc.dram_tensor("ws3", [D, FF], F32, kind="ExternalInput").ap()
    ws2 = nc.dram_tensor("ws2", [FF, D], F32, kind="ExternalInput").ap()
    we1 = nc.dram_tensor("we1", [E1, D, FF], F32, kind="ExternalInput").ap()
    we3 = nc.dram_tensor("we3", [E1, D, FF], F32, kind="ExternalInput").ap()
    we2 = nc.dram_tensor("we2", [E1, FF, D], F32, kind="ExternalInput").ap()
    iden_in = nc.dram_tensor("iden", [128, 128], F32, kind="ExternalInput").ap()
    out_f = nc.dram_tensor("out_f", [T, D], F32, kind="ExternalOutput").ap()

    em = Em()
    with contextlib.ExitStack() as st:
        def sb(name, shape, dtype):
            return st.enter_context(nc.sbuf_tensor(name, shape, dtype))

        def ps(name, shape, dtype):
            return _ps_pad(nc, st, name, shape, dtype)

        emit_phaseE(nc, em, st, sb, ps, x, ya, wg, ws1, ws3, ws2, we1, we3, we2, iden_in, out_f)
        em.finalize(nc)
    return nc



def build_fused():
    HI = 4
    DI = 64
    nc = bass.Bass("TRN2", target_bir_lowering=False, detect_race_conditions=False)
    x = nc.dram_tensor("x", [T, D], F32, kind="ExternalInput").ap()
    wk = nc.dram_tensor("wk", [D, D], F32, kind="ExternalInput").ap()
    wv = nc.dram_tensor("wv", [D, D], F32, kind="ExternalInput").ap()
    wq0 = nc.dram_tensor("wq0", [D, D], F32, kind="ExternalInput").ap()
    wout0 = nc.dram_tensor("wout0", [D, D], F32, kind="ExternalInput").ap()
    proj = nc.dram_tensor("proj", [H, F2, HD], F32, kind="ExternalInput").ap()
    iden = nc.dram_tensor("iden", [128, 128], F32, kind="ExternalInput").ap()
    rw = nc.dram_tensor("rw", [D, 8], F32, kind="ExternalInput").ap()
    rb = nc.dram_tensor("rb", [1, 8], F32, kind="ExternalInput").ap()
    w1 = nc.dram_tensor("w1", [8, D, 4096], F32, kind="ExternalInput").ap()
    b1t = nc.dram_tensor("b1t", [8, 128, 32], F32, kind="ExternalInput").ap()
    w2 = nc.dram_tensor("w2", [8, 4096, D], F32, kind="ExternalInput").ap()
    b2 = nc.dram_tensor("b2", [8, D], F32, kind="ExternalInput").ap()
    wqkv1 = nc.dram_tensor("wqkv1", [3 * D, D], F32, kind="ExternalInput").ap()
    wo1 = nc.dram_tensor("wo1", [D, D], F32, kind="ExternalInput").ap()
    wqi = nc.dram_tensor("wqi", [HI * DI, D], F32, kind="ExternalInput").ap()
    wki = nc.dram_tensor("wki", [DI, D], F32, kind="ExternalInput").ap()
    wwi = nc.dram_tensor("wwi", [HI, D], F32, kind="ExternalInput").ap()
    cosf = nc.dram_tensor("cosf", [4096, D], F32, kind="ExternalInput").ap()
    sinf = nc.dram_tensor("sinf", [4096, D], F32, kind="ExternalInput").ap()
    camask = nc.dram_tensor("camask", [T, 4096], BF16, kind="ExternalInput").ap()
    wg = nc.dram_tensor("wg", [D, 8], F32, kind="ExternalInput").ap()
    ws1 = nc.dram_tensor("ws1", [D, 2048], F32, kind="ExternalInput").ap()
    ws3 = nc.dram_tensor("ws3", [D, 2048], F32, kind="ExternalInput").ap()
    ws2 = nc.dram_tensor("ws2", [2048, D], F32, kind="ExternalInput").ap()
    we1 = nc.dram_tensor("we1", [8, D, 2048], F32, kind="ExternalInput").ap()
    we3 = nc.dram_tensor("we3", [8, D, 2048], F32, kind="ExternalInput").ap()
    we2 = nc.dram_tensor("we2", [8, 2048, D], F32, kind="ExternalInput").ap()
    out_f = nc.dram_tensor("out_f", [T, D], F32, kind="ExternalOutput").ap()
    kvsrc = nc.dram_tensor("kvsrc", [128, 32 * 65 + 1], F32, kind="Internal").ap()
    kvg = nc.dram_tensor("kvg", [256, 32 * 65 + 1], F32, kind="Internal").ap()
    ya_d = nc.dram_tensor("ya_d", [T, D], F32, kind="Internal").ap()
    x1_d = nc.dram_tensor("x1_d", [T, D], F32, kind="Internal").ap()
    x1g = nc.dram_tensor("x1g", [4096, D], F32, kind="Internal").ap()
    y1_d = nc.dram_tensor("y1_d", [T, D], F32, kind="Internal").ap()
    kts = nc.dram_tensor("kts", [128, KC * 4096], BF16, kind="Internal").ap()
    vds = nc.dram_tensor("vds", [4096, D], BF16, kind="Internal").ap()
    qts_d = nc.dram_tensor("qts_d", [128, KC * 4096], BF16, kind="Internal").ap()
    qits_d = nc.dram_tensor("qits_d", [128, (HI // 2) * 4096], BF16, kind="Internal").ap()
    wis_d = nc.dram_tensor("wis_d", [128, 32 * HI], F32, kind="Internal").ap()

    em = Em()
    with contextlib.ExitStack() as st0:
        def mk_sb(pfx, stx):
            def sb(name, shape, dtype):
                return stx.enter_context(nc.sbuf_tensor(pfx + name, shape, dtype))
            def ps(name, shape, dtype):
                return _ps_pad(nc, stx, pfx + name, shape, dtype)
            return sb, ps

        zpad = st0.enter_context(nc.sbuf_tensor("zpad", [128, 1], F32))
        em.step("vector", lambda vv: vv.memset(zpad[:], 0.0))

        with contextlib.ExitStack() as stA:
            sbA, psA = mk_sb("A_", stA)
            emit_phaseA(nc, em, stA, sbA, psA, x, wk, wv, proj, iden,
                        kvsrc[:, 0:2080], kvsrc[0:16, 2080:2081])
            zv = em.n["vector"]
            em.wait("sync", "vector", zv)
            em.dma("sync", lambda s: s.dma_start(
                out=kvsrc[16:128, 2080:2081], in_=zpad[16:128, 0:1],
                allow_slow_non_contiguous=True))
            em.wait("sync", "sync", em.n["sync"])
            em.barrier()
        def _coll1(eng, sems):
            eng.collective_compute(
                "AllGather", ALU.bypass,
                replica_groups=[[0, 1], [2, 3], [4, 5], [6, 7]],
                ins=[kvsrc[:, :]], outs=[kvg[:, :]],
            ).then_inc(sems["coll"], 1)
        em.raw("gpsimd", _coll1, sem="coll", inc=1)
        for e in ("sync", "tensor", "vector", "scalar"):
            em.wait(e, "coll", 1)
        em.wait("gpsimd", "coll", 1)

        with contextlib.ExitStack() as stB:
            sbB, psB = mk_sb("B_", stB)
            emit_phaseB(nc, em, stB, sbB, psB, x, wq0, wout0, proj, iden,
                        kvg[0:128, 0:2080], kvg[128:256, 0:2080],
                        kvg[0:16, 2080:2081], kvg[128:144, 2080:2081], ya_d)
            em.barrier()

        with contextlib.ExitStack() as stC:
            sbC, psC = mk_sb("C_", stC)
            emit_phaseC(nc, em, stC, sbC, psC, x, ya_d, rw, rb, w1, b1t, w2, b2,
                        iden, x1_d)
            em.barrier()

        def _coll2(eng, sems):
            eng.collective_compute(
                "AllGather", ALU.bypass,
                replica_groups=[[0, 1], [2, 3], [4, 5], [6, 7]],
                ins=[x1_d[:, :]], outs=[x1g[:, :]],
            ).then_inc(sems["coll"], 1)
        em.raw("gpsimd", _coll2, sem="coll", inc=1)
        for e in ("sync", "tensor", "vector", "scalar"):
            em.wait(e, "coll", 2)
        em.wait("gpsimd", "coll", 2)

        with contextlib.ExitStack() as stD:
            sbD, psD = mk_sb("D_", stD)
            emit_phaseD_uniform(nc, em, stD, sbD, psD, x1g, wqkv1, wo1, wqi, wki,
                                wwi, iden, cosf, sinf, camask, y1_d, kts, vds,
                                qts_d, qits_d, wis_d)
            em.barrier()

        with contextlib.ExitStack() as stE:
            sbE, psE = mk_sb("E_", stE)
            emit_phaseE(nc, em, stE, sbE, psE, x1_d, y1_d, wg, ws1, ws3, ws2,
                        we1, we3, we2, iden, out_f)

        em.wait("sync", "sync", em.n["sync"])
        em.finalize(nc)
    return nc


def _parity_rows(p):
    idx = []
    for j in range(16):
        blk = 2 * j + p
        idx.extend(range(blk * 128, (blk + 1) * 128))
    return np.array(idx)


def _build_camask(p):
    import ml_dtypes as mld
    cm = np.zeros((T, 4096), np.float32)
    for i in range(16):
        ab = 2 * i + p
        if ab < 4:
            continue
        W = min(((2 * i + 2) * 128 + 511) // 512 * 512, 4096)
        for r in range(128):
            t_abs = ab * 128 + r
            cm[i * 128 + r, t_abs + 1:W] = NEG
    return cm.astype(mld.bfloat16)


def run_fused(inputs):
    x_all = inputs["L0_attn_x"]
    wqkv = inputs["L0_attn_W_qkv"]
    comC = _prepC_common(inputs)
    cosf, sinf = _rope_tables()
    com = dict(
        wk=np.ascontiguousarray(wqkv[:, D:2 * D]),
        wv=np.ascontiguousarray(wqkv[:, 2 * D:3 * D]),
        wq0=np.ascontiguousarray(wqkv[:, 0:D]),
        wout0=np.ascontiguousarray(inputs["L0_attn_W_out"]),
        proj=np.ascontiguousarray(inputs["L0_attn_proj"]),
        iden=np.eye(128, dtype=np.float32),
        rw=comC["rw"], rb=comC["rb"], w1=comC["w1"], b1t=comC["b1t"],
        w2=comC["w2"], b2=comC["b2"],
        wqkv1=np.ascontiguousarray(inputs["L1_attn_w_qkv"]),
        wo1=np.ascontiguousarray(inputs["L1_attn_w_o"]),
        wqi=np.ascontiguousarray(inputs["L1_attn_w_qi"]),
        wki=np.ascontiguousarray(inputs["L1_attn_w_ki"]),
        wwi=np.ascontiguousarray(inputs["L1_attn_w_wi"]),
        cosf=cosf, sinf=sinf,
        wg=np.ascontiguousarray(inputs["L1_ffn_Wg"]),
        ws1=np.ascontiguousarray(inputs["L1_ffn_Ws1"]),
        ws3=np.ascontiguousarray(inputs["L1_ffn_Ws3"]),
        ws2=np.ascontiguousarray(inputs["L1_ffn_Ws2"]),
        we1=inputs["L1_ffn_We1"], we3=inputs["L1_ffn_We3"],
        we2=inputs["L1_ffn_We2"])
    cams = [_build_camask(0), _build_camask(1)]
    rows = [_parity_rows(0), _parity_rows(1)]
    nc = build_fused()
    maps = []
    for c in range(NCORES):
        b, p = c // 2, c % 2
        xsh = np.ascontiguousarray(x_all[b][rows[p]])
        maps.append(dict(x=xsh, camask=cams[p], **com))
    res = run_bass_kernel_spmd(nc, maps, core_ids=list(range(NCORES)))
    out = np.empty((4, 4096, D), np.float32)
    for c in range(NCORES):
        b, p = c // 2, c % 2
        out[b][rows[p]] = res.results[c]["out_f"]
    return out



def _shards(x_all):
    out = []
    for c in range(NCORES):
        b, p = c // 2, c % 2
        out.append(np.ascontiguousarray(x_all[b, p * T:(p + 1) * T, :]))
    return out


def run_A(inputs):
    x_all = inputs["L0_attn_x"]
    wqkv = inputs["L0_attn_W_qkv"]
    wk = np.ascontiguousarray(wqkv[:, D:2 * D])
    wv = np.ascontiguousarray(wqkv[:, 2 * D:3 * D])
    proj = np.ascontiguousarray(inputs["L0_attn_proj"])
    iden = np.eye(128, dtype=np.float32)
    nc = build_launchA()
    maps = []
    for c, xsh in enumerate(_shards(x_all)):
        maps.append(dict(x=xsh, wk=wk, wv=wv, proj=proj, iden=iden))
    res = run_bass_kernel_spmd(nc, maps, core_ids=list(range(NCORES)))
    return [dict(kv=r["kv_out"], stab=r["stab_out"]) for r in res.results]


def run_B(inputs, resA):
    x_all = inputs["L0_attn_x"]
    wqkv = inputs["L0_attn_W_qkv"]
    wq = np.ascontiguousarray(wqkv[:, 0:D])
    wout = np.ascontiguousarray(inputs["L0_attn_W_out"])
    proj = np.ascontiguousarray(inputs["L0_attn_proj"])
    iden = np.eye(128, dtype=np.float32)
    nc = build_launchB()
    maps = []
    for c, xsh in enumerate(_shards(x_all)):
        sib = c ^ 1
        lo, hi = (c, sib) if c < sib else (sib, c)
        maps.append(dict(x=xsh, wq=wq, wout=wout, proj=proj, iden=iden,
                         kv0=resA[lo]["kv"], kv1=resA[hi]["kv"],
                         st0=resA[lo]["stab"], st1=resA[hi]["stab"]))
    res = run_bass_kernel_spmd(nc, maps, core_ids=list(range(NCORES)))
    return [r["y_out"] for r in res.results]


def _prepC_common(inputs):
    b1 = inputs["L0_ffn_b1"]
    b1t = np.ascontiguousarray(b1.reshape(8, 32, 128).transpose(0, 2, 1))
    return dict(
        rw=np.ascontiguousarray(inputs["L0_ffn_router_w"]),
        rb=np.ascontiguousarray(inputs["L0_ffn_router_b"][None, :]),
        w1=inputs["L0_ffn_w1"], b1t=b1t,
        w2=inputs["L0_ffn_w2"],
        b2=np.ascontiguousarray(inputs["L0_ffn_b2"]),
        iden=np.eye(128, dtype=np.float32))


def run_C(inputs, resB):
    x_all = inputs["L0_attn_x"]
    com = _prepC_common(inputs)
    nc = build_launchC()
    maps = []
    for c, xsh in enumerate(_shards(x_all)):
        maps.append(dict(x=xsh, ya=resB[c], **com))
    res = run_bass_kernel_spmd(nc, maps, core_ids=list(range(NCORES)))
    return [r["x1_out"] for r in res.results]


def _rope_tables():
    S = 4096
    inv = 1.0 / 10000.0 ** (np.arange(0, HD, 2, dtype=np.float32) / HD)
    ang = np.arange(S, dtype=np.float32)[:, None] * inv
    cos_h = np.zeros((S, HD), np.float32)
    sin_h = np.zeros((S, HD), np.float32)
    cos_h[:, 0::2] = np.cos(ang)
    cos_h[:, 1::2] = np.cos(ang)
    sin_h[:, 0::2] = -np.sin(ang)
    sin_h[:, 1::2] = np.sin(ang)
    cosf = np.tile(cos_h, (1, H))
    sinf = np.tile(sin_h, (1, H))
    return np.ascontiguousarray(cosf), np.ascontiguousarray(sinf)


def _tri_tile():
    t = np.zeros((128, 128), np.float32)
    for i in range(128):
        t[i, i + 1:] = NEG
    return t


def _prepD_common(inputs):
    cosf, sinf = _rope_tables()
    return dict(
        wqkv=np.ascontiguousarray(inputs["L1_attn_w_qkv"]),
        wo=np.ascontiguousarray(inputs["L1_attn_w_o"]),
        wqi=np.ascontiguousarray(inputs["L1_attn_w_qi"]),
        wki=np.ascontiguousarray(inputs["L1_attn_w_ki"]),
        wwi=np.ascontiguousarray(inputs["L1_attn_w_wi"]),
        iden=np.eye(128, dtype=np.float32),
        cosf=cosf, sinf=sinf, tri=_tri_tile())


def run_D(inputs, resC):
    com = _prepD_common(inputs)
    outs = [None] * NCORES
    for p in (0, 1):
        nc = build_launchD(p)
        cores = [c for c in range(NCORES) if c % 2 == p]
        maps = []
        for c in cores:
            maps.append(dict(x1o=resC[c], x1s=resC[c ^ 1], **com))
        res = run_bass_kernel_spmd(nc, maps, core_ids=cores)
        for i, c in enumerate(cores):
            outs[c] = res.results[i]["y_out"]
    return outs


def run_E(inputs, resC, resD):
    com = dict(
        wg=np.ascontiguousarray(inputs["L1_ffn_Wg"]),
        ws1=np.ascontiguousarray(inputs["L1_ffn_Ws1"]),
        ws3=np.ascontiguousarray(inputs["L1_ffn_Ws3"]),
        ws2=np.ascontiguousarray(inputs["L1_ffn_Ws2"]),
        we1=inputs["L1_ffn_We1"], we3=inputs["L1_ffn_We3"], we2=inputs["L1_ffn_We2"],
        iden=np.eye(128, dtype=np.float32))
    nc = build_launchE()
    maps = []
    for c in range(NCORES):
        maps.append(dict(x=resC[c], ya=resD[c], **com))
    res = run_bass_kernel_spmd(nc, maps, core_ids=list(range(NCORES)))
    return [r["out_f"] for r in res.results]


def kernel(**inputs):
    inputs = {k: np.asarray(v) for k, v in inputs.items()}
    return run_fused(inputs)
